# version 13
# speedup vs baseline: 1.3816x; 1.0364x over previous
import numpy as np
import ml_dtypes
import concourse.bass as bass
import concourse.mybir as mybir
from concourse.bass_utils import run_bass_kernel_spmd
from contextlib import ExitStack

F32 = mybir.dt.float32
BF16 = mybir.dt.bfloat16
AF = mybir.ActivationFunctionType
ALU = mybir.AluOpType
AX = mybir.AxisListType

SEQ = 4096
DM = 1024
NG = SEQ // 128
TREE_OFF = [8 * ((1 << d) - 1) for d in range(9)]
BF16NP = ml_dtypes.bfloat16


def tree_perm():
    cur = [(t, 0) for t in range(8)]
    order = list(cur)
    for _ in range(7):
        lefts = [(t, 2 * c + 1) for (t, c) in cur]
        rights = [(t, 2 * c + 2) for (t, c) in cur]
        cur = lefts + rights
        order += cur
    return np.array([t * 255 + c for (t, c) in order], dtype=np.int64)


def host_consts():
    ident = np.eye(128, dtype=BF16NP)
    ones_row = np.ones((1, 512), dtype=BF16NP)
    cm = np.where(np.tril(np.ones((128, 128), bool)), 0.0, -1e30).astype(np.float32)
    jj, ii = np.meshgrid(np.arange(128), np.arange(128), indexing="ij")
    expt = np.where(ii > jj, (ii - 1 - jj).astype(np.float32), 0.0)
    stril = (ii > jj).astype(np.float32)
    return {"c_ident": ident, "c_ones": ones_row, "c_cmask": cm,
            "c_expt": expt, "c_stril": stril}


class Prog:
    def __init__(self):
        self.nc = bass.Bass("TRN2", target_bir_lowering=False)
        self.ctx = ExitStack()
        self.ctx.__enter__()
        self.block = None
        self.sems = {}
        self.bufs = {}
        self.cvec_slots = {}
        for s in ["ld", "gld", "sld", "mm", "dve", "act", "pool", "st", "wst"]:
            h = self.ctx.enter_context(self.nc.semaphore(s))
            self.sems[s] = [h, 0]

    def sb(self, name, shape, dtype=F32):
        ctx = getattr(self, "scope_ctx", None) or self.ctx
        t = ctx.enter_context(self.nc.sbuf_tensor(name, shape, dtype))
        self.bufs[name] = t
        return t

    def psum(self, name, shape, dtype=F32):
        ctx = getattr(self, "scope_ctx", None) or self.ctx
        return ctx.enter_context(self.nc.psum_tensor(name, shape, dtype))

    def open_scope(self):
        self.scope_ctx = ExitStack()
        self.scope_ctx.__enter__()

    def close_scope(self):
        self.scope_ctx.__exit__(None, None, None)
        self.scope_ctx = None

    def h(self, name):
        return self.sems[name][0]

    def bump(self, name, k=1):
        s = self.sems[name]
        s[1] += k
        return s[1]

    def cnt(self, name):
        return self.sems[name][1]

    def cvec(self, value):
        key = float(value)
        if key not in self.cvec_slots:
            self.cvec_slots[key] = len(self.cvec_slots)
            assert len(self.cvec_slots) <= 8
        return self.bufs["cvec"][:, self.cvec_slots[key]:self.cvec_slots[key] + 1]

    def finish(self):
        self.ctx.__exit__(None, None, None)


class _FakeIns:
    def __init__(self, blk, eng):
        self.blk, self.eng = blk, eng

    def then_inc(self, sem, amt):
        self.blk.rec[self.eng][-1][1].append((id(sem), amt))
        return self


class _FakeEng:
    def __init__(self, blk, name):
        self.blk, self.name = blk, name

    def wait_ge(self, sem, val):
        self.blk.rec[self.name].append(("wait", (id(sem), val), self.blk.tags.get(id(sem))))

    def __getattr__(self, nm):
        def op(*a, **k):
            self.blk.rec[self.name].append((nm, [], None))
            return _FakeIns(self.blk, self.name)
        return op


class FakeBlock:

    ENGS = ["sync", "tensor", "vector", "scalar", "gpsimd"]

    def __init__(self, P):
        self.rec = {e: [] for e in self.ENGS}
        self.tags = {h.__repr__() and id(h): nm for nm, (h, _) in P.sems.items()}

    def _run(self, name, f):
        f(_FakeEng(self, name))

    def sync(self, f):
        self._run("sync", f)

    def tensor(self, f):
        self._run("tensor", f)

    def vector(self, f):
        self._run("vector", f)

    def scalar(self, f):
        self._run("scalar", f)

    def gpsimd(self, f):
        self._run("gpsimd", f)

    def simulate(self):
        ptr = {e: 0 for e in self.ENGS}
        sem = {}
        stuck = 0
        while True:
            progress = False
            done = True
            for e in self.ENGS:
                q = self.rec[e]
                while ptr[e] < len(q):
                    kind, payload, tag = q[ptr[e]][0], q[ptr[e]][1], q[ptr[e]][2]
                    if kind == "wait":
                        sid, val = payload
                        if sem.get(sid, 0) >= val:
                            ptr[e] += 1
                            progress = True
                            continue
                        done = False
                        break
                    else:
                        for sid, amt in payload:
                            sem[sid] = sem.get(sid, 0) + amt
                        ptr[e] += 1
                        progress = True
                if ptr[e] < len(q):
                    done = False
            if done:
                return True, None
            if not progress:
                info = []
                for e in self.ENGS:
                    q = self.rec[e]
                    if ptr[e] < len(q):
                        kind, payload, tag = q[ptr[e]]
                        info.append((e, ptr[e], len(q), tag, payload[1] if kind == "wait" else None,
                                     sem.get(payload[0], 0) if kind == "wait" else None))
                return False, info


def alloc_common(P):
    P.sb("ident", [128, 128], BF16)
    P.sb("ones", [1, 512], BF16)
    P.sb("cmask", [128, 128], F32)
    P.sb("cvec", [128, 8], F32)
    P.sb("onesf", [1, 128], F32)
    P.sb("iota_p", [128, 1], F32)
    P.sb("iota_r", [128, 1], F32)
    P.sb("swp0", [128, 1024], F32)
    P.sb("swp1", [128, 1024], F32)
    P.sb("mv", [128, 12 + 2 * NG], F32)
    P.sb("varc", [128, 2 * NG], F32)
    P.sb("rst", [128, NG], F32)
    P.sb("nmr", [128, NG], F32)


def emit_preamble(P, dram, want_cvecs):
    block = P.block
    for v in want_cvecs:
        P.cvec(v)

    def loads(s):
        for name in ["ident", "ones", "cmask"]:
            s.dma_start(out=P.bufs[name][:], in_=dram["c_" + name]).then_inc(P.h("ld"), 16)
            P.bump("ld", 16)
    block.sync(loads)

    def vsec(v):
        for val, slot in P.cvec_slots.items():
            v.memset(P.bufs["cvec"][:, slot:slot + 1], val)
        ins = v.memset(P.bufs["onesf"][:], 1.0)
        ins.then_inc(P.h("dve"), 1)
        P.bump("dve")
    block.vector(vsec)

    def gsec(g):
        g.iota(P.bufs["iota_p"][:], pattern=[[0, 1]], base=0, channel_multiplier=1,
               allow_small_or_imprecise_dtypes=True)
        ins = g.iota(P.bufs["iota_r"][:], pattern=[[0, 1]], base=127, channel_multiplier=-1,
                     allow_small_or_imprecise_dtypes=True)
        ins.then_inc(P.h("pool"), 1)
        P.bump("pool")
    block.gpsimd(gsec)
    return P.cnt("dve"), P.cnt("pool")


def emit_stats_sweep(P, resid_ap, pre_dve):
    block = P.block
    bufs = [P.bufs["swp0"], P.bufs["swp1"]]
    mv, varc = P.bufs["mv"], P.bufs["varc"]
    rst, nmr = P.bufs["rst"], P.bufs["nmr"]

    ld0 = P.cnt("ld")
    dve0 = P.cnt("dve")

    def loads(s):
        for g in range(NG):
            if g >= 2:
                s.wait_ge(P.h("dve"), dve0 + (g - 1))
            s.dma_start(out=bufs[g % 2][:],
                        in_=resid_ap[128 * g:128 * (g + 1), :]).then_inc(P.h("ld"), 16)
            P.bump("ld", 16)
    block.sync(loads)

    def vsec(v):
        v.wait_ge(P.h("dve"), pre_dve)
        for g in range(NG):
            v.wait_ge(P.h("ld"), ld0 + 16 * (g + 1))
            buf = bufs[g % 2]
            v.bn_stats(mv[:, 0:6], buf[:, 0:512])
            v.bn_stats(mv[:, 6:12], buf[:, 512:1024])
            v.drain()
            ins = v.bn_aggr(mv[:, 12 + 2 * g:14 + 2 * g], mv[:, 0:12])
            ins.then_inc(P.h("dve"), 1)
            P.bump("dve")
        v.drain()
        ins = v.tensor_copy(varc[:, 0:NG], mv[:, 13:12 + 2 * NG:2])
        ins.then_inc(P.h("dve"), 1)
        P.bump("dve")
    block.vector(vsec)
    t_varc = P.cnt("dve")

    def asec(s):
        s.wait_ge(P.h("dve"), t_varc)
        s.activation(varc[:, NG:2 * NG], varc[:, 0:NG], AF.Ln, bias=P.cvec(1e-5))
        s.drain()
        ins = s.activation(rst[:, 0:NG], varc[:, NG:2 * NG], AF.Exp, scale=P.cvec(-0.5))
        ins.then_inc(P.h("act"), 1)
        P.bump("act")
    block.scalar(asec)
    t_rst = P.cnt("act")

    def vsec2(v):
        v.wait_ge(P.h("act"), t_rst)
        ins = v.scalar_tensor_tensor(nmr[:, 0:NG], mv[:, 12:12 + 2 * NG:2], -1.0,
                                     rst[:, 0:NG], op0=ALU.mult, op1=ALU.mult)
        v.drain()
        ins.then_inc(P.h("dve"), 1)
        P.bump("dve")
    block.vector(vsec2)
    return P.cnt("dve")


def emit_stats_finish(P, sld_target):
    block = P.block
    mv, varc = P.bufs["mv"], P.bufs["varc"]
    rst, nmr = P.bufs["rst"], P.bufs["nmr"]

    def vsec(v):
        v.wait_ge(P.h("sld"), sld_target)
        ins = v.tensor_copy(varc[:, 0:NG], mv[:, 13:12 + 2 * NG:2])
        v.drain()
        ins.then_inc(P.h("dve"), 1)
        P.bump("dve")
    block.vector(vsec)
    t_varc = P.cnt("dve")

    def asec(sc):
        sc.wait_ge(P.h("dve"), t_varc)
        sc.activation(varc[:, NG:2 * NG], varc[:, 0:NG], AF.Ln, bias=P.cvec(1e-5))
        sc.drain()
        ins = sc.activation(rst[:, 0:NG], varc[:, NG:2 * NG], AF.Exp, scale=P.cvec(-0.5))
        ins.then_inc(P.h("act"), 1)
        P.bump("act")
    block.scalar(asec)
    t_rst = P.cnt("act")

    def vsec2(v):
        v.wait_ge(P.h("act"), t_rst)
        ins = v.scalar_tensor_tensor(nmr[:, 0:NG], mv[:, 12:12 + 2 * NG:2], -1.0,
                                     rst[:, 0:NG], op0=ALU.mult, op1=ALU.mult)
        v.drain()
        ins.then_inc(P.h("dve"), 1)
        P.bump("dve")
    block.vector(vsec2)
    return P.cnt("dve")


def _emit_writer_stats(P, v, buf, g):
    mv = P.bufs["mv"]
    v.drain()
    v.bn_stats(mv[:, 0:6], buf[:, 0:512])
    v.bn_stats(mv[:, 6:12], buf[:, 512:1024])
    v.drain()
    ins = v.bn_aggr(mv[:, 12 + 2 * g:14 + 2 * g], mv[:, 0:12])
    ins.then_inc(P.h("sld"), 1)
    P.bump("sld")


def emit_l0ffn(P, tag, resid_in, resid_out, w, stats_ready, do_stats=True):
    nc, block = P.nc, P.block

    winT = P.sb(tag + "_winT", [128, 8 * 2048], BF16)
    bin_row = P.sb(tag + "_bin", [1, 2048], BF16)
    woutT = P.sb(tag + "_woutT", [128, 16 * 1024], BF16)
    xs = [P.sb(tag + "_x0", [128, 1024], F32), P.sb(tag + "_x1", [128, 1024], F32)]
    ybfs = [P.sb(tag + "_y0", [128, 1024], BF16), P.sb(tag + "_y1", [128, 1024], BF16)]
    yfms = [P.sb(tag + "_yf0", [128, 1024], BF16), P.sb(tag + "_yf1", [128, 1024], BF16)]
    actss = [P.sb(tag + "_a0", [128, 2048], F32), P.sb(tag + "_a1", [128, 2048], F32)]
    mask = P.sb(tag + "_mask", [128, 2048], F32)
    mbfs = [P.sb(tag + "_mb0", [128, 2048], BF16), P.sb(tag + "_mb1", [128, 2048], BF16)]
    mfms = [P.sb(tag + "_mf0", [128, 2048], BF16), P.sb(tag + "_mf1", [128, 2048], BF16)]
    os_ = [P.sb(tag + "_o0", [128, 1024], F32), P.sb(tag + "_o1", [128, 1024], F32)]

    pls = [P.psum(tag + f"_pl{i}", [128, 512], F32) for i in range(4)]
    pst = P.psum(tag + "_pt", [128, 1024], BF16)
    pst2 = P.psum(tag + "_pu", [128, 2048], BF16)
    po = P.psum(tag + "_po", [128, 512], F32)

    gld0 = P.cnt("gld")

    def wloads(g):
        for k in range(8):
            g.dma_start(out=winT[:, 2048 * k:2048 * (k + 1)],
                        in_=w["winT"][128 * k:128 * (k + 1), :]).then_inc(P.h("gld"), 16)
            P.bump("gld", 16)
        g.dma_start(out=bin_row[:], in_=w["b_in"]).then_inc(P.h("gld"), 16)
        P.bump("gld", 16)
        for k in range(16):
            g.dma_start(out=woutT[:, 1024 * k:1024 * (k + 1)],
                        in_=w["woutT"][128 * k:128 * (k + 1), :]).then_inc(P.h("gld"), 16)
            P.bump("gld", 16)
    block.gpsimd(wloads)
    w_ready = P.cnt("gld")

    dve_pre0 = P.cnt("dve")

    def vpre(v):
        v.memset(mask[:, 0:8], 1.0)
        ins = v.memset(mask[:, 2040:2048], 0.0)
        v.drain()
        ins.then_inc(P.h("dve"), 1)
        P.bump("dve")
    block.vector(vpre)

    ld0 = P.cnt("ld")
    mm0 = P.cnt("mm")
    dve0 = P.cnt("dve")
    act0 = P.cnt("act")
    st0 = P.cnt("st")

    Mtab = {}
    pos = mm0
    for g in range(NG):
        for i in (1, 2):
            pos += 1
            Mtab[(g, i)] = pos
        if g >= 1:
            for i in (3, 4, 5):
                pos += 1
                Mtab[(g - 1, i)] = pos
    for i in (3, 4, 5):
        pos += 1
        Mtab[(NG - 1, i)] = pos
    Dtab = {}
    pos = dve0
    for g in range(NG):
        for i in (1, 2):
            pos += 1
            Dtab[(g, i)] = pos
        if g >= 1:
            for i in (3, 4, 5, 6):
                pos += 1
                Dtab[(g - 1, i)] = pos
    for i in (3, 4, 5, 6):
        pos += 1
        Dtab[(NG - 1, i)] = pos

    def MM(g, i):
        return Mtab[(g, i)]

    def DVE(g, i):
        return Dtab[(g, i)]

    def loads(s):
        for g in range(NG):
            if g >= 2:
                s.wait_ge(P.h("dve"), DVE(g - 2, 6))
            s.dma_start(out=xs[g % 2][:],
                        in_=resid_in[128 * g:128 * (g + 1), :]).then_inc(P.h("ld"), 16)
            P.bump("ld", 16)
    block.sync(loads)

    def tsec(t):
        def emit_ab(g):
            t.wait_ge(P.h("dve"), DVE(g, 1))
            if g == 0:
                t.wait_ge(P.h("gld"), w_ready)
            if g >= 1:
                t.wait_ge(P.h("dve"), DVE(g - 1, 2))
            y = ybfs[g % 2]
            for k in range(8):
                ins = t.transpose(pst[:, 128 * k:128 * (k + 1)],
                                  y[:, 128 * k:128 * (k + 1)], P.bufs["ident"][:])
            ins.then_inc(P.h("mm"), 1)
            P.bump("mm")
            t.wait_ge(P.h("dve"), DVE(g, 2))
            if g >= 1:
                t.wait_ge(P.h("act"), act0 + g)
            yf = yfms[g % 2]
            for nb in range(4):
                for k in range(8):
                    t.matmul(pls[nb][:], yf[:, 128 * k:128 * (k + 1)],
                             winT[:, 2048 * k + 512 * nb:2048 * k + 512 * (nb + 1)],
                             start=(k == 0), stop=False)
                ins = t.matmul(pls[nb][:], P.bufs["ones"][0:1, 0:128],
                               bin_row[0:1, 512 * nb:512 * (nb + 1)],
                               start=False, stop=True)
            ins.then_inc(P.h("mm"), 1)
            P.bump("mm")

        def emit_cd(g):
            t.wait_ge(P.h("dve"), DVE(g, 3))
            if g >= 1:
                t.wait_ge(P.h("dve"), DVE(g - 1, 4))
            mb = mbfs[g % 2]
            for c in range(16):
                ins = t.transpose(pst2[:, 128 * c:128 * (c + 1)],
                                  mb[:, 128 * c:128 * (c + 1)], P.bufs["ident"][:])
            ins.then_inc(P.h("mm"), 1)
            P.bump("mm")
            mf = mfms[g % 2]
            for nb in range(2):
                t.wait_ge(P.h("dve"), DVE(g, 4) if nb == 0 else DVE(g, 5))
                if nb == 0 and g >= 1:
                    t.wait_ge(P.h("dve"), DVE(g - 1, 6))
                for c in range(16):
                    ins = t.matmul(po[:],
                                   mf[:, 128 * c:128 * (c + 1)],
                                   woutT[:, 1024 * c + 512 * nb:1024 * c + 512 * (nb + 1)],
                                   start=(c == 0), stop=(c == 15))
                ins.then_inc(P.h("mm"), 1)
                P.bump("mm")

        for g in range(NG):
            emit_ab(g)
            if g >= 1:
                emit_cd(g - 1)
        emit_cd(NG - 1)
    block.tensor(tsec)

    def asec(s):
        for g in range(NG):
            s.wait_ge(P.h("mm"), MM(g, 2))
            a = actss[g % 2]
            if g >= 2:
                s.wait_ge(P.h("dve"), DVE(g - 2, 3))
            for nb in range(4):
                ins = s.activation(a[:, 512 * nb:512 * (nb + 1)], pls[nb][:], AF.Silu)
            ins.then_inc(P.h("act"), 1)
            P.bump("act")
    block.scalar(asec)

    def vsec(v):
        def emit_ny(g):
            x = xs[g % 2]
            v.wait_ge(P.h("ld"), ld0 + 16 * (g + 1))
            if g == 0:
                v.wait_ge(P.h("dve"), stats_ready)
                v.wait_ge(P.h("dve"), dve_pre0 + 1)
            if g >= 2:
                v.wait_ge(P.h("mm"), MM(g - 2, 1))
            ins = v.tensor_scalar(ybfs[g % 2][:], x[:], P.bufs["rst"][:, g:g + 1],
                                  P.bufs["nmr"][:, g:g + 1], op0=ALU.mult, op1=ALU.add)
            ins.then_inc(P.h("dve"), 1)
            P.bump("dve")
            v.wait_ge(P.h("mm"), MM(g, 1))
            if g >= 2:
                v.wait_ge(P.h("mm"), MM(g - 2, 2))
            ins = v.tensor_copy(yfms[g % 2][:], pst[:])
            ins.then_inc(P.h("dve"), 1)
            P.bump("dve")

        def emit_rest(g):
            x = xs[g % 2]
            v.wait_ge(P.h("act"), act0 + g + 1)
            a = actss[g % 2]
            for d in range(7):
                B = 8 << d
                op, oc = TREE_OFF[d], TREE_OFF[d + 1]
                v.scalar_tensor_tensor(mask[:, oc + B:oc + 2 * B], a[:, op:op + B], 0.0,
                                       mask[:, op:op + B], op0=ALU.is_gt, op1=ALU.mult)
                v.scalar_tensor_tensor(mask[:, oc:oc + B], a[:, op:op + B], 0.0,
                                       mask[:, op:op + B], op0=ALU.is_le, op1=ALU.mult)
                v.drain()
            if g >= 2:
                v.wait_ge(P.h("mm"), MM(g - 2, 3))
            ins = v.tensor_tensor(mbfs[g % 2][:], a[:], mask[:], op=ALU.mult)
            ins.then_inc(P.h("dve"), 1)
            P.bump("dve")
            v.wait_ge(P.h("mm"), MM(g, 3))
            if g >= 2:
                v.wait_ge(P.h("mm"), MM(g - 2, 5))
            ins = v.tensor_copy(mfms[g % 2][:], pst2[:])
            ins.then_inc(P.h("dve"), 1)
            P.bump("dve")
            for nb in range(2):
                v.wait_ge(P.h("mm"), MM(g, 4 + nb))
                if nb == 0 and g >= 2:
                    v.wait_ge(P.h("st"), st0 + 16 * (g - 1))
                ins = v.scalar_tensor_tensor(os_[g % 2][:, 512 * nb:512 * (nb + 1)],
                                             po[:], 0.5, x[:, 512 * nb:512 * (nb + 1)],
                                             op0=ALU.mult, op1=ALU.add)
                ins.then_inc(P.h("dve"), 1)
                P.bump("dve")
            if do_stats:
                _emit_writer_stats(P, v, os_[g % 2], g)

        for g in range(NG):
            emit_ny(g)
            if g >= 1:
                emit_rest(g - 1)
        emit_rest(NG - 1)
    block.vector(vsec)

    def stores(g_):
        for g in range(NG):
            g_.wait_ge(P.h("dve"), DVE(g, 6))
            g_.dma_start(out=resid_out[128 * g:128 * (g + 1), :],
                         in_=os_[g % 2][:]).then_inc(P.h("st"), 16)
            P.bump("st", 16)
    block.gpsimd(stores)
    return P.cnt("st")


def emit_phase_barrier(P):
    block = P.block
    targets = [(nm, P.cnt(nm)) for nm in ["ld", "gld", "sld", "mm", "dve", "act", "pool", "st", "wst"]]

    def mk(engname):
        def f(e):
            for nm, v in targets:
                if v > 0:
                    e.wait_ge(P.h(nm), v)
        return f
    block.sync(mk("sync"))
    block.tensor(mk("tensor"))
    block.vector(mk("vector"))
    block.scalar(mk("scalar"))
    block.gpsimd(mk("gpsimd"))


def emit_decay_prep(P, tag, decay_dram, pre_dve, consts):
    nc, block = P.nc, P.block
    dp = P.sb(tag + "_dp", [1, 1], F32)
    expt = P.sb(tag + "_expt", [128, 128], F32)
    stril = P.sb(tag + "_stril", [128, 128], F32)
    P.bufs["expt"] = expt
    P.bufs["stril"] = stril
    onesf = P.bufs["onesf"]
    lnd1 = P.sb(tag + "_lnd1", [1, 1], F32)
    lndv = P.sb(tag + "_lndv", [128, 1], F32)
    dpows = P.sb(tag + "_dpows", [128, 1], F32)
    drev = P.sb(tag + "_drev", [128, 1], F32)
    d128 = P.sb(tag + "_d128", [128, 1], F32)
    dmt = P.sb(tag + "_dmt", [128, 128], F32)
    dmte = P.sb(tag + "_dmte", [128, 128], F32)
    plnd_cm = P.nc.psum_tensor(tag + "_plnd", [128, 1], F32)
    plnd = plnd_cm.__enter__()

    ld0 = P.cnt("ld")

    def loads(s):
        s.dma_start(out=dp[:], in_=decay_dram).then_inc(P.h("ld"), 16)
        P.bump("ld", 16)
        s.dma_start(out=expt[:], in_=consts["c_expt"]).then_inc(P.h("ld"), 16)
        P.bump("ld", 16)
        s.dma_start(out=stril[:], in_=consts["c_stril"]).then_inc(P.h("ld"), 16)
        P.bump("ld", 16)
    block.sync(loads)
    t_ld = P.cnt("ld")

    t_ones = pre_dve

    def as0(s):
        s.wait_ge(P.h("ld"), t_ld)
        ins = s.activation(dp[:], dp[:], AF.Sigmoid)
        s.drain()
        ins.then_inc(P.h("act"), 1)
        P.bump("act")
    block.scalar(as0)
    t_sig = P.cnt("act")

    def vs1(v):
        v.wait_ge(P.h("act"), t_sig)
        ins = v.tensor_scalar(dp[:], dp[:], 1e-19, None, op0=ALU.max)
        v.drain()
        ins.then_inc(P.h("dve"), 1)
        P.bump("dve")
    block.vector(vs1)
    t_clamp = P.cnt("dve")

    def as1(s):
        s.wait_ge(P.h("dve"), t_clamp)
        ins = s.activation(lnd1[:], dp[:], AF.Ln)
        s.drain()
        ins.then_inc(P.h("act"), 1)
        P.bump("act")
    block.scalar(as1)
    t_lnd = P.cnt("act")

    def ts0(t):
        t.wait_ge(P.h("act"), t_lnd)
        t.wait_ge(P.h("dve"), t_ones)
        ins = t.matmul(plnd[:], onesf[0:1, :], lnd1[:], start=True, stop=True)
        ins.then_inc(P.h("mm"), 1)
        P.bump("mm")
    block.tensor(ts0)
    t_bc = P.cnt("mm")

    def vs2(v):
        v.wait_ge(P.h("mm"), t_bc)
        ins = v.tensor_copy(lndv[:], plnd[:])
        v.drain()
        ins.then_inc(P.h("dve"), 1)
        P.bump("dve")
    block.vector(vs2)
    t_lndv = P.cnt("dve")

    def as2(s):
        s.wait_ge(P.h("dve"), t_lndv)
        s.wait_ge(P.h("pool"), 1)
        s.activation(dpows[:], P.bufs["iota_p"][:], AF.Exp, scale=lndv[:, 0:1])
        s.activation(drev[:], P.bufs["iota_r"][:], AF.Exp, scale=lndv[:, 0:1])
        s.activation(d128[:], lndv[:], AF.Exp, scale=P.cvec(128.0))
        ins = s.activation(dmte[:], P.bufs["expt"][:], AF.Exp, scale=lndv[:, 0:1])
        ins.then_inc(P.h("act"), 1)
        P.bump("act")
    block.scalar(as2)
    t_exps = P.cnt("act")

    def vs3(v):
        v.wait_ge(P.h("act"), t_exps)
        ins = v.tensor_tensor(dmt[:], dmte[:], P.bufs["stril"][:], op=ALU.mult)
        ins.then_inc(P.h("dve"), 1)
        P.bump("dve")
    block.vector(vs3)
    plnd_cm.__exit__(None, None, None)
    return P.cnt("dve"), {"dpows": dpows, "drev": drev, "d128": d128, "dmt": dmt}


def emit_l0attn(P, tag, resid_in, resid_out, w, stats_ready, dec_ready, dv, do_stats=True):
    nc, block = P.nc, P.block
    NB = 8

    wq = P.sb(tag + "_wq", [128, 8 * 1024], BF16)
    wk = P.sb(tag + "_wk", [128, 8 * 1024], BF16)
    wv = P.sb(tag + "_wv", [128, 8 * 1024], BF16)
    wo = P.sb(tag + "_wo", [128, 8 * 1024], BF16)
    brows = P.sb(tag + "_br", [1, 4 * 1024], BF16)
    mem = P.sb(tag + "_mem", [128, 8 * 1024], F32)
    membf = P.sb(tag + "_membf", [128, 8 * 1024], BF16)
    xs = [P.sb(tag + f"_x{i}", [128, 1024], F32) for i in range(2)]
    ybfs = [P.sb(tag + f"_y{i}", [128, 1024], BF16) for i in range(2)]
    yfms = [P.sb(tag + f"_yf{i}", [128, 1024], BF16) for i in range(2)]
    qtm = P.sb(tag + "_qtm", [128, 1024], BF16)
    qfm = P.sb(tag + "_qfm", [128, 1024], BF16)
    ktm = P.sb(tag + "_ktm", [128, 1024], BF16)
    kwtm = P.sb(tag + "_kwtm", [128, 1024], BF16)
    kfm = P.sb(tag + "_kfm", [128, 1024], BF16)
    vbf = P.sb(tag + "_vbf", [128, 1024], BF16)
    rtmp = P.sb(tag + "_rt", [128, 1024], F32)
    etmp = P.sb(tag + "_et", [128, 1024], F32)
    abf = P.sb(tag + "_abf", [128, 128], BF16)
    ctmp = P.sb(tag + "_ct", [128, 1024], F32)
    otm = P.sb(tag + "_otm", [128, 1024], BF16)
    ofm = P.sb(tag + "_ofm", [128, 1024], BF16)
    os_ = [P.sb(tag + f"_o{i}", [128, 1024], F32) for i in range(2)]

    pstA = P.psum(tag + "_pstA", [128, 1024], BF16)
    pstB = P.psum(tag + "_pstB", [128, 1024], BF16)
    pp0 = P.psum(tag + "_pp0", [128, 1024], F32)
    pp1 = P.psum(tag + "_pp1", [128, 1024], F32)
    pp2 = P.psum(tag + "_pp2", [128, 1024], F32)

    def wloads(g):
        for nm, buf in [("wq", wq), ("wk", wk), ("wv", wv), ("wo", wo)]:
            for k in range(8):
                g.dma_start(out=buf[:, 1024 * k:1024 * (k + 1)],
                            in_=w[nm][128 * k:128 * (k + 1), :]).then_inc(P.h("gld"), 16)
                P.bump("gld", 16)
        for i, nm in enumerate(["bq", "bk", "bv", "bo"]):
            g.dma_start(out=brows[0:1, 1024 * i:1024 * (i + 1)],
                        in_=w[nm]).then_inc(P.h("gld"), 16)
            P.bump("gld", 16)
    block.gpsimd(wloads)
    w_ready = P.cnt("gld")

    dve_pre0 = P.cnt("dve")

    def vpre(v):
        v.memset(mem[:], 0.0)
        ins = v.memset(membf[:], 0.0)
        ins.then_inc(P.h("dve"), 1)
        P.bump("dve")
    block.vector(vpre)
    mem_init = P.cnt("dve")

    ld0 = P.cnt("ld")
    mm0 = P.cnt("mm")
    dve0 = P.cnt("dve")
    act0 = P.cnt("act")
    pool0 = P.cnt("pool")
    st0 = P.cnt("st")

    NMM = 19
    MMI = {"ytrans": 1, "qmm": 2, "qtrans": 3, "kmm": 4, "ktrans": 5, "vmm": 6,
           "scores": 7, "cross": 8, "intra": 9, "atrans": 10, "fin": 11}
    NDVE = 21
    DVI = {"norm": 1, "yfm": 2, "qevac": 3, "qfm": 4, "kelu": 5, "kw": 6, "kfm": 7,
           "vevac": 8, "aevac": 9, "cscale": 10, "ocomb": 11, "ofm": 12, "resid": 13}
    NACT = 2
    NPOOL = 8

    def MM(c, nm):
        return mm0 + NMM * c + (MMI[nm] if isinstance(nm, str) else nm)

    def MMU(c, m):
        return mm0 + NMM * c + 11 + (m + 1)

    def DVE(c, nm):
        return dve0 + NDVE * c + DVI[nm]

    def DVU(c, m):
        return dve0 + NDVE * c + 13 + (m + 1)

    def loads(s):
        for c in range(NG):
            if c >= 2:
                s.wait_ge(P.h("dve"), DVE(c - 2, "resid"))
            s.dma_start(out=xs[c % 2][:],
                        in_=resid_in[128 * c:128 * (c + 1), :]).then_inc(P.h("ld"), 16)
            P.bump("ld", 16)
    block.sync(loads)

    ONES = P.bufs["ones"]

    def tsec(t):
        for c in range(NG):
            t.wait_ge(P.h("dve"), DVE(c, "norm"))
            if c == 0:
                t.wait_ge(P.h("gld"), w_ready)
            if c >= 1:
                t.wait_ge(P.h("dve"), DVE(c - 1, "kfm"))
            for k in range(8):
                ins = t.transpose(pstA[:, 128 * k:128 * (k + 1)],
                                  ybfs[c % 2][:, 128 * k:128 * (k + 1)], P.bufs["ident"][:])
            ins.then_inc(P.h("mm"), 1)
            P.bump("mm")
            yf = yfms[c % 2]
            t.wait_ge(P.h("dve"), DVE(c, "yfm"))
            if c >= 1:
                t.wait_ge(P.h("dve"), DVU(c - 1, 6))
            for nb in range(2):
                for k in range(8):
                    t.matmul(pp0[:, 512 * nb:512 * (nb + 1)], yf[:, 128 * k:128 * (k + 1)],
                             wq[:, 1024 * k + 512 * nb:1024 * k + 512 * (nb + 1)],
                             start=(k == 0), stop=False)
                ins = t.matmul(pp0[:, 512 * nb:512 * (nb + 1)], ONES[0:1, 0:128],
                               brows[0:1, 512 * nb:512 * (nb + 1)], start=False, stop=True)
            ins.then_inc(P.h("mm"), 1)
            P.bump("mm")
            t.wait_ge(P.h("dve"), DVE(c, "qevac"))
            if c >= 1:
                t.wait_ge(P.h("dve"), DVE(c - 1, "ofm"))
            for k in range(8):
                ins = t.transpose(pstB[:, 128 * k:128 * (k + 1)],
                                  qtm[:, 128 * k:128 * (k + 1)], P.bufs["ident"][:])
            ins.then_inc(P.h("mm"), 1)
            P.bump("mm")
            if c >= 1:
                t.wait_ge(P.h("dve"), DVU(c - 1, 7))
            for nb in range(2):
                for k in range(8):
                    t.matmul(pp1[:, 512 * nb:512 * (nb + 1)], yf[:, 128 * k:128 * (k + 1)],
                             wk[:, 1024 * k + 512 * nb:1024 * k + 512 * (nb + 1)],
                             start=(k == 0), stop=False)
                ins = t.matmul(pp1[:, 512 * nb:512 * (nb + 1)], ONES[0:1, 0:128],
                               brows[0:1, 1024 + 512 * nb:1024 + 512 * (nb + 1)],
                               start=False, stop=True)
            ins.then_inc(P.h("mm"), 1)
            P.bump("mm")
            t.wait_ge(P.h("dve"), DVE(c, "kelu"))
            t.wait_ge(P.h("dve"), DVE(c, "yfm"))
            for k in range(8):
                ins = t.transpose(pstA[:, 128 * k:128 * (k + 1)],
                                  ktm[:, 128 * k:128 * (k + 1)], P.bufs["ident"][:])
            ins.then_inc(P.h("mm"), 1)
            P.bump("mm")
            if c >= 1:
                t.wait_ge(P.h("dve"), DVE(c - 1, "resid"))
            for nb in range(2):
                for k in range(8):
                    t.matmul(pp2[:, 512 * nb:512 * (nb + 1)], yf[:, 128 * k:128 * (k + 1)],
                             wv[:, 1024 * k + 512 * nb:1024 * k + 512 * (nb + 1)],
                             start=(k == 0), stop=False)
                ins = t.matmul(pp2[:, 512 * nb:512 * (nb + 1)], ONES[0:1, 0:128],
                               brows[0:1, 2048 + 512 * nb:2048 + 512 * (nb + 1)],
                               start=False, stop=True)
            ins.then_inc(P.h("mm"), 1)
            P.bump("mm")
            t.wait_ge(P.h("dve"), DVE(c, "qfm"))
            t.wait_ge(P.h("dve"), DVE(c, "kfm"))
            t.wait_ge(P.h("dve"), DVE(c, "qevac"))
            for k in range(8):
                ins = t.matmul(pp0[:, 0:128], kfm[:, 128 * k:128 * (k + 1)],
                               qfm[:, 128 * k:128 * (k + 1)], start=(k == 0), stop=(k == 7))
            ins.then_inc(P.h("mm"), 1)
            P.bump("mm")
            t.wait_ge(P.h("dve"), DVE(c, "aevac"))
            if c >= 1:
                t.wait_ge(P.h("pool"), pool0 + NPOOL * c)
            for nb in range(2):
                for dk in range(8):
                    ins = t.matmul(pp0[:, 512 * nb:512 * (nb + 1)], qfm[:, 128 * dk:128 * (dk + 1)],
                                   membf[:, 1024 * dk + 512 * nb:1024 * dk + 512 * (nb + 1)],
                                   start=(dk == 0), stop=(dk == 7))
            ins.then_inc(P.h("mm"), 1)
            P.bump("mm")
            t.wait_ge(P.h("act"), act0 + NACT * c + 2)
            t.wait_ge(P.h("dve"), DVE(c, "aevac"))
            t.wait_ge(P.h("dve"), DVE(c, "vevac"))
            for nb in range(2):
                ins = t.matmul(pp1[:, 512 * nb:512 * (nb + 1)], abf[:],
                               vbf[:, 512 * nb:512 * (nb + 1)], start=True, stop=True)
            ins.then_inc(P.h("mm"), 1)
            P.bump("mm")
            t.wait_ge(P.h("dve"), DVE(c, "ocomb"))
            t.wait_ge(P.h("dve"), DVE(c, "qfm"))
            for k in range(8):
                ins = t.transpose(pstB[:, 128 * k:128 * (k + 1)],
                                  otm[:, 128 * k:128 * (k + 1)], P.bufs["ident"][:])
            ins.then_inc(P.h("mm"), 1)
            P.bump("mm")
            t.wait_ge(P.h("dve"), DVE(c, "ofm"))
            t.wait_ge(P.h("dve"), DVE(c, "vevac"))
            for nb in range(2):
                for k in range(8):
                    t.matmul(pp2[:, 512 * nb:512 * (nb + 1)], ofm[:, 128 * k:128 * (k + 1)],
                             wo[:, 1024 * k + 512 * nb:1024 * k + 512 * (nb + 1)],
                             start=(k == 0), stop=False)
                ins = t.matmul(pp2[:, 512 * nb:512 * (nb + 1)], ONES[0:1, 0:128],
                               brows[0:1, 3072 + 512 * nb:3072 + 512 * (nb + 1)],
                               start=False, stop=True)
            ins.then_inc(P.h("mm"), 1)
            P.bump("mm")
            t.wait_ge(P.h("dve"), DVE(c, "kw"))
            for m in range(8):
                pp = pp0 if m % 2 == 0 else pp1
                if m == 0:
                    t.wait_ge(P.h("dve"), DVE(c, "cscale"))
                if m == 1:
                    t.wait_ge(P.h("dve"), DVE(c, "ocomb"))
                if m >= 2:
                    t.wait_ge(P.h("dve"), DVU(c, m - 2))
                for nb in range(2):
                    ins = t.matmul(pp[:, 512 * nb:512 * (nb + 1)], kwtm[:, 128 * m:128 * (m + 1)],
                                   vbf[:, 512 * nb:512 * (nb + 1)], start=True, stop=True)
                ins.then_inc(P.h("mm"), 1)
                P.bump("mm")
    block.tensor(tsec)

    def asec(s):
        for c in range(NG):
            s.wait_ge(P.h("mm"), MM(c, "kmm"))
            if c >= 1:
                s.wait_ge(P.h("dve"), DVE(c - 1, "kelu"))
            s.activation(rtmp[:], pp1[:], AF.Relu).then_inc(P.h("act"), 1)
            P.bump("act")
            s.activation(etmp[:], pp1[:], AF.Exp).then_inc(P.h("act"), 1)
            P.bump("act")
            for m in range(8):
                if m % 4 == 0:
                    continue
                s.wait_ge(P.h("dve"), DVU(c, m))
                s.wait_ge(P.h("mm"), MM(c, "cross"))
                ins = s.copy(membf[:, 1024 * m:1024 * (m + 1)],
                             mem[:, 1024 * m:1024 * (m + 1)])
                ins.then_inc(P.h("pool"), 1)
                P.bump("pool")
    block.scalar(asec)

    def vsec(v):
        for c in range(NG):
            x = xs[c % 2]
            v.wait_ge(P.h("ld"), ld0 + 16 * (c + 1))
            if c == 0:
                v.wait_ge(P.h("dve"), stats_ready)
                v.wait_ge(P.h("dve"), dec_ready)
                v.wait_ge(P.h("dve"), mem_init)
            if c >= 2:
                v.wait_ge(P.h("mm"), MM(c - 2, "ytrans"))
            ins = v.tensor_scalar(ybfs[c % 2][:], x[:], P.bufs["rst"][:, c:c + 1],
                                  P.bufs["nmr"][:, c:c + 1], op0=ALU.mult, op1=ALU.add)
            ins.then_inc(P.h("dve"), 1)
            P.bump("dve")
            v.wait_ge(P.h("mm"), MM(c, "ytrans"))
            if c >= 2:
                v.wait_ge(P.h("mm"), MM(c - 2, "vmm"))
            ins = v.tensor_copy(yfms[c % 2][:], pstA[:])
            ins.then_inc(P.h("dve"), 1)
            P.bump("dve")
            v.wait_ge(P.h("mm"), MM(c, "qmm"))
            if c >= 1:
                v.wait_ge(P.h("mm"), MM(c - 1, "qtrans"))
            ins = v.tensor_copy(qtm[:], pp0[:])
            ins.then_inc(P.h("dve"), 1)
            P.bump("dve")
            v.wait_ge(P.h("mm"), MM(c, "qtrans"))
            if c >= 1:
                v.wait_ge(P.h("mm"), MM(c - 1, "cross"))
            ins = v.tensor_copy(qfm[:], pstB[:])
            ins.then_inc(P.h("dve"), 1)
            P.bump("dve")
            v.wait_ge(P.h("act"), act0 + NACT * c + 2)
            if c >= 1:
                v.wait_ge(P.h("mm"), MM(c - 1, "ktrans"))
            ins = v.scalar_tensor_tensor(ktm[:], etmp[:], 1.0, rtmp[:],
                                         op0=ALU.min, op1=ALU.add)
            v.drain()
            ins.then_inc(P.h("dve"), 1)
            P.bump("dve")
            if c >= 1:
                v.wait_ge(P.h("mm"), MMU(c - 1, 7))
            ins = v.tensor_scalar(kwtm[:], ktm[:], dv["drev"][:, 0:1], None, op0=ALU.mult)
            ins.then_inc(P.h("dve"), 1)
            P.bump("dve")
            v.wait_ge(P.h("mm"), MM(c, "ktrans"))
            if c >= 1:
                v.wait_ge(P.h("mm"), MM(c - 1, "scores"))
            ins = v.tensor_copy(kfm[:], pstA[:])
            ins.then_inc(P.h("dve"), 1)
            P.bump("dve")
            v.wait_ge(P.h("mm"), MM(c, "vmm"))
            if c >= 1:
                v.wait_ge(P.h("mm"), MMU(c - 1, 7))
            ins = v.tensor_copy(vbf[:], pp2[:])
            ins.then_inc(P.h("dve"), 1)
            P.bump("dve")
            v.wait_ge(P.h("mm"), MM(c, "scores"))
            if c >= 1:
                v.wait_ge(P.h("mm"), MM(c - 1, "intra"))
            ins = v.tensor_tensor(abf[:], pp0[:, 0:128], dv["dmt"][:], op=ALU.mult)
            ins.then_inc(P.h("dve"), 1)
            P.bump("dve")
            v.wait_ge(P.h("mm"), MM(c, "cross"))
            ins = v.tensor_scalar(ctmp[:], pp0[:], dv["dpows"][:, 0:1], None, op0=ALU.mult)
            ins.then_inc(P.h("dve"), 1)
            P.bump("dve")
            v.wait_ge(P.h("mm"), MM(c, "intra"))
            if c >= 1:
                v.wait_ge(P.h("mm"), MM(c - 1, "atrans"))
            ins = v.tensor_tensor(otm[:], ctmp[:], pp1[:], op=ALU.add)
            ins.then_inc(P.h("dve"), 1)
            P.bump("dve")
            v.wait_ge(P.h("mm"), MM(c, "atrans"))
            if c >= 1:
                v.wait_ge(P.h("mm"), MM(c - 1, "fin"))
            ins = v.tensor_copy(ofm[:], pstB[:])
            ins.then_inc(P.h("dve"), 1)
            P.bump("dve")
            v.wait_ge(P.h("mm"), MM(c, "fin"))
            if c >= 2:
                v.wait_ge(P.h("st"), st0 + 16 * (c - 1))
            ins = v.tensor_tensor(os_[c % 2][:], x[:], pp2[:], op=ALU.add)
            ins.then_inc(P.h("dve"), 1)
            P.bump("dve")
            if do_stats:
                _emit_writer_stats(P, v, os_[c % 2], c)
            for m in range(8):
                pp = pp0 if m % 2 == 0 else pp1
                v.wait_ge(P.h("mm"), MMU(c, m))
                if c >= 1:
                    v.wait_ge(P.h("pool"), pool0 + NPOOL * c)
                ins = v.scalar_tensor_tensor(mem[:, 1024 * m:1024 * (m + 1)],
                                             mem[:, 1024 * m:1024 * (m + 1)],
                                             dv["d128"][:, 0:1], pp[:],
                                             op0=ALU.mult, op1=ALU.add)
                ins.then_inc(P.h("dve"), 1)
                P.bump("dve")
    block.vector(vsec)

    def psec(g):
        for c in range(NG):
            g.wait_ge(P.h("dve"), DVE(c, "resid"))
            g.dma_start(out=resid_out[128 * c:128 * (c + 1), :],
                        in_=os_[c % 2][:]).then_inc(P.h("st"), 16)
            P.bump("st", 16)
            for m in range(8):
                if m % 4 != 0:
                    continue
                g.wait_ge(P.h("dve"), DVU(c, m))
                g.wait_ge(P.h("mm"), MM(c, "cross"))
                ins = g.tensor_copy(membf[:, 1024 * m:1024 * (m + 1)],
                                    mem[:, 1024 * m:1024 * (m + 1)])
                ins.then_inc(P.h("pool"), 1)
                P.bump("pool")
    block.gpsimd(psec)
    return P.cnt("st")


def emit_stage_bf16(P, tag, jobs):
    nc, block = P.nc, P.block
    bufs = [P.sb(tag + f"_b{i}", [128, 4096], BF16) for i in range(2)]
    chunks = []
    for (src, dst, rows, cols) in jobs:
        for r0 in range(0, rows, 128):
            chunks.append((src, dst, r0, cols))
    gld0 = P.cnt("gld")
    wst0 = P.cnt("wst")

    def gsec(g):
        for i, (src, dst, r0, cols) in enumerate(chunks):
            if i >= 2:
                g.wait_ge(P.h("wst"), wst0 + 16 * (i - 1))
            g.dma_start(out=bufs[i % 2][:, 0:cols],
                        in_=src[r0:r0 + 128, :]).then_inc(P.h("gld"), 16)
            P.bump("gld", 16)
    block.gpsimd(gsec)

    def ssec(sy):
        for i, (src, dst, r0, cols) in enumerate(chunks):
            sy.wait_ge(P.h("gld"), gld0 + 16 * (i + 1))
            sy.dma_start(out=dst[r0:r0 + 128, :],
                         in_=bufs[i % 2][:, 0:cols]).then_inc(P.h("wst"), 16)
            P.bump("wst", 16)
    block.scalar(ssec)
    return P.cnt("wst")


def emit_l1ffn(P, tag, resid_in, resid_out, w, stats_ready, do_stats=True, reuse=None):
    import os
    nc, block = P.nc, P.block
    NT = int(os.environ.get("L1FFN_NT", "8"))

    if reuse is not None:
        (w1t, w2b, b1c, b2r, scs, sc16, xs, ybf, yfm, hfm, lrows, ocols1,
         spread, prows, d2k, e2f, pst, pbs, pout) = reuse
    else:
        w1t = P.sb(tag + "_w1t", [128, 8 * 4096], BF16)
        w2b = [P.sb(tag + f"_w2b{i}", [128, 8 * 1024], BF16) for i in range(2)]
        b1c = P.sb(tag + "_b1c", [128, 32], F32)
        b2r = P.sb(tag + "_b2r", [1, 1024], BF16)
        scs = P.sb(tag + "_scs", [1, 1], F32)
        sc16 = P.sb(tag + "_sc16", [16, 1], F32)
        xs = [P.sb(tag + f"_x{i}", [128, 1024], F32) for i in range(2)]
        ybf = P.sb(tag + "_y0", [128, 1024], BF16)
        yfm = P.sb(tag + "_yf", [128, 8 * 512], BF16)
        hfm = P.sb(tag + "_h", [128, 32 * 512], BF16)
        lrows = P.sb(tag + "_lr", [1, 8 * 1024], BF16)
        ocols1 = P.sb(tag + "_oc1", [128, 32], BF16)
        spread = P.sb(tag + "_spr", [128, 32 * 16], BF16)
        prows = P.sb(tag + "_pr", [1, 2048], BF16)
        d2k = P.sb(tag + "_d2k", [16, 512], BF16)
        e2f = P.sb(tag + "_e2f", [128, 1024], BF16)
        pst = P.psum(tag + "_pt", [128, 1024], BF16)
        pbs = [P.psum(tag + f"_pb{i}", [128, 512], F32) for i in range(4)]
        pout = P.psum(tag + "_po", [128, 1024], F32)
    os_ = [P.bufs["swp0"], P.bufs["swp1"]]

    ONES = P.bufs["ones"]
    IDENT = P.bufs["ident"]

    stage_done = P.cnt("wst")
    if reuse is None:
        def wloads(g):
            g.wait_ge(P.h("wst"), stage_done)
            for k in range(8):
                g.dma_start(out=w1t[:, 4096 * k:4096 * (k + 1)],
                            in_=w["w1bf"][128 * k:128 * (k + 1), :]).then_inc(P.h("gld"), 16)
                P.bump("gld", 16)
            for nm, buf in [("b1", b1c), ("b2", b2r)]:
                g.dma_start(out=buf[:], in_=w[nm]).then_inc(P.h("gld"), 16)
                P.bump("gld", 16)
            g.dma_start(out=scs[:], in_=w["scale"]).then_inc(P.h("gld"), 16)
            P.bump("gld", 16)
        block.gpsimd(wloads)
    w_ready = P.cnt("gld")

    def load0(s):
        s.dma_start(out=xs[0][:], in_=resid_in[0:128, :]).then_inc(P.h("ld"), 16)
        P.bump("ld", 16)
    block.sync(load0)
    t_ld0 = P.cnt("ld")

    def vnorm0(v):
        v.wait_ge(P.h("ld"), t_ld0)
        v.wait_ge(P.h("dve"), stats_ready)
        ins = v.tensor_scalar(ybf[:], xs[0][:], P.bufs["rst"][:, 0:1],
                              P.bufs["nmr"][:, 0:1], op0=ALU.mult, op1=ALU.add)
        ins.then_inc(P.h("dve"), 1)
        P.bump("dve")
    block.vector(vnorm0)
    t_n0 = P.cnt("dve")

    def tt0(t):
        t.wait_ge(P.h("dve"), t_n0)
        if reuse is None:
            t.wait_ge(P.h("gld"), w_ready)
        for k in range(8):
            ins = t.transpose(pst[:, 128 * k:128 * (k + 1)],
                              ybf[:, 128 * k:128 * (k + 1)], IDENT[:])
        ins.then_inc(P.h("mm"), 1)
        P.bump("mm")
    block.tensor(tt0)
    t_tt0 = P.cnt("mm")

    def vyf0(v):
        v.wait_ge(P.h("mm"), t_tt0)
        ins = v.tensor_copy(
            yfm[:].rearrange("p (k s) -> p k s", s=512)[:, :, 0:128],
            pst[:].rearrange("p (k s) -> p k s", s=128))
        ins.then_inc(P.h("dve"), 1)
        P.bump("dve")
    block.vector(vyf0)
    t_yf0 = P.cnt("dve")

    mmp0 = P.cnt("mm")
    dvp0 = P.cnt("dve")

    def pr_tsec(t):
        t.wait_ge(P.h("dve"), t_yf0)
        for j in range(16):
            if j >= 2:
                t.wait_ge(P.h("dve"), dvp0 + (j - 1))
            half = (j % 2) * 512
            for q in range(4):
                c = 4 * j + q
                tok, dk = 1 + 2 * (c // 8), c % 8
                ins = t.matmul(pout[0:1, half + 128 * q:half + 128 * (q + 1)],
                               yfm[:, 512 * dk + tok:512 * dk + tok + 1],
                               IDENT[:], start=True, stop=True)
            ins.then_inc(P.h("mm"), 1)
            P.bump("mm")
        t.wait_ge(P.h("dve"), dvp0 + 16)
        ins = t.matmul(pout[0:16, 0:1], P.bufs["onesf"][0:1, 0:16], scs[:],
                       start=True, stop=True)
        ins.then_inc(P.h("mm"), 1)
        P.bump("mm")
    block.tensor(pr_tsec)
    t_sc16mm = P.cnt("mm")

    def pr_vsec(v):
        for j in range(16):
            v.wait_ge(P.h("mm"), mmp0 + j + 1)
            half = (j % 2) * 512
            ins = v.tensor_copy(lrows[0:1, 512 * j:512 * (j + 1)],
                                pout[0:1, half:half + 512])
            ins.then_inc(P.h("dve"), 1)
            P.bump("dve")
        v.drain()
        for dk in range(8):
            v.tensor_copy(ocols1[:, 4 * dk:4 * (dk + 1)],
                          yfm[:, 512 * dk + 2:512 * dk + 9:2])
        v.memset(spread[:], 0.0)
        v.drain()
        for q in range(32):
            hi, dk = q // 8, q % 8
            v.tensor_copy(spread[:, 16 * q + hi:16 * q + 16:4],
                          yfm[:, 512 * dk + 10:512 * dk + 17:2])
        v.memset(P.bufs["swp0"][:], 0.0)
        v.drain()
        v.wait_ge(P.h("mm"), t_sc16mm)
        ins = v.tensor_copy(sc16[:], pout[0:16, 0:1])
        v.drain()
        ins.then_inc(P.h("dve"), 1)
        P.bump("dve")
    block.vector(pr_vsec)
    t_pr_done = P.cnt("dve")

    def pr_gz(g):
        g.wait_ge(P.h("dve"), t_pr_done)
        ez = w["e2x"].rearrange("(p a) c -> p (a c)", p=128)
        g.dma_start(out=ez[:], in_=P.bufs["swp0"][:]).then_inc(P.h("gld"), 16)
        P.bump("gld", 16)
    block.gpsimd(pr_gz)
    t_ez0 = P.cnt("gld")

    def pr_sync(s):
        s.wait_ge(P.h("dve"), t_pr_done)
        s.wait_ge(P.h("gld"), t_ez0)
        dst2 = w["e2x"].rearrange("(ro oo) c16 -> ro oo c16", ro=4)
        for ro in range(4):
            s.dma_start(out=dst2[ro:ro + 1, :, 4 * ro:4 * ro + 4],
                        in_=lrows[0:1, (4 + ro) * 1024:(5 + ro) * 1024]
                        .rearrange("o (oo hi) -> o oo hi", hi=4)).then_inc(P.h("ld"), 16)
            P.bump("ld", 16)
        s.wait_ge(P.h("ld"), P.cnt("ld"))
        s.dma_start_transpose(e2f[:], w["e2x"]).then_inc(P.h("ld"), 16)
        P.bump("ld", 16)
    block.sync(pr_sync)
    t_e2f = P.cnt("ld")

    mm0 = P.cnt("mm")
    dve0 = P.cnt("dve")
    act0 = P.cnt("act")
    st0 = P.cnt("st")
    ld1 = P.cnt("ld")
    wst0 = P.cnt("wst")

    NMM = 23
    NDVE = 19
    NACT = 32
    NLD = 8

    def MM(i, k):
        return mm0 + NMM * i + k

    def DVE(i, k):
        return dve0 + NDVE * i + k

    def ACT(i, r, q):
        return act0 + NACT * i + 4 * r + q + 1

    def LD(i, k):
        return ld1 + 16 * (NLD * i + k)

    def WST(i, j):
        return wst0 + 16 * (8 * i + j + 1)

    def loads(s):
        def w2load(i, j):
            q4 = j % 4
            if 8 * i + j >= 2:
                pj = 8 * i + j - 2
                s.wait_ge(P.h("mm"), MM(pj // 8, 16 + pj % 8))
            else:
                s.wait_ge(P.h("wst"), stage_done)
            src = w["w2bf"][1024 * q4:1024 * (q4 + 1), :]
            s.dma_start(out=w2b[j % 2][:].rearrange("p (c d) -> p c d", d=1024),
                        in_=src.rearrange("(c p) d -> p c d", p=128)
                        ).then_inc(P.h("wst"), 16)
            P.bump("wst", 16)

        for i in range(NT):
            for sub in range(4):
                if i == 0 and sub == 0:
                    s.wait_ge(P.h("dve"), t_n0)
                    s.dma_start(out=xs[0][:], in_=resid_in[0:128, :]).then_inc(P.h("ld"), 16)
                    P.bump("ld", 16)
                    continue
                if sub < 2:
                    if i >= 1:
                        s.wait_ge(P.h("dve"), DVE(i - 1, 17 + 2 * sub))
                else:
                    s.wait_ge(P.h("dve"), DVE(i, 2 * (sub - 2) + 1))
                s.dma_start(out=xs[sub % 2][:],
                            in_=resid_in[512 * i + 128 * sub:512 * i + 128 * (sub + 1), :]
                            ).then_inc(P.h("ld"), 16)
                P.bump("ld", 16)
            def xr(sub):
                if sub < 2:
                    s.wait_ge(P.h("dve"), DVE(i, 2 * sub + 5))
                else:
                    s.wait_ge(P.h("dve"), DVE(i, 12 + 2 * (sub - 2) + 1))
                s.dma_start(out=xs[sub % 2][:],
                            in_=resid_in[512 * i + 128 * sub:512 * i + 128 * (sub + 1), :]
                            ).then_inc(P.h("ld"), 16)
                P.bump("ld", 16)

            for j in range(4):
                w2load(i, j)
            xr(0)
            xr(1)
            w2load(i, 4)
            w2load(i, 5)
            xr(2)
            xr(3)
            w2load(i, 6)
            w2load(i, 7)
    block.sync(loads)

    def tsec(t):
        for i in range(NT):
            for sub in range(4):
                if i == 0 and sub == 0:
                    t.wait_ge(P.h("dve"), t_yf0)
                    ins = t.transpose(pst[:, 0:128], ybf[:, 0:128], IDENT[:])
                    ins.then_inc(P.h("mm"), 1)
                    P.bump("mm")
                    continue
                t.wait_ge(P.h("dve"), DVE(i, 2 * sub + 1))
                if sub >= 1:
                    if i == 0 and sub == 1:
                        t.wait_ge(P.h("dve"), t_yf0)
                    else:
                        t.wait_ge(P.h("dve"), DVE(i, 2 * sub))
                else:
                    t.wait_ge(P.h("dve"), DVE(i - 1, 8))
                for k in range(8):
                    ins = t.transpose(pst[:, 128 * k:128 * (k + 1)],
                                      ybf[:, 128 * k:128 * (k + 1)], IDENT[:])
                ins.then_inc(P.h("mm"), 1)
                P.bump("mm")
            t.wait_ge(P.h("dve"), DVE(i, 8))
            if i == 0:
                t.wait_ge(P.h("dve"), t_pr_done)
            if i >= 1:
                t.wait_ge(P.h("dve"), DVE(i - 1, 19))
            for pr in range(2):
                if pr == 1:
                    t.wait_ge(P.h("dve"), DVE(i, 9))
                for r in (0, 1) if pr == 0 else (2, 3):
                    for dk in range(8):
                        ins = t.matmul(pout[0:1, 512 * (r % 2):512 * (r % 2) + 512],
                                       ocols1[:, 4 * dk + r:4 * dk + r + 1],
                                       yfm[:, 512 * dk:512 * (dk + 1)],
                                       start=(dk == 0), stop=(dk == 7))
                ins.then_inc(P.h("mm"), 1)
                P.bump("mm")
            for r in range(8):
                for q in range(4):
                    if i > 0 or r > 0:
                        pr_, pi = (r - 1, i) if r > 0 else (7, i - 1)
                        t.wait_ge(P.h("act"), ACT(pi, pr_, q))
                    if r == 0 and q == 0:
                        t.wait_ge(P.h("dve"), DVE(i, 10))
                        if i >= 1:
                            t.wait_ge(P.h("dve"), DVE(i - 1, 19))
                    mH = 4 * r + q
                    for k in range(8):
                        t.matmul(pbs[q][:], w1t[:, 4096 * k + 128 * mH:4096 * k + 128 * (mH + 1)],
                                 yfm[:, 512 * k:512 * (k + 1)],
                                 start=(k == 0), stop=False)
                    rr = mH // 8
                    ins = t.matmul(pbs[q][:],
                                   lrows[0:1, 1024 * rr + 128 * (mH % 8):
                                         1024 * rr + 128 * (mH % 8) + 128],
                                   prows[0:1, 512 * rr:512 * (rr + 1)],
                                   start=False, stop=True)
                ins.then_inc(P.h("mm"), 1)
                P.bump("mm")
            t.wait_ge(P.h("act"), ACT(i, 7, 3))
            t.wait_ge(P.h("dve"), DVE(i, 10))
            for q in range(32):
                ins = t.matmul(pout[0:16, 0:512], spread[:, 16 * q:16 * (q + 1)],
                               hfm[:, 512 * q:512 * (q + 1)],
                               start=(q == 0), stop=(q == 31))
            ins.then_inc(P.h("mm"), 1)
            P.bump("mm")
            for pair in range(2):
                for q4 in range(4):
                    j = 4 * pair + q4
                    t.wait_ge(P.h("wst"), WST(i, j))
                    if q4 == 0:
                        t.wait_ge(P.h("dve"), DVE(i, 11))
                        if i == 0 and pair == 0:
                            t.wait_ge(P.h("ld"), t_e2f)
                        if pair == 1:
                            t.wait_ge(P.h("dve"), DVE(i, 15))
                        elif i >= 1:
                            t.wait_ge(P.h("dve"), DVE(i - 1, 19))
                    for ms in range(2):
                        msub = 2 * pair + ms
                        for nb in range(2):
                            for k in range(8):
                                hb = 8 * q4 + k
                                ins = t.matmul(pbs[2 * ms + nb][:],
                                               hfm[:, 512 * hb + 128 * msub:512 * hb + 128 * (msub + 1)],
                                               w2b[j % 2][:, 1024 * k + 512 * nb:1024 * k + 512 * (nb + 1)],
                                               start=(q4 == 0 and k == 0), stop=False)
                    if q4 == 3:
                        for ms in range(2):
                            msub = 2 * pair + ms
                            for nb in range(2):
                                t.matmul(pbs[2 * ms + nb][:], ONES[0:1, 0:128],
                                         b2r[0:1, 512 * nb:512 * (nb + 1)],
                                         start=False, stop=False)
                                ins = t.matmul(pbs[2 * ms + nb][:],
                                               d2k[0:16, 128 * msub:128 * (msub + 1)],
                                               e2f[0:16, 512 * nb:512 * (nb + 1)],
                                               start=False, stop=True)
                    ins.then_inc(P.h("mm"), 1)
                    P.bump("mm")
    block.tensor(tsec)

    def asec(s):
        for i in range(NT):
            for r in range(8):
                for q in range(4):
                    s.wait_ge(P.h("mm"), MM(i, 6 + (r + 1)))
                    if i == 0 and r == 0 and q == 0 and reuse is None:
                        s.wait_ge(P.h("gld"), w_ready)
                    mH = 4 * r + q
                    ins = s.activation(hfm[:, 512 * mH:512 * (mH + 1)], pbs[q][:],
                                       AF.Gelu, bias=b1c[:, mH:mH + 1])
                    ins.then_inc(P.h("act"), 1)
                    P.bump("act")
    block.scalar(asec)

    def vsec(v):
        for i in range(NT):
            for sub in range(4):
                if i == 0 and sub == 0:
                    for _ in range(2):
                        ins = v.tensor_copy(ybf[:, 0:1], ybf[:, 0:1])
                        ins.then_inc(P.h("dve"), 1)
                        P.bump("dve")
                    continue
                g = 4 * i + sub
                v.wait_ge(P.h("ld"), LD(i, sub + 1))
                if sub >= 1:
                    v.wait_ge(P.h("mm"), MM(i, sub))
                else:
                    v.wait_ge(P.h("mm"), MM(i - 1, 4))
                ins = v.tensor_scalar(ybf[:], xs[sub % 2][:], P.bufs["rst"][:, g:g + 1],
                                      P.bufs["nmr"][:, g:g + 1], op0=ALU.mult, op1=ALU.add)
                ins.then_inc(P.h("dve"), 1)
                P.bump("dve")
                v.wait_ge(P.h("mm"), MM(i, 1 + sub))
                if i >= 1 and sub == 0:
                    v.wait_ge(P.h("mm"), MM(i - 1, 15))
                ins = v.tensor_copy(
                    yfm[:].rearrange("p (k s) -> p k s", s=512)[:, :, 128 * sub:128 * (sub + 1)],
                    pst[:].rearrange("p (k s) -> p k s", s=128))
                ins.then_inc(P.h("dve"), 1)
                P.bump("dve")
            for pr in range(2):
                v.wait_ge(P.h("mm"), MM(i, 5 + pr))
                ins = v.tensor_scalar(prows[0:1, 1024 * pr:1024 * (pr + 1)],
                                      pout[0:1, 0:1024], scs[0:1, 0:1], None,
                                      op0=ALU.mult)
                ins.then_inc(P.h("dve"), 1)
                P.bump("dve")
            v.wait_ge(P.h("mm"), MM(i, 15))
            ins = v.tensor_scalar(d2k[:], pout[0:16, 0:512], sc16[:, 0:1], None,
                                  op0=ALU.mult)
            ins.then_inc(P.h("dve"), 1)
            P.bump("dve")
            for msub in range(4):
                pair, ms = msub // 2, msub % 2
                v.wait_ge(P.h("mm"), MM(i, 16 + 4 * pair + 3))
                v.wait_ge(P.h("ld"), LD(i, 5 + msub))
                if i >= 1 or msub >= 2:
                    pm = 4 * i + msub - 2
                    v.wait_ge(P.h("st"), st0 + 16 * (pm + 1))
                for nb in range(2):
                    ins = v.scalar_tensor_tensor(
                        os_[msub % 2][:, 512 * nb:512 * (nb + 1)],
                        pbs[2 * ms + nb][:], 0.5, xs[msub % 2][:, 512 * nb:512 * (nb + 1)],
                        op0=ALU.mult, op1=ALU.add)
                    ins.then_inc(P.h("dve"), 1)
                    P.bump("dve")
                if do_stats:
                    _emit_writer_stats(P, v, os_[msub % 2], 4 * i + msub)
    block.vector(vsec)

    def stores(g_):
        for i in range(NT):
            for msub in range(4):
                g_.wait_ge(P.h("dve"), DVE(i, 12 + 2 * msub + 1))
                g_.dma_start(out=resid_out[512 * i + 128 * msub:512 * i + 128 * (msub + 1), :],
                             in_=os_[msub % 2][:]).then_inc(P.h("st"), 16)
                P.bump("st", 16)
    block.gpsimd(stores)
    return P.cnt("st"), (w1t, w2b, b1c, b2r, scs, sc16, xs, ybf, yfm, hfm,
                         lrows, ocols1, spread, prows, d2k, e2f, pst, pbs, pout)


def emit_l1attn_proj(P, tag, resid_in, scr, stats_ready):
    nc, block = P.nc, P.block
    P.open_scope()
    wbufs = {}
    for nm in ["wq", "wk", "wv", "wg"]:
        wbufs[nm] = P.sb(tag + "_" + nm, [128, 8 * 2048], BF16)
    brow = P.sb(tag + "_br", [1, 4 * 2048], BF16)
    xs = [P.sb(tag + f"_x{i}", [128, 1024], F32) for i in range(2)]
    ybfs = [P.sb(tag + f"_y{i}", [128, 1024], BF16) for i in range(2)]
    yfms = [P.sb(tag + f"_yf{i}", [128, 1024], BF16) for i in range(2)]
    qtm = P.sb(tag + "_tmstage", [128, 2048], BF16)
    ktm = qtm
    qfm = P.sb(tag + "_qfm", [128, 2048], BF16)
    kfm = P.sb(tag + "_kfm", [128, 2048], BF16)
    vtm = P.sb(tag + "_vtm", [128, 2048], BF16)
    gtm = P.sb(tag + "_gtm", [128, 2048], BF16)

    pst = P.psum(tag + "_pt", [128, 1024], BF16)
    pstB = P.psum(tag + "_pB", [128, 2048], BF16)
    prod = P.psum(tag + "_pr", [128, 2048], F32)

    ONES = P.bufs["ones"]
    IDENT = P.bufs["ident"]

    def wloads(g):
        for i, nm in enumerate(["wq", "wk", "wv", "wg"]):
            buf = wbufs[nm]
            for k in range(8):
                g.dma_start(out=buf[:, 2048 * k:2048 * (k + 1)],
                            in_=scr[nm][128 * k:128 * (k + 1), :]).then_inc(P.h("gld"), 16)
                P.bump("gld", 16)
        for i, nm in enumerate(["bq", "bk", "bv", "bg"]):
            g.dma_start(out=brow[0:1, 2048 * i:2048 * (i + 1)],
                        in_=scr[nm]).then_inc(P.h("gld"), 16)
            P.bump("gld", 16)
    block.gpsimd(wloads)
    w_ready = P.cnt("gld")

    ld0 = P.cnt("ld")
    mm0 = P.cnt("mm")
    dve0 = P.cnt("dve")
    act0 = P.cnt("act")
    st0 = P.cnt("st")

    def MM(g, k):
        return mm0 + 7 * g + k

    def DVE(g, k):
        return dve0 + 7 * g + k

    def ST(g, k):
        return st0 + 16 * (5 * g + k)

    def loads(s):
        for g in range(NG):
            if g >= 2:
                s.wait_ge(P.h("dve"), DVE(g - 2, 1))
            s.dma_start(out=xs[g % 2][:],
                        in_=resid_in[128 * g:128 * (g + 1), :]).then_inc(P.h("ld"), 16)
            P.bump("ld", 16)
    block.sync(loads)

    def tsec(t):
        for g in range(NG):
            t.wait_ge(P.h("dve"), DVE(g, 1))
            if g == 0:
                t.wait_ge(P.h("gld"), w_ready)
            if g >= 1:
                t.wait_ge(P.h("dve"), DVE(g - 1, 2))
            for k in range(8):
                ins = t.transpose(pst[:, 128 * k:128 * (k + 1)],
                                  ybfs[g % 2][:, 128 * k:128 * (k + 1)], IDENT[:])
            ins.then_inc(P.h("mm"), 1)
            P.bump("mm")
            yf = yfms[g % 2]
            t.wait_ge(P.h("dve"), DVE(g, 2))
            if g >= 1:
                t.wait_ge(P.h("act"), act0 + g)
            for nm, bi, stepi in [("wq", 0, 3)]:
                for nb in range(4):
                    for k in range(8):
                        t.matmul(prod[:, 512 * nb:512 * (nb + 1)], yf[:, 128 * k:128 * (k + 1)],
                                 wbufs[nm][:, 2048 * k + 512 * nb:2048 * k + 512 * (nb + 1)],
                                 start=(k == 0), stop=False)
                    ins = t.matmul(prod[:, 512 * nb:512 * (nb + 1)], ONES[0:1, 0:128],
                                   brow[0:1, 2048 * bi + 512 * nb:2048 * bi + 512 * (nb + 1)],
                                   start=False, stop=True)
            ins.then_inc(P.h("mm"), 1)
            P.bump("mm")
            t.wait_ge(P.h("dve"), DVE(g, 3))
            if g >= 1:
                t.wait_ge(P.h("dve"), DVE(g - 1, 6))
            for k in range(16):
                ins = t.transpose(pstB[:, 128 * k:128 * (k + 1)],
                                  qtm[:, 128 * k:128 * (k + 1)], IDENT[:])
            ins.then_inc(P.h("mm"), 1)
            P.bump("mm")
            t.wait_ge(P.h("dve"), DVE(g, 3))
            for nb in range(4):
                for k in range(8):
                    t.matmul(prod[:, 512 * nb:512 * (nb + 1)], yf[:, 128 * k:128 * (k + 1)],
                             wbufs["wk"][:, 2048 * k + 512 * nb:2048 * k + 512 * (nb + 1)],
                             start=(k == 0), stop=False)
                ins = t.matmul(prod[:, 512 * nb:512 * (nb + 1)], ONES[0:1, 0:128],
                               brow[0:1, 2048 + 512 * nb:2048 + 512 * (nb + 1)],
                               start=False, stop=True)
            ins.then_inc(P.h("mm"), 1)
            P.bump("mm")
            t.wait_ge(P.h("dve"), DVE(g, 5))
            t.wait_ge(P.h("dve"), DVE(g, 4))
            for k in range(16):
                ins = t.transpose(pstB[:, 128 * k:128 * (k + 1)],
                                  ktm[:, 128 * k:128 * (k + 1)], IDENT[:])
            ins.then_inc(P.h("mm"), 1)
            P.bump("mm")
            t.wait_ge(P.h("dve"), DVE(g, 5))
            for nb in range(4):
                for k in range(8):
                    t.matmul(prod[:, 512 * nb:512 * (nb + 1)], yf[:, 128 * k:128 * (k + 1)],
                             wbufs["wv"][:, 2048 * k + 512 * nb:2048 * k + 512 * (nb + 1)],
                             start=(k == 0), stop=False)
                ins = t.matmul(prod[:, 512 * nb:512 * (nb + 1)], ONES[0:1, 0:128],
                               brow[0:1, 4096 + 512 * nb:4096 + 512 * (nb + 1)],
                               start=False, stop=True)
            ins.then_inc(P.h("mm"), 1)
            P.bump("mm")
            t.wait_ge(P.h("dve"), DVE(g, 7))
            for nb in range(4):
                for k in range(8):
                    t.matmul(prod[:, 512 * nb:512 * (nb + 1)], yf[:, 128 * k:128 * (k + 1)],
                             wbufs["wg"][:, 2048 * k + 512 * nb:2048 * k + 512 * (nb + 1)],
                             start=(k == 0), stop=False)
                ins = t.matmul(prod[:, 512 * nb:512 * (nb + 1)], ONES[0:1, 0:128],
                               brow[0:1, 6144 + 512 * nb:6144 + 512 * (nb + 1)],
                               start=False, stop=True)
            ins.then_inc(P.h("mm"), 1)
            P.bump("mm")
    block.tensor(tsec)

    def vsec(v):
        for g in range(NG):
            v.wait_ge(P.h("ld"), ld0 + 16 * (g + 1))
            if g == 0:
                v.wait_ge(P.h("dve"), stats_ready)
            if g >= 2:
                v.wait_ge(P.h("mm"), MM(g - 2, 1))
            ins = v.tensor_scalar(ybfs[g % 2][:], xs[g % 2][:], P.bufs["rst"][:, g:g + 1],
                                  P.bufs["nmr"][:, g:g + 1], op0=ALU.mult, op1=ALU.add)
            ins.then_inc(P.h("dve"), 1)
            P.bump("dve")
            v.wait_ge(P.h("mm"), MM(g, 1))
            if g >= 2:
                v.wait_ge(P.h("mm"), MM(g - 2, 7))
            ins = v.tensor_copy(yfms[g % 2][:], pst[:])
            ins.then_inc(P.h("dve"), 1)
            P.bump("dve")
            v.wait_ge(P.h("mm"), MM(g, 2))
            if g >= 1:
                v.wait_ge(P.h("mm"), MM(g - 1, 5))
                v.wait_ge(P.h("st"), ST(g - 1, 2))
            ins = v.tensor_copy(qtm[:], prod[:])
            ins.then_inc(P.h("dve"), 1)
            P.bump("dve")
            v.wait_ge(P.h("mm"), MM(g, 3))
            if g >= 1:
                v.wait_ge(P.h("st"), ST(g - 1, 1))
            ins = v.tensor_copy(qfm[:], pstB[:])
            ins.then_inc(P.h("dve"), 1)
            P.bump("dve")
            v.wait_ge(P.h("mm"), MM(g, 4))
            v.wait_ge(P.h("mm"), MM(g, 3))
            ins = v.tensor_copy(ktm[:], prod[:])
            ins.then_inc(P.h("dve"), 1)
            P.bump("dve")
            v.wait_ge(P.h("mm"), MM(g, 5))
            if g >= 1:
                v.wait_ge(P.h("st"), ST(g - 1, 3))
            ins = v.tensor_copy(kfm[:], pstB[:])
            ins.then_inc(P.h("dve"), 1)
            P.bump("dve")
            v.wait_ge(P.h("mm"), MM(g, 6))
            if g >= 1:
                v.wait_ge(P.h("st"), ST(g - 1, 4))
            ins = v.tensor_copy(vtm[:], prod[:])
            ins.then_inc(P.h("dve"), 1)
            P.bump("dve")
    block.vector(vsec)

    def asec(s):
        for g in range(NG):
            s.wait_ge(P.h("mm"), MM(g, 7))
            if g >= 1:
                s.wait_ge(P.h("st"), ST(g - 1, 5))
            for nb in range(4):
                ins = s.activation(gtm[:, 512 * nb:512 * (nb + 1)],
                                   prod[:, 512 * nb:512 * (nb + 1)], AF.Sigmoid)
            ins.then_inc(P.h("act"), 1)
            P.bump("act")
    block.scalar(asec)

    def stores(g_):
        for g in range(NG):
            g_.wait_ge(P.h("dve"), DVE(g, 4))
            g_.dma_start(out=scr["qd3"][g].rearrange("bk p x -> p bk x"),
                         in_=qfm[:].rearrange("p (bk x) -> p bk x", x=128)).then_inc(P.h("st"), 16)
            P.bump("st", 16)
            g_.wait_ge(P.h("dve"), DVE(g, 5))
            g_.dma_start(out=scr["ktd"][128 * g:128 * (g + 1), :],
                         in_=ktm[:]).then_inc(P.h("st"), 16)
            P.bump("st", 16)
            g_.wait_ge(P.h("dve"), DVE(g, 6))
            g_.dma_start(out=scr["kd3"][g].rearrange("bk p x -> p bk x"),
                         in_=kfm[:].rearrange("p (bk x) -> p bk x", x=128)).then_inc(P.h("st"), 16)
            P.bump("st", 16)
            g_.wait_ge(P.h("dve"), DVE(g, 7))
            g_.dma_start(out=scr["vd"][128 * g:128 * (g + 1), :],
                         in_=vtm[:]).then_inc(P.h("st"), 16)
            P.bump("st", 16)
            g_.wait_ge(P.h("act"), act0 + g + 1)
            g_.dma_start(out=scr["gd"][128 * g:128 * (g + 1), :],
                         in_=gtm[:]).then_inc(P.h("st"), 16)
            P.bump("st", 16)
    block.gpsimd(stores)
    st_done = P.cnt("st")
    P.close_scope()
    emit_phase_barrier(P)
    return st_done


def emit_l1attn_pass(P, tag, half, pass_a, resid_in, resid_out, scr, rings, nch=NG, do_stats=False):
    nc, block = P.nc, P.block
    G = 4
    P.open_scope()
    mem = P.sb(tag + "_mem", [128, 16 * 1024], F32)
    membf = P.sb(tag + "_mbf", [128, 16 * 1024], BF16)
    wob = P.sb(tag + "_wo", [128, 8 * 1024], BF16)
    borow = P.sb(tag + "_bo", [1, 1024], BF16)
    qs = [P.sb(tag + f"_q{i}", [128, 2048], BF16) for i in range(2)]
    kts = [P.sb(tag + f"_kt{i}", [128, 2048], BF16) for i in range(4)]
    kfs = [P.sb(tag + f"_kf{i}", [128, 2048], BF16) for i in range(4)] if pass_a else None
    vs = [P.sb(tag + f"_v{i}", [128, 1024], BF16) for i in range(4)]
    gs = [P.sb(tag + f"_g{i}", [128, 1024], BF16) for i in range(2)]
    smx = P.sb(tag + "_smx", [128, 128], F32)
    rmax = P.sb(tag + "_rmx", [128, 4], F32)
    pex = P.sb(tag + "_pex", [128, 128], F32)
    abf = P.sb(tag + "_abf", [128, 128], BF16)
    otm = P.sb(tag + "_otm", [128, 1024], BF16)
    ofm = P.sb(tag + "_ofm", [128, 1024], BF16)
    os_ = [P.bufs["swp0"], P.bufs["swp1"]]
    xs = os_

    pout = P.psum(tag + "_po", [128, 1024], F32)
    pus = [P.psum(tag + f"_pu{i}", [128, 1024], F32) for i in range(2)]
    psc = P.psum(tag + "_ps", [128, 512], F32)
    pstT = P.psum(tag + "_pT", [128, 1024], BF16)

    ONES = P.bufs["ones"]
    IDENT = P.bufs["ident"]
    atr = rings["at"]
    a2r = rings["a2"]

    def a2slot(c, k):
        j = c % 4
        base = [0, 0, 1, 3][j]
        return 6 * (c // 4) + base + (k - 1)

    def wloads(g):
        for k in range(8):
            g.dma_start(out=wob[:, 1024 * k:1024 * (k + 1)],
                        in_=scr["wo"][1024 * half + 128 * k:1024 * half + 128 * (k + 1), :]
                        ).then_inc(P.h("gld"), 16)
            P.bump("gld", 16)
        g.dma_start(out=borow[:], in_=scr["bo"]).then_inc(P.h("gld"), 16)
        P.bump("gld", 16)
    block.gpsimd(wloads)
    w_ready = P.cnt("gld")

    def vpre(v):
        v.memset(mem[:], 0.0)
        ins = v.memset(membf[:], 0.0)
        ins.then_inc(P.h("dve"), 1)
        P.bump("dve")
    block.vector(vpre)
    mem_init = P.cnt("dve")

    nld_c = 6 if pass_a else 5

    def mm_order(c):
        j = c % 4
        if pass_a:
            o = ["sc"] + [f"a2_{k}" for k in range(1, j + 1)] + ["at", "og", "tr", "fin"]
        else:
            o = ["og", "tr", "fin"]
        if j == 3:
            o += [f"u{m}" for m in range(16)]
        return o

    def dve_order(c):
        j = c % 4
        if pass_a:
            o = (["s1", "s2", "s3", "s4", "s5"]
                 + [f"s6_{k}" for k in range(1, j + 1)] + ["s7", "s8", "s9"])
        else:
            o = ["s7", "s8", "s9"]
        if j == 3:
            o += [f"u{m}" for m in range(16)]
        return o

    nmm_c = [len(mm_order(c)) for c in range(nch)]
    ndve_c = [len(dve_order(c)) for c in range(nch)]
    nact_c = [1 if pass_a else 0] * nch
    npool_c = [16 if c % 4 == 3 else 0 for c in range(nch)]

    ld0 = P.cnt("ld")
    mm0 = P.cnt("mm")
    dve0 = P.cnt("dve")
    act0 = P.cnt("act")
    pool0 = P.cnt("pool")
    st0 = P.cnt("st")
    mmb = [mm0] * nch
    dveb = [dve0] * nch
    actb = [act0] * nch
    poolb = [pool0] * nch
    for c in range(1, nch):
        mmb[c] = mmb[c - 1] + nmm_c[c - 1]
        dveb[c] = dveb[c - 1] + ndve_c[c - 1]
        actb[c] = actb[c - 1] + nact_c[c - 1]
        poolb[c] = poolb[c - 1] + npool_c[c - 1]

    def mm_slot(c, nm):
        return mmb[c] + mm_order(c).index(nm) + 1

    def dve_slot(c, nm):
        return dveb[c] + dve_order(c).index(nm) + 1

    def ue_chunk(c):
        return 4 * (c // 4) + 3

    sld0 = P.cnt("sld")

    def loads(s):
        for c in range(nch):
            if c >= 2:
                s.wait_ge(P.h("st"), st0 + 16 * (c - 1))
                if do_stats:
                    s.wait_ge(P.h("sld"), sld0 + (c - 1))
            s.dma_start(out=xs[c % 2][:],
                        in_=resid_in[128 * c:128 * (c + 1), :]).then_inc(P.h("ld"), 16)
            P.bump("ld", 16)
            if c >= 2:
                s.wait_ge(P.h("mm"), mm_slot(c - 2, "og"))
            s.dma_start(out=qs[c % 2][:].rearrange("p (bk x) -> p bk x", x=128),
                        in_=scr["qd3"][c].rearrange("bk p x -> p bk x")).then_inc(P.h("ld"), 16)
            P.bump("ld", 16)
            if c >= 4:
                s.wait_ge(P.h("mm"), mm_slot(ue_chunk(c - 4), "u15"))
            s.dma_start(out=kts[c % 4][:],
                        in_=scr["ktd"][128 * c:128 * (c + 1), :]).then_inc(P.h("ld"), 16)
            P.bump("ld", 16)
            if c >= 4:
                s.wait_ge(P.h("mm"), mm_slot(ue_chunk(c - 4), "u15"))
            s.dma_start(out=vs[c % 4][:],
                        in_=scr["vd"][128 * c:128 * (c + 1),
                                      1024 * half:1024 * (half + 1)]).then_inc(P.h("ld"), 16)
            P.bump("ld", 16)
            if c >= 2:
                s.wait_ge(P.h("dve"), dve_slot(c - 2, "s7"))
            s.dma_start(out=gs[c % 2][:],
                        in_=scr["gd"][128 * c:128 * (c + 1),
                                      1024 * half:1024 * (half + 1)]).then_inc(P.h("ld"), 16)
            P.bump("ld", 16)
            if pass_a:
                if c >= 4:
                    s.wait_ge(P.h("mm"), mm_slot(c - 1, "og"))
                s.dma_start(out=kfs[c % 4][:].rearrange("p (bk x) -> p bk x", x=128),
                            in_=scr["kd3"][c].rearrange("bk p x -> p bk x")).then_inc(P.h("ld"), 16)
                P.bump("ld", 16)
    block.sync(loads)

    def LD(c, k):
        return ld0 + 16 * (nld_c * c + k)

    SCL = 1.0 / float(np.sqrt(2048.0))

    def tsec(t):
        for c in range(nch):
            j = c % 4
            if pass_a:
                t.wait_ge(P.h("ld"), LD(c, 6))
                t.wait_ge(P.h("ld"), LD(c, 2))
                if c == 0:
                    t.wait_ge(P.h("gld"), w_ready)
                if c >= 1:
                    jp = (c - 1) % 4
                    last = f"s6_{jp}" if jp >= 1 else "s1"
                    t.wait_ge(P.h("dve"), dve_slot(c - 1, last))
                for bk in range(16):
                    ins = t.matmul(psc[:, 0:128], qs[c % 2][:, 128 * bk:128 * (bk + 1)],
                                   kfs[c % 4][:, 128 * bk:128 * (bk + 1)],
                                   start=(bk == 0), stop=(bk == 15))
                ins.then_inc(P.h("mm"), 1)
                P.bump("mm")
                for k in range(1, j + 1):
                    t.wait_ge(P.h("dve"), dve_slot(c, "s1") if k == 1 else 0) if False else None
                    if k == 1:
                        t.wait_ge(P.h("dve"), dve_slot(c, "s1"))
                    for bk in range(16):
                        ins = t.matmul(psc[:, 128 * k:128 * (k + 1)],
                                       kfs[(c - k) % 4][:, 128 * bk:128 * (bk + 1)],
                                       qs[c % 2][:, 128 * bk:128 * (bk + 1)],
                                       start=(bk == 0), stop=(bk == 15))
                    ins.then_inc(P.h("mm"), 1)
                    P.bump("mm")
                t.wait_ge(P.h("dve"), dve_slot(c, "s4"))
                if c >= 1:
                    t.wait_ge(P.h("dve"), dve_slot(c - 1, "s8"))
                ins = t.transpose(pstT[:, 0:128], abf[:], IDENT[:])
                ins.then_inc(P.h("mm"), 1)
                P.bump("mm")
            t.wait_ge(P.h("ld"), LD(c, 4))
            if not pass_a:
                if c == 0:
                    t.wait_ge(P.h("gld"), w_ready)
                t.wait_ge(P.h("ld"), LD(c, 2))
            if c == 0:
                t.wait_ge(P.h("dve"), mem_init)
            if c >= 1:
                t.wait_ge(P.h("dve"), dve_slot(c - 1, "s9"))
            if c >= 4:
                t.wait_ge(P.h("pool"), pool0 + 16 * (c // 4))
            for nb in range(2):
                for bk in range(16):
                    t.matmul(pout[:, 512 * nb:512 * (nb + 1)],
                             qs[c % 2][:, 128 * bk:128 * (bk + 1)],
                             membf[:, 1024 * bk + 512 * nb:1024 * bk + 512 * (nb + 1)],
                             start=(bk == 0), stop=False)
            if pass_a:
                t.wait_ge(P.h("dve"), dve_slot(c, "s5"))
            for nb in range(2):
                ins = t.matmul(pout[:, 512 * nb:512 * (nb + 1)], atr[:, 128 * c:128 * (c + 1)],
                               vs[c % 4][:, 512 * nb:512 * (nb + 1)], start=False,
                               stop=(j == 0))
            for k in range(1, j + 1):
                if pass_a:
                    t.wait_ge(P.h("dve"), dve_slot(c, f"s6_{k}"))
                for nb in range(2):
                    ins = t.matmul(pout[:, 512 * nb:512 * (nb + 1)],
                                   a2r[:, 128 * a2slot(c, k):128 * (a2slot(c, k) + 1)],
                                   vs[(c - k) % 4][:, 512 * nb:512 * (nb + 1)],
                                   start=False, stop=(k == j))
            ins.then_inc(P.h("mm"), 1)
            P.bump("mm")
            t.wait_ge(P.h("dve"), dve_slot(c, "s7"))
            if c >= 1:
                t.wait_ge(P.h("dve"), dve_slot(c - 1, "s8"))
            for k in range(8):
                ins = t.transpose(pstT[:, 128 * k:128 * (k + 1)],
                                  otm[:, 128 * k:128 * (k + 1)], IDENT[:])
            ins.then_inc(P.h("mm"), 1)
            P.bump("mm")
            t.wait_ge(P.h("dve"), dve_slot(c, "s8"))
            for nb in range(2):
                for k in range(8):
                    ins = t.matmul(pout[:, 512 * nb:512 * (nb + 1)], ofm[:, 128 * k:128 * (k + 1)],
                                   wob[:, 1024 * k + 512 * nb:1024 * k + 512 * (nb + 1)],
                                   start=(k == 0), stop=(not pass_a and k == 7))
                if pass_a:
                    ins = t.matmul(pout[:, 512 * nb:512 * (nb + 1)], ONES[0:1, 0:128],
                                   borow[0:1, 512 * nb:512 * (nb + 1)],
                                   start=False, stop=True)
            ins.then_inc(P.h("mm"), 1)
            P.bump("mm")
            if j == 3:
                t.wait_ge(P.h("ld"), LD(c, 3))
                for m in range(16):
                    pu = pus[m % 2]
                    if m >= 2:
                        t.wait_ge(P.h("dve"), dve_slot(c, f"u{m - 2}"))
                    for cc in range(c - 3, c + 1):
                        for nb in range(2):
                            ins = t.matmul(pu[:, 512 * nb:512 * (nb + 1)],
                                           kts[cc % 4][:, 128 * m:128 * (m + 1)],
                                           vs[cc % 4][:, 512 * nb:512 * (nb + 1)],
                                           start=(cc == c - 3), stop=(cc == c))
                    ins.then_inc(P.h("mm"), 1)
                    P.bump("mm")
    block.tensor(tsec)

    def asec(s):
        for c in range(nch):
            j = c % 4
            if pass_a:
                s.wait_ge(P.h("dve"), dve_slot(c, "s2"))
                ins = s.activation(pex[:], smx[:], AF.Exp, bias=rmax[:, 0:1],
                                   accum_out=rmax[:, 1:2])
                ins.then_inc(P.h("act"), 1)
                P.bump("act")
            if j == 3:
                for m in range(16):
                    if m % 4 == 0:
                        continue
                    s.wait_ge(P.h("dve"), dve_slot(c, f"u{m}"))
                    s.wait_ge(P.h("mm"), mm_slot(c, "og"))
                    ins = s.copy(membf[:, 1024 * m:1024 * (m + 1)],
                                 mem[:, 1024 * m:1024 * (m + 1)])
                    ins.then_inc(P.h("pool"), 1)
                    P.bump("pool")
    block.scalar(asec)

    def vsec(v):
        for c in range(nch):
            j = c % 4
            if pass_a:
                v.wait_ge(P.h("mm"), mm_slot(c, "sc"))
                if c >= 1:
                    v.wait_ge(P.h("act"), actb[c - 1] + 1)
                ins = v.scalar_tensor_tensor(smx[:], psc[:, 0:128], SCL, P.bufs["cmask"][:],
                                             op0=ALU.mult, op1=ALU.add)
                ins.then_inc(P.h("dve"), 1)
                P.bump("dve")
                ins = v.reduce_max(rmax[:, 0:1], smx[:], axis=AX.X, negate=True)
                v.drain()
                ins.then_inc(P.h("dve"), 1)
                P.bump("dve")
                v.wait_ge(P.h("act"), actb[c] + 1)
                ins = v.reciprocal(rmax[:, 2:3], rmax[:, 1:2])
                v.drain()
                ins.then_inc(P.h("dve"), 1)
                P.bump("dve")
                if c >= 1:
                    v.wait_ge(P.h("mm"), mm_slot(c - 1, "at"))
                ins = v.tensor_scalar(abf[:], pex[:], rmax[:, 2:3], None, op0=ALU.mult)
                ins.then_inc(P.h("dve"), 1)
                P.bump("dve")
                v.wait_ge(P.h("mm"), mm_slot(c, "at"))
                ins = v.tensor_copy(atr[:, 128 * c:128 * (c + 1)], pstT[:, 0:128])
                ins.then_inc(P.h("dve"), 1)
                P.bump("dve")
                for k in range(1, j + 1):
                    v.wait_ge(P.h("mm"), mm_slot(c, f"a2_{k}"))
                    ins = v.tensor_copy(a2r[:, 128 * a2slot(c, k):128 * (a2slot(c, k) + 1)],
                                        psc[:, 128 * k:128 * (k + 1)])
                    ins.then_inc(P.h("dve"), 1)
                    P.bump("dve")
            v.wait_ge(P.h("mm"), mm_slot(c, "og"))
            v.wait_ge(P.h("ld"), LD(c, 5))
            if c >= 1:
                v.wait_ge(P.h("mm"), mm_slot(c - 1, "tr"))
            ins = v.tensor_tensor(otm[:], pout[:], gs[c % 2][:], op=ALU.mult)
            ins.then_inc(P.h("dve"), 1)
            P.bump("dve")
            v.wait_ge(P.h("mm"), mm_slot(c, "tr"))
            if c >= 1:
                v.wait_ge(P.h("mm"), mm_slot(c - 1, "fin"))
            ins = v.tensor_copy(ofm[:], pstT[:])
            ins.then_inc(P.h("dve"), 1)
            P.bump("dve")
            v.wait_ge(P.h("mm"), mm_slot(c, "fin"))
            v.wait_ge(P.h("ld"), LD(c, 1))
            if c >= 2:
                v.wait_ge(P.h("st"), st0 + 16 * (c - 1))
            ins = v.tensor_tensor(os_[c % 2][:], xs[c % 2][:], pout[:], op=ALU.add)
            ins.then_inc(P.h("dve"), 1)
            P.bump("dve")
            if (not pass_a) and do_stats:
                _emit_writer_stats(P, v, os_[c % 2], c)
            if j == 3:
                for m in range(16):
                    v.wait_ge(P.h("mm"), mm_slot(c, f"u{m}"))
                    if c >= 4:
                        v.wait_ge(P.h("pool"), pool0 + 16 * (c // 4))
                    ins = v.tensor_tensor(mem[:, 1024 * m:1024 * (m + 1)],
                                          mem[:, 1024 * m:1024 * (m + 1)],
                                          pus[m % 2][:], op=ALU.add)
                    ins.then_inc(P.h("dve"), 1)
                    P.bump("dve")
    block.vector(vsec)

    def psec(g):
        for c in range(nch):
            g.wait_ge(P.h("dve"), dve_slot(c, "s9"))
            g.dma_start(out=resid_out[128 * c:128 * (c + 1), :],
                        in_=os_[c % 2][:]).then_inc(P.h("st"), 16)
            P.bump("st", 16)
            if c % 4 == 3:
                for m in range(16):
                    if m % 4 != 0:
                        continue
                    g.wait_ge(P.h("dve"), dve_slot(c, f"u{m}"))
                    g.wait_ge(P.h("mm"), mm_slot(c, "og"))
                    ins = g.tensor_copy(membf[:, 1024 * m:1024 * (m + 1)],
                                        mem[:, 1024 * m:1024 * (m + 1)])
                    ins.then_inc(P.h("pool"), 1)
                    P.bump("pool")
    block.gpsimd(psec)
    st_done = P.cnt("st")
    P.close_scope()
    emit_phase_barrier(P)
    return st_done

def prep_l0ffn_weights(W_in, b_in, W_out):
    perm = tree_perm()
    winT = np.zeros((1024, 2048), np.float32)
    winT[:, 0:2040] = np.asarray(W_in)[perm].T
    b = np.zeros((1, 2048), np.float32)
    b[0, 0:2040] = np.asarray(b_in)[perm]
    woutT = np.zeros((2048, 1024), np.float32)
    woutT[0:2040, :] = np.asarray(W_out)[:, perm].T
    return {"winT": winT, "b_in": b, "woutT": woutT}


def build_p_l0ffn():
    P = Prog()
    nc = P.nc
    xin = nc.dram_tensor("xin", [SEQ, DM], F32, kind="ExternalInput").ap()
    xout = nc.dram_tensor("xout", [SEQ, DM], F32, kind="ExternalOutput").ap()
    w = {"winT": nc.dram_tensor("winT", [1024, 2048], F32, kind="ExternalInput").ap(),
         "b_in": nc.dram_tensor("b_in", [1, 2048], F32, kind="ExternalInput").ap(),
         "woutT": nc.dram_tensor("woutT", [2048, 1024], F32, kind="ExternalInput").ap()}
    consts = {k: nc.dram_tensor(k, list(v.shape),
                                BF16 if v.dtype == BF16NP else F32,
                                kind="ExternalInput").ap()
              for k, v in host_consts().items()}
    alloc_common(P)
    P.block = FakeBlock(P) if getattr(Prog, 'use_fake', False) else P.ctx.enter_context(nc.Block())
    pre_dve, _ = emit_preamble(P, consts, want_cvecs=[1e-5, -0.5])
    stats_ready = emit_stats_sweep(P, xin, pre_dve)
    st_done = emit_l0ffn(P, "f0", xin, xout, w, stats_ready)

    def fin(g):
        g.wait_ge(P.h("st"), st_done)
    P.block.gpsimd(fin)
    P.finish()
    return P


def build_p_l0attn():
    P = Prog()
    nc = P.nc
    xin = nc.dram_tensor("xin", [SEQ, DM], F32, kind="ExternalInput").ap()
    xout = nc.dram_tensor("xout", [SEQ, DM], F32, kind="ExternalOutput").ap()
    w = {nm: nc.dram_tensor(nm, [1024, 1024], F32, kind="ExternalInput").ap()
         for nm in ["wq", "wk", "wv", "wo"]}
    for nm in ["bq", "bk", "bv", "bo"]:
        w[nm] = nc.dram_tensor(nm, [1, 1024], F32, kind="ExternalInput").ap()
    dp = nc.dram_tensor("decay", [1, 1], F32, kind="ExternalInput").ap()
    consts = {k: nc.dram_tensor(k, list(v.shape),
                                BF16 if v.dtype == BF16NP else F32,
                                kind="ExternalInput").ap()
              for k, v in host_consts().items()}
    alloc_common(P)
    P.block = FakeBlock(P) if getattr(Prog, 'use_fake', False) else P.ctx.enter_context(nc.Block())
    pre_dve, _ = emit_preamble(P, consts, want_cvecs=[1e-5, -0.5, 128.0])
    dec_ready, dv = emit_decay_prep(P, "dc", dp, pre_dve, consts)
    stats_ready = emit_stats_sweep(P, xin, pre_dve)
    st_done = emit_l0attn(P, "a0", xin, xout, w, stats_ready, dec_ready, dv)

    def fin(g):
        g.wait_ge(P.h("st"), st_done)
    P.block.gpsimd(fin)
    P.finish()
    return P


def prep_l1ffn_weights(W1, b1, W2, b2, scale):
    return {"w1T": np.ascontiguousarray(np.asarray(W1).T),
            "w2T": np.ascontiguousarray(np.asarray(W2).T),
            "b1": np.ascontiguousarray(np.asarray(b1).reshape(32, 128).T),
            "b2": np.asarray(b2).reshape(1, 1024),
            "scale": np.asarray(scale, np.float32).reshape(1, 1)}


def build_p_l1ffn():
    P = Prog()
    nc = P.nc
    xin = nc.dram_tensor("xin", [SEQ, DM], F32, kind="ExternalInput").ap()
    xout = nc.dram_tensor("xout", [SEQ, DM], F32, kind="ExternalOutput").ap()
    w = {"w1T": nc.dram_tensor("w1T", [1024, 4096], F32, kind="ExternalInput").ap(),
         "w2T": nc.dram_tensor("w2T", [4096, 1024], F32, kind="ExternalInput").ap(),
         "w1bf": nc.dram_tensor("w1bf", [1024, 4096], BF16, kind="Internal").ap(),
         "w2bf": nc.dram_tensor("w2bf", [4096, 1024], BF16, kind="Internal").ap(),
         "b1": nc.dram_tensor("b1", [128, 32], F32, kind="ExternalInput").ap(),
         "b2": nc.dram_tensor("b2", [1, 1024], F32, kind="ExternalInput").ap(),
         "scale": nc.dram_tensor("scale", [1, 1], F32, kind="ExternalInput").ap(),
         "e2x": nc.dram_tensor("e2x", [1024, 128], BF16, kind="Internal").ap()}
    consts = {k: nc.dram_tensor(k, list(v.shape),
                                BF16 if v.dtype == BF16NP else F32,
                                kind="ExternalInput").ap()
              for k, v in host_consts().items()}
    alloc_common(P)
    P.block = FakeBlock(P) if getattr(Prog, 'use_fake', False) else P.ctx.enter_context(nc.Block())
    pre_dve, _ = emit_preamble(P, consts, want_cvecs=[1e-5, -0.5])
    emit_stage_bf16(P, "stg", [(w["w1T"], w["w1bf"], 1024, 4096),
                               (w["w2T"], w["w2bf"], 4096, 1024)])
    stats_ready = emit_stats_sweep(P, xin, pre_dve)
    st_done, _ = emit_l1ffn(P, "f1", xin, xout, w, stats_ready, do_stats=False)

    def fin(g):
        g.wait_ge(P.h("st"), st_done)
    P.block.gpsimd(fin)
    P.finish()
    return P


def build_p_l1attn():
    P = Prog()
    nc = P.nc
    xin = nc.dram_tensor("xin", [SEQ, DM], F32, kind="ExternalInput").ap()
    xout = nc.dram_tensor("xout", [SEQ, DM], F32, kind="ExternalOutput").ap()
    scr = {}
    for nm in ["wq", "wk", "wv", "wg"]:
        scr[nm] = nc.dram_tensor(nm, [1024, 2048], F32, kind="ExternalInput").ap()
    for nm in ["bq", "bk", "bv", "bg"]:
        scr[nm] = nc.dram_tensor(nm, [1, 2048], F32, kind="ExternalInput").ap()
    scr["wo"] = nc.dram_tensor("wo", [2048, 1024], F32, kind="ExternalInput").ap()
    scr["bo"] = nc.dram_tensor("bo", [1, 1024], F32, kind="ExternalInput").ap()
    for nm in ["qd3", "kd3"]:
        scr[nm] = nc.dram_tensor(nm, [32, 16, 128, 128], BF16, kind="Internal").ap()
    for nm in ["ktd", "vd", "gd"]:
        scr[nm] = nc.dram_tensor(nm, [4096, 2048], BF16, kind="Internal").ap()
    xmid = nc.dram_tensor("xmid", [SEQ, DM], F32, kind="Internal").ap()
    consts = {k: nc.dram_tensor(k, list(v.shape),
                                BF16 if v.dtype == BF16NP else F32,
                                kind="ExternalInput").ap()
              for k, v in host_consts().items()}
    alloc_common(P)
    P.block = FakeBlock(P) if getattr(Prog, 'use_fake', False) else P.ctx.enter_context(nc.Block())
    pre_dve, _ = emit_preamble(P, consts, want_cvecs=[1e-5, -0.5])
    stats_ready = emit_stats_sweep(P, xin, pre_dve)
    emit_l1attn_proj(P, "pa", xin, scr, stats_ready)
    rings = {"at": P.sb("r_at", [128, 32 * 128], BF16),
             "a2": P.sb("r_a2", [128, 48 * 128], BF16)}
    emit_l1attn_pass(P, "pb", 0, True, xin, xmid, scr, rings)
    st_done = emit_l1attn_pass(P, "pc", 1, False, xmid, xout, scr, rings)

    def fin(g):
        g.wait_ge(P.h("st"), st_done)
    P.block.gpsimd(fin)
    P.finish()
    return P


def _run_single(P, in_map, n_cores=1):
    res = run_bass_kernel_spmd(P.nc, [in_map] * n_cores, core_ids=list(range(n_cores)))
    return res


def build_mega():
    P = Prog()
    nc = P.nc
    xin = nc.dram_tensor("xin", [SEQ, DM], F32, kind="ExternalInput").ap()
    out = nc.dram_tensor("out", [SEQ, DM], F32, kind="ExternalOutput").ap()
    rA = nc.dram_tensor("rA", [SEQ, DM], F32, kind="Internal").ap()
    rB = nc.dram_tensor("rB", [SEQ, DM], F32, kind="Internal").ap()

    w_f0 = {"winT": nc.dram_tensor("d_f0_winT", [1024, 2048], F32, kind="ExternalInput").ap(),
            "b_in": nc.dram_tensor("d_f0_bin", [1, 2048], F32, kind="ExternalInput").ap(),
            "woutT": nc.dram_tensor("d_f0_woutT", [2048, 1024], F32, kind="ExternalInput").ap()}
    w_a0 = {}
    for nm in ["wq", "wk", "wv", "wo"]:
        w_a0[nm] = nc.dram_tensor("d_a0_" + nm, [1024, 1024], F32, kind="ExternalInput").ap()
    for nm in ["bq", "bk", "bv", "bo"]:
        w_a0[nm] = nc.dram_tensor("d_a0_" + nm, [1, 1024], F32, kind="ExternalInput").ap()
    decay_ap = nc.dram_tensor("d_a0_decay", [1, 1], F32, kind="ExternalInput").ap()
    w_f1 = {"w1T": nc.dram_tensor("d_f1_w1T", [1024, 4096], F32, kind="ExternalInput").ap(),
            "w2T": nc.dram_tensor("d_f1_w2T", [4096, 1024], F32, kind="ExternalInput").ap(),
            "w1bf": nc.dram_tensor("d_f1_w1bf", [1024, 4096], BF16, kind="Internal").ap(),
            "w2bf": nc.dram_tensor("d_f1_w2bf", [4096, 1024], BF16, kind="Internal").ap(),
            "b1": nc.dram_tensor("d_f1_b1", [128, 32], F32, kind="ExternalInput").ap(),
            "b2": nc.dram_tensor("d_f1_b2", [1, 1024], F32, kind="ExternalInput").ap(),
            "scale": nc.dram_tensor("d_f1_scale", [1, 1], F32, kind="ExternalInput").ap(),
            "e2x": nc.dram_tensor("d_f1_e2x", [1024, 128], BF16, kind="Internal").ap()}
    w_a1 = {}
    for nm in ["wq", "wk", "wv", "wg"]:
        w_a1[nm] = nc.dram_tensor("d_a1_" + nm, [1024, 2048], F32, kind="ExternalInput").ap()
    for nm in ["bq", "bk", "bv", "bg"]:
        w_a1[nm] = nc.dram_tensor("d_a1_" + nm, [1, 2048], F32, kind="ExternalInput").ap()
    w_a1["wo"] = nc.dram_tensor("d_a1_wo", [2048, 1024], F32, kind="ExternalInput").ap()
    w_a1["bo"] = nc.dram_tensor("d_a1_bo", [1, 1024], F32, kind="ExternalInput").ap()
    for nm in ["qd3", "kd3"]:
        w_a1[nm] = nc.dram_tensor("d_a1_" + nm, [32, 16, 128, 128], BF16, kind="Internal").ap()
    for nm in ["ktd", "vd", "gd"]:
        w_a1[nm] = nc.dram_tensor("d_a1_" + nm, [4096, 2048], BF16, kind="Internal").ap()

    consts = {k: nc.dram_tensor(k, list(v.shape),
                                BF16 if v.dtype == BF16NP else F32,
                                kind="ExternalInput").ap()
              for k, v in host_consts().items()}
    alloc_common(P)
    P.block = FakeBlock(P) if getattr(Prog, 'use_fake', False) else P.ctx.enter_context(nc.Block())
    pre_dve, _ = emit_preamble(P, consts, want_cvecs=[1e-5, -0.5, 128.0])

    sw = emit_stats_sweep(P, xin, pre_dve)
    P.open_scope()
    emit_stage_bf16(P, "stg", [(w_f1["w1T"], w_f1["w1bf"], 1024, 4096),
                               (w_f1["w2T"], w_f1["w2bf"], 4096, 1024)])
    emit_l0ffn(P, "f0a", xin, rA, w_f0, sw)
    P.close_scope()
    emit_phase_barrier(P)
    sw = emit_stats_finish(P, P.cnt("sld"))
    P.open_scope()
    dec_ready, dv = emit_decay_prep(P, "dc", decay_ap, pre_dve, consts)
    emit_l0attn(P, "a0", rA, rB, w_a0, sw, dec_ready, dv)
    P.close_scope()
    emit_phase_barrier(P)
    sw = emit_stats_finish(P, P.cnt("sld"))
    P.open_scope()
    emit_l0ffn(P, "f0b", rB, rA, w_f0, sw)
    P.close_scope()
    emit_phase_barrier(P)
    sw = emit_stats_finish(P, P.cnt("sld"))
    P.open_scope()
    emit_l1ffn(P, "f1a", rA, rB, w_f1, sw)
    P.close_scope()
    emit_phase_barrier(P)
    sw = emit_stats_finish(P, P.cnt("sld"))
    emit_l1attn_proj(P, "a1p", rB, w_a1, sw)
    _rc1 = nc.sbuf_tensor("r_at", [128, 32 * 128], BF16)
    _rc2 = nc.sbuf_tensor("r_a2", [128, 48 * 128], BF16)
    rings = {"at": _rc1.__enter__(), "a2": _rc2.__enter__()}
    emit_l1attn_pass(P, "a1x", 0, True, rB, rA, w_a1, rings)
    emit_l1attn_pass(P, "a1y", 1, False, rA, rB, w_a1, rings, do_stats=True)
    _rc2.__exit__(None, None, None)
    _rc1.__exit__(None, None, None)
    sw = emit_stats_finish(P, P.cnt("sld"))
    P.open_scope()
    _, wre = emit_l1ffn(P, "f1b", rB, rA, w_f1, sw)
    emit_phase_barrier(P)
    sw = emit_stats_finish(P, P.cnt("sld"))
    st_done, _ = emit_l1ffn(P, "f1c", rA, out, w_f1, sw, do_stats=False, reuse=wre)
    P.close_scope()

    def fin(g):
        g.wait_ge(P.h("st"), st_done)
    P.block.gpsimd(fin)
    P.finish()
    return P


_PROG_CACHE = {}


def _get_prog(name):
    if name not in _PROG_CACHE:
        _PROG_CACHE[name] = {"l0ffn": build_p_l0ffn, "l0attn": build_p_l0attn,
                             "l1ffn": build_p_l1ffn, "l1attn": build_p_l1attn,
                             "mega": build_mega}[name]()
    return _PROG_CACHE[name]


def _prep_host_weights(inputs):
    w_f0 = prep_l0ffn_weights(inputs["L0_ffn_W_in"], inputs["L0_ffn_b_in"],
                              inputs["L0_ffn_W_out"])
    w_f1 = prep_l1ffn_weights(inputs["L1_ffn_W1"], inputs["L1_ffn_b1"],
                              inputs["L1_ffn_W2"], inputs["L1_ffn_b2"],
                              inputs["L1_ffn_scale"])
    m = {}
    m["d_f0_winT"] = w_f0["winT"]
    m["d_f0_bin"] = w_f0["b_in"]
    m["d_f0_woutT"] = w_f0["woutT"]
    for nm, key in [("wq", "Wq"), ("wk", "Wk"), ("wv", "Wv"), ("wo", "Wo")]:
        m["d_a0_" + nm] = np.asarray(inputs["L0_attn_" + key])
    for nm, key in [("bq", "bq"), ("bk", "bk"), ("bv", "bv"), ("bo", "bo")]:
        m["d_a0_" + nm] = np.asarray(inputs["L0_attn_" + key]).reshape(1, 1024)
    m["d_a0_decay"] = np.asarray(inputs["L0_attn_decay_param"], np.float32).reshape(1, 1)
    m["d_f1_w1T"] = w_f1["w1T"]
    m["d_f1_w2T"] = w_f1["w2T"]
    m["d_f1_b1"] = w_f1["b1"]
    m["d_f1_b2"] = w_f1["b2"]
    m["d_f1_scale"] = w_f1["scale"]
    for nm, key in [("wq", "Wq"), ("wk", "Wk"), ("wv", "Wv"), ("wg", "Wg")]:
        m["d_a1_" + nm] = np.asarray(inputs["L1_attn_" + key])
    for nm, key in [("bq", "bq"), ("bk", "bk"), ("bv", "bv"), ("bg", "bg")]:
        m["d_a1_" + nm] = np.asarray(inputs["L1_attn_" + key]).reshape(1, 2048)
    m["d_a1_wo"] = np.asarray(inputs["L1_attn_Wo"])
    m["d_a1_bo"] = np.asarray(inputs["L1_attn_bo"]).reshape(1, 1024)
    return m


def kernel(**inputs):
    consts = host_consts()
    n = 8
    m = _prep_host_weights(inputs)
    x = np.asarray(inputs["L0_attn_x"])
    P = _get_prog("mega")
    in_maps = [{"xin": np.ascontiguousarray(x[b]), **m, **consts} for b in range(n)]
    res = run_bass_kernel_spmd(P.nc, in_maps, core_ids=list(range(n)))
    return np.stack([res.results[b]["out"] for b in range(n)]).astype(np.float32)


def kernel_unfused(**inputs):
    consts = host_consts()
    n = 8
    w_f0 = prep_l0ffn_weights(inputs["L0_ffn_W_in"], inputs["L0_ffn_b_in"],
                              inputs["L0_ffn_W_out"])
    w_a0 = {"wq": np.asarray(inputs["L0_attn_Wq"]), "wk": np.asarray(inputs["L0_attn_Wk"]),
            "wv": np.asarray(inputs["L0_attn_Wv"]), "wo": np.asarray(inputs["L0_attn_Wo"]),
            "bq": np.asarray(inputs["L0_attn_bq"]).reshape(1, 1024),
            "bk": np.asarray(inputs["L0_attn_bk"]).reshape(1, 1024),
            "bv": np.asarray(inputs["L0_attn_bv"]).reshape(1, 1024),
            "bo": np.asarray(inputs["L0_attn_bo"]).reshape(1, 1024),
            "decay": np.asarray(inputs["L0_attn_decay_param"],
                                np.float32).reshape(1, 1)}
    w_f1 = prep_l1ffn_weights(inputs["L1_ffn_W1"], inputs["L1_ffn_b1"],
                              inputs["L1_ffn_W2"], inputs["L1_ffn_b2"],
                              inputs["L1_ffn_scale"])
    w_a1 = {"wq": np.asarray(inputs["L1_attn_Wq"]), "wk": np.asarray(inputs["L1_attn_Wk"]),
            "wv": np.asarray(inputs["L1_attn_Wv"]), "wg": np.asarray(inputs["L1_attn_Wg"]),
            "wo": np.asarray(inputs["L1_attn_Wo"]),
            "bq": np.asarray(inputs["L1_attn_bq"]).reshape(1, 2048),
            "bk": np.asarray(inputs["L1_attn_bk"]).reshape(1, 2048),
            "bv": np.asarray(inputs["L1_attn_bv"]).reshape(1, 2048),
            "bg": np.asarray(inputs["L1_attn_bg"]).reshape(1, 2048),
            "bo": np.asarray(inputs["L1_attn_bo"]).reshape(1, 1024)}

    x = np.asarray(inputs["L0_attn_x"])
    resid = [np.ascontiguousarray(x[b]) for b in range(n)]

    def launch(pname, extra):
        P = _get_prog(pname)
        in_maps = [{"xin": resid[b], **extra, **consts} for b in range(n)]
        res = run_bass_kernel_spmd(P.nc, in_maps, core_ids=list(range(n)))
        for b in range(n):
            resid[b] = res.results[b]["xout"]

    launch("l0ffn", w_f0)
    launch("l0attn", {k: v for k, v in w_a0.items()})
    launch("l0ffn", w_f0)
    launch("l1ffn", w_f1)
    launch("l1attn", w_a1)
    launch("l1ffn", w_f1)
    launch("l1ffn", w_f1)
    return np.stack(resid).astype(np.float32)


# version 14 (final)
# speedup vs baseline: 1.3981x; 1.0120x over previous
import numpy as np
import ml_dtypes
import concourse.bass as bass
import concourse.mybir as mybir
from concourse.bass_utils import run_bass_kernel_spmd
from contextlib import ExitStack

F32 = mybir.dt.float32
BF16 = mybir.dt.bfloat16
AF = mybir.ActivationFunctionType
ALU = mybir.AluOpType
AX = mybir.AxisListType

SEQ = 4096
DM = 1024
NG = SEQ // 128
TREE_OFF = [8 * ((1 << d) - 1) for d in range(9)]
BF16NP = ml_dtypes.bfloat16


def tree_perm():
    cur = [(t, 0) for t in range(8)]
    order = list(cur)
    for _ in range(7):
        lefts = [(t, 2 * c + 1) for (t, c) in cur]
        rights = [(t, 2 * c + 2) for (t, c) in cur]
        cur = lefts + rights
        order += cur
    return np.array([t * 255 + c for (t, c) in order], dtype=np.int64)


def host_consts():
    ident = np.eye(128, dtype=BF16NP)
    ones_row = np.ones((1, 512), dtype=BF16NP)
    cm = np.where(np.tril(np.ones((128, 128), bool)), 0.0, -1e30).astype(np.float32)
    jj, ii = np.meshgrid(np.arange(128), np.arange(128), indexing="ij")
    expt = np.where(ii > jj, (ii - 1 - jj).astype(np.float32), 0.0)
    stril = (ii > jj).astype(np.float32)
    return {"c_ident": ident, "c_ones": ones_row, "c_cmask": cm,
            "c_expt": expt, "c_stril": stril}


class Prog:
    def __init__(self):
        self.nc = bass.Bass("TRN2", target_bir_lowering=False)
        self.ctx = ExitStack()
        self.ctx.__enter__()
        self.block = None
        self.sems = {}
        self.bufs = {}
        self.cvec_slots = {}
        for s in ["ld", "gld", "sld", "mm", "dve", "act", "pool", "st", "wst"]:
            h = self.ctx.enter_context(self.nc.semaphore(s))
            self.sems[s] = [h, 0]

    def sb(self, name, shape, dtype=F32):
        ctx = getattr(self, "scope_ctx", None) or self.ctx
        t = ctx.enter_context(self.nc.sbuf_tensor(name, shape, dtype))
        self.bufs[name] = t
        return t

    def psum(self, name, shape, dtype=F32):
        ctx = getattr(self, "scope_ctx", None) or self.ctx
        return ctx.enter_context(self.nc.psum_tensor(name, shape, dtype))

    def open_scope(self):
        self.scope_ctx = ExitStack()
        self.scope_ctx.__enter__()

    def close_scope(self):
        self.scope_ctx.__exit__(None, None, None)
        self.scope_ctx = None

    def h(self, name):
        return self.sems[name][0]

    def bump(self, name, k=1):
        s = self.sems[name]
        s[1] += k
        return s[1]

    def cnt(self, name):
        return self.sems[name][1]

    def cvec(self, value):
        key = float(value)
        if key not in self.cvec_slots:
            self.cvec_slots[key] = len(self.cvec_slots)
            assert len(self.cvec_slots) <= 8
        return self.bufs["cvec"][:, self.cvec_slots[key]:self.cvec_slots[key] + 1]

    def finish(self):
        self.ctx.__exit__(None, None, None)


class _FakeIns:
    def __init__(self, blk, eng):
        self.blk, self.eng = blk, eng

    def then_inc(self, sem, amt):
        self.blk.rec[self.eng][-1][1].append((id(sem), amt))
        return self


class _FakeEng:
    def __init__(self, blk, name):
        self.blk, self.name = blk, name

    def wait_ge(self, sem, val):
        self.blk.rec[self.name].append(("wait", (id(sem), val), self.blk.tags.get(id(sem))))

    def __getattr__(self, nm):
        def op(*a, **k):
            self.blk.rec[self.name].append((nm, [], None))
            return _FakeIns(self.blk, self.name)
        return op


class FakeBlock:

    ENGS = ["sync", "tensor", "vector", "scalar", "gpsimd"]

    def __init__(self, P):
        self.rec = {e: [] for e in self.ENGS}
        self.tags = {h.__repr__() and id(h): nm for nm, (h, _) in P.sems.items()}

    def _run(self, name, f):
        f(_FakeEng(self, name))

    def sync(self, f):
        self._run("sync", f)

    def tensor(self, f):
        self._run("tensor", f)

    def vector(self, f):
        self._run("vector", f)

    def scalar(self, f):
        self._run("scalar", f)

    def gpsimd(self, f):
        self._run("gpsimd", f)

    def simulate(self):
        ptr = {e: 0 for e in self.ENGS}
        sem = {}
        stuck = 0
        while True:
            progress = False
            done = True
            for e in self.ENGS:
                q = self.rec[e]
                while ptr[e] < len(q):
                    kind, payload, tag = q[ptr[e]][0], q[ptr[e]][1], q[ptr[e]][2]
                    if kind == "wait":
                        sid, val = payload
                        if sem.get(sid, 0) >= val:
                            ptr[e] += 1
                            progress = True
                            continue
                        done = False
                        break
                    else:
                        for sid, amt in payload:
                            sem[sid] = sem.get(sid, 0) + amt
                        ptr[e] += 1
                        progress = True
                if ptr[e] < len(q):
                    done = False
            if done:
                return True, None
            if not progress:
                info = []
                for e in self.ENGS:
                    q = self.rec[e]
                    if ptr[e] < len(q):
                        kind, payload, tag = q[ptr[e]]
                        info.append((e, ptr[e], len(q), tag, payload[1] if kind == "wait" else None,
                                     sem.get(payload[0], 0) if kind == "wait" else None))
                return False, info


def alloc_common(P):
    P.sb("ident", [128, 128], BF16)
    P.sb("ones", [1, 512], BF16)
    P.sb("cmask", [128, 128], F32)
    P.sb("cvec", [128, 8], F32)
    P.sb("onesf", [1, 128], F32)
    P.sb("iota_p", [128, 1], F32)
    P.sb("iota_r", [128, 1], F32)
    P.sb("swp0", [128, 1024], F32)
    P.sb("swp1", [128, 1024], F32)
    P.sb("mv", [128, 12 + 2 * NG], F32)
    P.sb("varc", [128, 2 * NG], F32)
    P.sb("rst", [128, NG], F32)
    P.sb("nmr", [128, NG], F32)


def emit_preamble(P, dram, want_cvecs):
    block = P.block
    for v in want_cvecs:
        P.cvec(v)

    def loads(s):
        for name in ["ident", "ones", "cmask"]:
            s.dma_start(out=P.bufs[name][:], in_=dram["c_" + name]).then_inc(P.h("ld"), 16)
            P.bump("ld", 16)
    block.sync(loads)

    def vsec(v):
        for val, slot in P.cvec_slots.items():
            v.memset(P.bufs["cvec"][:, slot:slot + 1], val)
        ins = v.memset(P.bufs["onesf"][:], 1.0)
        ins.then_inc(P.h("dve"), 1)
        P.bump("dve")
    block.vector(vsec)

    def gsec(g):
        g.iota(P.bufs["iota_p"][:], pattern=[[0, 1]], base=0, channel_multiplier=1,
               allow_small_or_imprecise_dtypes=True)
        ins = g.iota(P.bufs["iota_r"][:], pattern=[[0, 1]], base=127, channel_multiplier=-1,
                     allow_small_or_imprecise_dtypes=True)
        ins.then_inc(P.h("pool"), 1)
        P.bump("pool")
    block.gpsimd(gsec)
    return P.cnt("dve"), P.cnt("pool")


def emit_stats_sweep(P, resid_ap, pre_dve):
    block = P.block
    bufs = [P.bufs["swp0"], P.bufs["swp1"]]
    mv, varc = P.bufs["mv"], P.bufs["varc"]
    rst, nmr = P.bufs["rst"], P.bufs["nmr"]

    ld0 = P.cnt("ld")
    dve0 = P.cnt("dve")

    def loads(s):
        for g in range(NG):
            if g >= 2:
                s.wait_ge(P.h("dve"), dve0 + (g - 1))
            s.dma_start(out=bufs[g % 2][:],
                        in_=resid_ap[128 * g:128 * (g + 1), :]).then_inc(P.h("ld"), 16)
            P.bump("ld", 16)
    block.sync(loads)

    def vsec(v):
        v.wait_ge(P.h("dve"), pre_dve)
        for g in range(NG):
            v.wait_ge(P.h("ld"), ld0 + 16 * (g + 1))
            buf = bufs[g % 2]
            v.bn_stats(mv[:, 0:6], buf[:, 0:512])
            v.bn_stats(mv[:, 6:12], buf[:, 512:1024])
            v.drain()
            ins = v.bn_aggr(mv[:, 12 + 2 * g:14 + 2 * g], mv[:, 0:12])
            ins.then_inc(P.h("dve"), 1)
            P.bump("dve")
        v.drain()
        ins = v.tensor_copy(varc[:, 0:NG], mv[:, 13:12 + 2 * NG:2])
        ins.then_inc(P.h("dve"), 1)
        P.bump("dve")
    block.vector(vsec)
    t_varc = P.cnt("dve")

    def asec(s):
        s.wait_ge(P.h("dve"), t_varc)
        s.activation(varc[:, NG:2 * NG], varc[:, 0:NG], AF.Ln, bias=P.cvec(1e-5))
        s.drain()
        ins = s.activation(rst[:, 0:NG], varc[:, NG:2 * NG], AF.Exp, scale=P.cvec(-0.5))
        ins.then_inc(P.h("act"), 1)
        P.bump("act")
    block.scalar(asec)
    t_rst = P.cnt("act")

    def vsec2(v):
        v.wait_ge(P.h("act"), t_rst)
        ins = v.scalar_tensor_tensor(nmr[:, 0:NG], mv[:, 12:12 + 2 * NG:2], -1.0,
                                     rst[:, 0:NG], op0=ALU.mult, op1=ALU.mult)
        v.drain()
        ins.then_inc(P.h("dve"), 1)
        P.bump("dve")
    block.vector(vsec2)
    return P.cnt("dve")


def emit_stats_finish(P, sld_target):
    block = P.block
    mv, varc = P.bufs["mv"], P.bufs["varc"]
    rst, nmr = P.bufs["rst"], P.bufs["nmr"]

    def vsec(v):
        v.wait_ge(P.h("sld"), sld_target)
        ins = v.tensor_copy(varc[:, 0:NG], mv[:, 13:12 + 2 * NG:2])
        v.drain()
        ins.then_inc(P.h("dve"), 1)
        P.bump("dve")
    block.vector(vsec)
    t_varc = P.cnt("dve")

    def asec(sc):
        sc.wait_ge(P.h("dve"), t_varc)
        sc.activation(varc[:, NG:2 * NG], varc[:, 0:NG], AF.Ln, bias=P.cvec(1e-5))
        sc.drain()
        ins = sc.activation(rst[:, 0:NG], varc[:, NG:2 * NG], AF.Exp, scale=P.cvec(-0.5))
        ins.then_inc(P.h("act"), 1)
        P.bump("act")
    block.scalar(asec)
    t_rst = P.cnt("act")

    def vsec2(v):
        v.wait_ge(P.h("act"), t_rst)
        ins = v.scalar_tensor_tensor(nmr[:, 0:NG], mv[:, 12:12 + 2 * NG:2], -1.0,
                                     rst[:, 0:NG], op0=ALU.mult, op1=ALU.mult)
        v.drain()
        ins.then_inc(P.h("dve"), 1)
        P.bump("dve")
    block.vector(vsec2)
    return P.cnt("dve")


def _emit_writer_stats(P, v, buf, g):
    mv = P.bufs["mv"]
    v.drain()
    v.bn_stats(mv[:, 0:6], buf[:, 0:512])
    v.bn_stats(mv[:, 6:12], buf[:, 512:1024])
    v.drain()
    ins = v.bn_aggr(mv[:, 12 + 2 * g:14 + 2 * g], mv[:, 0:12])
    ins.then_inc(P.h("sld"), 1)
    P.bump("sld")


def emit_l0ffn(P, tag, resid_in, resid_out, w, stats_ready, do_stats=True):
    nc, block = P.nc, P.block

    winT = P.sb(tag + "_winT", [128, 8 * 2048], BF16)
    bin_row = P.sb(tag + "_bin", [1, 2048], BF16)
    woutT = P.sb(tag + "_woutT", [128, 16 * 1024], BF16)
    xs = [P.sb(tag + "_x0", [128, 1024], F32), P.sb(tag + "_x1", [128, 1024], F32)]
    ybfs = [P.sb(tag + "_y0", [128, 1024], BF16), P.sb(tag + "_y1", [128, 1024], BF16)]
    yfms = [P.sb(tag + "_yf0", [128, 1024], BF16), P.sb(tag + "_yf1", [128, 1024], BF16)]
    actss = [P.sb(tag + "_a0", [128, 2048], F32), P.sb(tag + "_a1", [128, 2048], F32)]
    mask = P.sb(tag + "_mask", [128, 2048], F32)
    mbfs = [P.sb(tag + "_mb0", [128, 2048], BF16), P.sb(tag + "_mb1", [128, 2048], BF16)]
    mfms = [P.sb(tag + "_mf0", [128, 2048], BF16), P.sb(tag + "_mf1", [128, 2048], BF16)]
    os_ = [P.sb(tag + "_o0", [128, 1024], F32), P.sb(tag + "_o1", [128, 1024], F32)]

    pls = [P.psum(tag + f"_pl{i}", [128, 512], F32) for i in range(4)]
    pst = P.psum(tag + "_pt", [128, 1024], BF16)
    pst2 = P.psum(tag + "_pu", [128, 2048], BF16)
    po = P.psum(tag + "_po", [128, 512], F32)

    gld0 = P.cnt("gld")

    def wloads(g):
        for k in range(8):
            g.dma_start(out=winT[:, 2048 * k:2048 * (k + 1)],
                        in_=w["winT"][128 * k:128 * (k + 1), :]).then_inc(P.h("gld"), 16)
            P.bump("gld", 16)
        g.dma_start(out=bin_row[:], in_=w["b_in"]).then_inc(P.h("gld"), 16)
        P.bump("gld", 16)
        for k in range(16):
            g.dma_start(out=woutT[:, 1024 * k:1024 * (k + 1)],
                        in_=w["woutT"][128 * k:128 * (k + 1), :]).then_inc(P.h("gld"), 16)
            P.bump("gld", 16)
    block.gpsimd(wloads)
    w_ready = P.cnt("gld")

    dve_pre0 = P.cnt("dve")

    def vpre(v):
        v.memset(mask[:, 0:8], 1.0)
        ins = v.memset(mask[:, 2040:2048], 0.0)
        v.drain()
        ins.then_inc(P.h("dve"), 1)
        P.bump("dve")
    block.vector(vpre)

    ld0 = P.cnt("ld")
    mm0 = P.cnt("mm")
    dve0 = P.cnt("dve")
    act0 = P.cnt("act")
    st0 = P.cnt("st")

    Mtab = {}
    pos = mm0
    for g in range(NG):
        for i in (1, 2):
            pos += 1
            Mtab[(g, i)] = pos
        if g >= 1:
            for i in (3, 4, 5):
                pos += 1
                Mtab[(g - 1, i)] = pos
    for i in (3, 4, 5):
        pos += 1
        Mtab[(NG - 1, i)] = pos
    Dtab = {}
    pos = dve0
    for g in range(NG):
        for i in (1, 2):
            pos += 1
            Dtab[(g, i)] = pos
        if g >= 1:
            for i in (3, 4, 5, 6):
                pos += 1
                Dtab[(g - 1, i)] = pos
    for i in (3, 4, 5, 6):
        pos += 1
        Dtab[(NG - 1, i)] = pos

    def MM(g, i):
        return Mtab[(g, i)]

    def DVE(g, i):
        return Dtab[(g, i)]

    def loads(s):
        for g in range(NG):
            if g >= 2:
                s.wait_ge(P.h("dve"), DVE(g - 2, 6))
            s.dma_start(out=xs[g % 2][:],
                        in_=resid_in[128 * g:128 * (g + 1), :]).then_inc(P.h("ld"), 16)
            P.bump("ld", 16)
    block.sync(loads)

    def tsec(t):
        def emit_ab(g):
            t.wait_ge(P.h("dve"), DVE(g, 1))
            if g == 0:
                t.wait_ge(P.h("gld"), w_ready)
            if g >= 1:
                t.wait_ge(P.h("dve"), DVE(g - 1, 2))
            y = ybfs[g % 2]
            for k in range(8):
                ins = t.transpose(pst[:, 128 * k:128 * (k + 1)],
                                  y[:, 128 * k:128 * (k + 1)], P.bufs["ident"][:])
            ins.then_inc(P.h("mm"), 1)
            P.bump("mm")
            t.wait_ge(P.h("dve"), DVE(g, 2))
            if g >= 1:
                t.wait_ge(P.h("act"), act0 + g)
            yf = yfms[g % 2]
            for nb in range(4):
                for k in range(8):
                    t.matmul(pls[nb][:], yf[:, 128 * k:128 * (k + 1)],
                             winT[:, 2048 * k + 512 * nb:2048 * k + 512 * (nb + 1)],
                             start=(k == 0), stop=False)
                ins = t.matmul(pls[nb][:], P.bufs["ones"][0:1, 0:128],
                               bin_row[0:1, 512 * nb:512 * (nb + 1)],
                               start=False, stop=True)
            ins.then_inc(P.h("mm"), 1)
            P.bump("mm")

        def emit_cd(g):
            t.wait_ge(P.h("dve"), DVE(g, 3))
            if g >= 1:
                t.wait_ge(P.h("dve"), DVE(g - 1, 4))
            mb = mbfs[g % 2]
            for c in range(16):
                ins = t.transpose(pst2[:, 128 * c:128 * (c + 1)],
                                  mb[:, 128 * c:128 * (c + 1)], P.bufs["ident"][:])
            ins.then_inc(P.h("mm"), 1)
            P.bump("mm")
            mf = mfms[g % 2]
            for nb in range(2):
                t.wait_ge(P.h("dve"), DVE(g, 4) if nb == 0 else DVE(g, 5))
                if nb == 0 and g >= 1:
                    t.wait_ge(P.h("dve"), DVE(g - 1, 6))
                for c in range(16):
                    ins = t.matmul(po[:],
                                   mf[:, 128 * c:128 * (c + 1)],
                                   woutT[:, 1024 * c + 512 * nb:1024 * c + 512 * (nb + 1)],
                                   start=(c == 0), stop=(c == 15))
                ins.then_inc(P.h("mm"), 1)
                P.bump("mm")

        for g in range(NG):
            emit_ab(g)
            if g >= 1:
                emit_cd(g - 1)
        emit_cd(NG - 1)
    block.tensor(tsec)

    def asec(s):
        for g in range(NG):
            s.wait_ge(P.h("mm"), MM(g, 2))
            a = actss[g % 2]
            if g >= 2:
                s.wait_ge(P.h("dve"), DVE(g - 2, 3))
            for nb in range(4):
                ins = s.activation(a[:, 512 * nb:512 * (nb + 1)], pls[nb][:], AF.Silu)
            ins.then_inc(P.h("act"), 1)
            P.bump("act")
    block.scalar(asec)

    def vsec(v):
        def emit_ny(g):
            x = xs[g % 2]
            v.wait_ge(P.h("ld"), ld0 + 16 * (g + 1))
            if g == 0:
                v.wait_ge(P.h("dve"), stats_ready)
                v.wait_ge(P.h("dve"), dve_pre0 + 1)
            if g >= 2:
                v.wait_ge(P.h("mm"), MM(g - 2, 1))
            ins = v.tensor_scalar(ybfs[g % 2][:], x[:], P.bufs["rst"][:, g:g + 1],
                                  P.bufs["nmr"][:, g:g + 1], op0=ALU.mult, op1=ALU.add)
            ins.then_inc(P.h("dve"), 1)
            P.bump("dve")
            v.wait_ge(P.h("mm"), MM(g, 1))
            if g >= 2:
                v.wait_ge(P.h("mm"), MM(g - 2, 2))
            ins = v.tensor_copy(yfms[g % 2][:], pst[:])
            ins.then_inc(P.h("dve"), 1)
            P.bump("dve")

        def emit_rest(g):
            x = xs[g % 2]
            v.wait_ge(P.h("act"), act0 + g + 1)
            a = actss[g % 2]
            for d in range(7):
                B = 8 << d
                op, oc = TREE_OFF[d], TREE_OFF[d + 1]
                v.scalar_tensor_tensor(mask[:, oc + B:oc + 2 * B], a[:, op:op + B], 0.0,
                                       mask[:, op:op + B], op0=ALU.is_gt, op1=ALU.mult)
                v.scalar_tensor_tensor(mask[:, oc:oc + B], a[:, op:op + B], 0.0,
                                       mask[:, op:op + B], op0=ALU.is_le, op1=ALU.mult)
                v.drain()
            if g >= 2:
                v.wait_ge(P.h("mm"), MM(g - 2, 3))
            ins = v.tensor_tensor(mbfs[g % 2][:], a[:], mask[:], op=ALU.mult)
            ins.then_inc(P.h("dve"), 1)
            P.bump("dve")
            v.wait_ge(P.h("mm"), MM(g, 3))
            if g >= 2:
                v.wait_ge(P.h("mm"), MM(g - 2, 5))
            ins = v.tensor_copy(mfms[g % 2][:], pst2[:])
            ins.then_inc(P.h("dve"), 1)
            P.bump("dve")
            for nb in range(2):
                v.wait_ge(P.h("mm"), MM(g, 4 + nb))
                if nb == 0 and g >= 2:
                    v.wait_ge(P.h("st"), st0 + 16 * (g - 1))
                ins = v.scalar_tensor_tensor(os_[g % 2][:, 512 * nb:512 * (nb + 1)],
                                             po[:], 0.5, x[:, 512 * nb:512 * (nb + 1)],
                                             op0=ALU.mult, op1=ALU.add)
                ins.then_inc(P.h("dve"), 1)
                P.bump("dve")
            if do_stats:
                _emit_writer_stats(P, v, os_[g % 2], g)

        for g in range(NG):
            emit_ny(g)
            if g >= 1:
                emit_rest(g - 1)
        emit_rest(NG - 1)
    block.vector(vsec)

    def stores(g_):
        for g in range(NG):
            g_.wait_ge(P.h("dve"), DVE(g, 6))
            g_.dma_start(out=resid_out[128 * g:128 * (g + 1), :],
                         in_=os_[g % 2][:]).then_inc(P.h("st"), 16)
            P.bump("st", 16)
    block.gpsimd(stores)
    return P.cnt("st")


def emit_phase_barrier(P):
    block = P.block
    targets = [(nm, P.cnt(nm)) for nm in ["ld", "gld", "sld", "mm", "dve", "act", "pool", "st", "wst"]]

    def mk(engname):
        def f(e):
            for nm, v in targets:
                if v > 0:
                    e.wait_ge(P.h(nm), v)
        return f
    block.sync(mk("sync"))
    block.tensor(mk("tensor"))
    block.vector(mk("vector"))
    block.scalar(mk("scalar"))
    block.gpsimd(mk("gpsimd"))


def emit_decay_prep(P, tag, decay_dram, pre_dve, consts):
    nc, block = P.nc, P.block
    dp = P.sb(tag + "_dp", [1, 1], F32)
    expt = P.sb(tag + "_expt", [128, 128], F32)
    stril = P.sb(tag + "_stril", [128, 128], F32)
    P.bufs["expt"] = expt
    P.bufs["stril"] = stril
    onesf = P.bufs["onesf"]
    lnd1 = P.sb(tag + "_lnd1", [1, 1], F32)
    lndv = P.sb(tag + "_lndv", [128, 1], F32)
    dpows = P.sb(tag + "_dpows", [128, 1], F32)
    drev = P.sb(tag + "_drev", [128, 1], F32)
    d128 = P.sb(tag + "_d128", [128, 1], F32)
    dmt = P.sb(tag + "_dmt", [128, 128], F32)
    dmte = P.sb(tag + "_dmte", [128, 128], F32)
    plnd_cm = P.nc.psum_tensor(tag + "_plnd", [128, 1], F32)
    plnd = plnd_cm.__enter__()

    ld0 = P.cnt("ld")

    def loads(s):
        s.dma_start(out=dp[:], in_=decay_dram).then_inc(P.h("ld"), 16)
        P.bump("ld", 16)
        s.dma_start(out=expt[:], in_=consts["c_expt"]).then_inc(P.h("ld"), 16)
        P.bump("ld", 16)
        s.dma_start(out=stril[:], in_=consts["c_stril"]).then_inc(P.h("ld"), 16)
        P.bump("ld", 16)
    block.sync(loads)
    t_ld = P.cnt("ld")

    t_ones = pre_dve

    def as0(s):
        s.wait_ge(P.h("ld"), t_ld)
        ins = s.activation(dp[:], dp[:], AF.Sigmoid)
        s.drain()
        ins.then_inc(P.h("act"), 1)
        P.bump("act")
    block.scalar(as0)
    t_sig = P.cnt("act")

    def vs1(v):
        v.wait_ge(P.h("act"), t_sig)
        ins = v.tensor_scalar(dp[:], dp[:], 1e-19, None, op0=ALU.max)
        v.drain()
        ins.then_inc(P.h("dve"), 1)
        P.bump("dve")
    block.vector(vs1)
    t_clamp = P.cnt("dve")

    def as1(s):
        s.wait_ge(P.h("dve"), t_clamp)
        ins = s.activation(lnd1[:], dp[:], AF.Ln)
        s.drain()
        ins.then_inc(P.h("act"), 1)
        P.bump("act")
    block.scalar(as1)
    t_lnd = P.cnt("act")

    def ts0(t):
        t.wait_ge(P.h("act"), t_lnd)
        t.wait_ge(P.h("dve"), t_ones)
        ins = t.matmul(plnd[:], onesf[0:1, :], lnd1[:], start=True, stop=True)
        ins.then_inc(P.h("mm"), 1)
        P.bump("mm")
    block.tensor(ts0)
    t_bc = P.cnt("mm")

    def vs2(v):
        v.wait_ge(P.h("mm"), t_bc)
        ins = v.tensor_copy(lndv[:], plnd[:])
        v.drain()
        ins.then_inc(P.h("dve"), 1)
        P.bump("dve")
    block.vector(vs2)
    t_lndv = P.cnt("dve")

    def as2(s):
        s.wait_ge(P.h("dve"), t_lndv)
        s.wait_ge(P.h("pool"), 1)
        s.activation(dpows[:], P.bufs["iota_p"][:], AF.Exp, scale=lndv[:, 0:1])
        s.activation(drev[:], P.bufs["iota_r"][:], AF.Exp, scale=lndv[:, 0:1])
        s.activation(d128[:], lndv[:], AF.Exp, scale=P.cvec(128.0))
        ins = s.activation(dmte[:], P.bufs["expt"][:], AF.Exp, scale=lndv[:, 0:1])
        ins.then_inc(P.h("act"), 1)
        P.bump("act")
    block.scalar(as2)
    t_exps = P.cnt("act")

    def vs3(v):
        v.wait_ge(P.h("act"), t_exps)
        ins = v.tensor_tensor(dmt[:], dmte[:], P.bufs["stril"][:], op=ALU.mult)
        ins.then_inc(P.h("dve"), 1)
        P.bump("dve")
    block.vector(vs3)
    plnd_cm.__exit__(None, None, None)
    return P.cnt("dve"), {"dpows": dpows, "drev": drev, "d128": d128, "dmt": dmt}


def emit_l0attn(P, tag, resid_in, resid_out, w, stats_ready, dec_ready, dv, do_stats=True):
    nc, block = P.nc, P.block
    NB = 8

    wq = P.sb(tag + "_wq", [128, 8 * 1024], BF16)
    wk = P.sb(tag + "_wk", [128, 8 * 1024], BF16)
    wv = P.sb(tag + "_wv", [128, 8 * 1024], BF16)
    wo = P.sb(tag + "_wo", [128, 8 * 1024], BF16)
    brows = P.sb(tag + "_br", [1, 4 * 1024], BF16)
    mem = P.sb(tag + "_mem", [128, 8 * 1024], F32)
    membf = P.sb(tag + "_membf", [128, 8 * 1024], BF16)
    xs = [P.sb(tag + f"_x{i}", [128, 1024], F32) for i in range(2)]
    ybfs = [P.sb(tag + f"_y{i}", [128, 1024], BF16) for i in range(2)]
    yfms = [P.sb(tag + f"_yf{i}", [128, 1024], BF16) for i in range(2)]
    qtm = P.sb(tag + "_qtm", [128, 1024], BF16)
    qfm = P.sb(tag + "_qfm", [128, 1024], BF16)
    ktm = P.sb(tag + "_ktm", [128, 1024], BF16)
    kwtm = P.sb(tag + "_kwtm", [128, 1024], BF16)
    kfm = P.sb(tag + "_kfm", [128, 1024], BF16)
    vbf = P.sb(tag + "_vbf", [128, 1024], BF16)
    rtmp = P.sb(tag + "_rt", [128, 1024], F32)
    etmp = P.sb(tag + "_et", [128, 1024], F32)
    abf = P.sb(tag + "_abf", [128, 128], BF16)
    ctmp = P.sb(tag + "_ct", [128, 1024], F32)
    otm = P.sb(tag + "_otm", [128, 1024], BF16)
    ofm = P.sb(tag + "_ofm", [128, 1024], BF16)
    os_ = [P.sb(tag + f"_o{i}", [128, 1024], F32) for i in range(2)]

    pstA = P.psum(tag + "_pstA", [128, 1024], BF16)
    pstB = P.psum(tag + "_pstB", [128, 1024], BF16)
    pp0 = P.psum(tag + "_pp0", [128, 1024], F32)
    pp1 = P.psum(tag + "_pp1", [128, 1024], F32)
    pp2 = P.psum(tag + "_pp2", [128, 1024], F32)

    def wloads(g):
        for nm, buf in [("wq", wq), ("wk", wk), ("wv", wv), ("wo", wo)]:
            for k in range(8):
                g.dma_start(out=buf[:, 1024 * k:1024 * (k + 1)],
                            in_=w[nm][128 * k:128 * (k + 1), :]).then_inc(P.h("gld"), 16)
                P.bump("gld", 16)
        for i, nm in enumerate(["bq", "bk", "bv", "bo"]):
            g.dma_start(out=brows[0:1, 1024 * i:1024 * (i + 1)],
                        in_=w[nm]).then_inc(P.h("gld"), 16)
            P.bump("gld", 16)
    block.gpsimd(wloads)
    w_ready = P.cnt("gld")

    dve_pre0 = P.cnt("dve")

    def vpre(v):
        v.memset(mem[:], 0.0)
        ins = v.memset(membf[:], 0.0)
        ins.then_inc(P.h("dve"), 1)
        P.bump("dve")
    block.vector(vpre)
    mem_init = P.cnt("dve")

    ld0 = P.cnt("ld")
    mm0 = P.cnt("mm")
    dve0 = P.cnt("dve")
    act0 = P.cnt("act")
    pool0 = P.cnt("pool")
    st0 = P.cnt("st")

    NMM = 19
    MMI = {"ytrans": 1, "qmm": 2, "qtrans": 3, "kmm": 4, "ktrans": 5, "vmm": 6,
           "scores": 7, "cross": 8, "intra": 9, "atrans": 10, "fin": 11}
    NDVE = 21
    DVI = {"norm": 1, "yfm": 2, "qevac": 3, "qfm": 4, "kelu": 5, "kw": 6, "kfm": 7,
           "vevac": 8, "aevac": 9, "cscale": 10, "ocomb": 11, "ofm": 12, "resid": 13}
    NACT = 2
    NPOOL = 8

    def MM(c, nm):
        return mm0 + NMM * c + (MMI[nm] if isinstance(nm, str) else nm)

    def MMU(c, m):
        return mm0 + NMM * c + 11 + (m + 1)

    def DVE(c, nm):
        return dve0 + NDVE * c + DVI[nm]

    def DVU(c, m):
        return dve0 + NDVE * c + 13 + (m + 1)

    def loads(s):
        for c in range(NG):
            if c >= 2:
                s.wait_ge(P.h("dve"), DVE(c - 2, "resid"))
            s.dma_start(out=xs[c % 2][:],
                        in_=resid_in[128 * c:128 * (c + 1), :]).then_inc(P.h("ld"), 16)
            P.bump("ld", 16)
    block.sync(loads)

    ONES = P.bufs["ones"]

    def tsec(t):
        for c in range(NG):
            t.wait_ge(P.h("dve"), DVE(c, "norm"))
            if c == 0:
                t.wait_ge(P.h("gld"), w_ready)
            if c >= 1:
                t.wait_ge(P.h("dve"), DVE(c - 1, "kfm"))
            for k in range(8):
                ins = t.transpose(pstA[:, 128 * k:128 * (k + 1)],
                                  ybfs[c % 2][:, 128 * k:128 * (k + 1)], P.bufs["ident"][:])
            ins.then_inc(P.h("mm"), 1)
            P.bump("mm")
            yf = yfms[c % 2]
            t.wait_ge(P.h("dve"), DVE(c, "yfm"))
            if c >= 1:
                t.wait_ge(P.h("dve"), DVU(c - 1, 6))
            for nb in range(2):
                for k in range(8):
                    t.matmul(pp0[:, 512 * nb:512 * (nb + 1)], yf[:, 128 * k:128 * (k + 1)],
                             wq[:, 1024 * k + 512 * nb:1024 * k + 512 * (nb + 1)],
                             start=(k == 0), stop=False)
                ins = t.matmul(pp0[:, 512 * nb:512 * (nb + 1)], ONES[0:1, 0:128],
                               brows[0:1, 512 * nb:512 * (nb + 1)], start=False, stop=True)
            ins.then_inc(P.h("mm"), 1)
            P.bump("mm")
            t.wait_ge(P.h("dve"), DVE(c, "qevac"))
            if c >= 1:
                t.wait_ge(P.h("dve"), DVE(c - 1, "ofm"))
            for k in range(8):
                ins = t.transpose(pstB[:, 128 * k:128 * (k + 1)],
                                  qtm[:, 128 * k:128 * (k + 1)], P.bufs["ident"][:])
            ins.then_inc(P.h("mm"), 1)
            P.bump("mm")
            if c >= 1:
                t.wait_ge(P.h("dve"), DVU(c - 1, 7))
            for nb in range(2):
                for k in range(8):
                    t.matmul(pp1[:, 512 * nb:512 * (nb + 1)], yf[:, 128 * k:128 * (k + 1)],
                             wk[:, 1024 * k + 512 * nb:1024 * k + 512 * (nb + 1)],
                             start=(k == 0), stop=False)
                ins = t.matmul(pp1[:, 512 * nb:512 * (nb + 1)], ONES[0:1, 0:128],
                               brows[0:1, 1024 + 512 * nb:1024 + 512 * (nb + 1)],
                               start=False, stop=True)
            ins.then_inc(P.h("mm"), 1)
            P.bump("mm")
            t.wait_ge(P.h("dve"), DVE(c, "kelu"))
            t.wait_ge(P.h("dve"), DVE(c, "yfm"))
            for k in range(8):
                ins = t.transpose(pstA[:, 128 * k:128 * (k + 1)],
                                  ktm[:, 128 * k:128 * (k + 1)], P.bufs["ident"][:])
            ins.then_inc(P.h("mm"), 1)
            P.bump("mm")
            if c >= 1:
                t.wait_ge(P.h("dve"), DVE(c - 1, "resid"))
            for nb in range(2):
                for k in range(8):
                    t.matmul(pp2[:, 512 * nb:512 * (nb + 1)], yf[:, 128 * k:128 * (k + 1)],
                             wv[:, 1024 * k + 512 * nb:1024 * k + 512 * (nb + 1)],
                             start=(k == 0), stop=False)
                ins = t.matmul(pp2[:, 512 * nb:512 * (nb + 1)], ONES[0:1, 0:128],
                               brows[0:1, 2048 + 512 * nb:2048 + 512 * (nb + 1)],
                               start=False, stop=True)
            ins.then_inc(P.h("mm"), 1)
            P.bump("mm")
            t.wait_ge(P.h("dve"), DVE(c, "qfm"))
            t.wait_ge(P.h("dve"), DVE(c, "kfm"))
            t.wait_ge(P.h("dve"), DVE(c, "qevac"))
            for k in range(8):
                ins = t.matmul(pp0[:, 0:128], kfm[:, 128 * k:128 * (k + 1)],
                               qfm[:, 128 * k:128 * (k + 1)], start=(k == 0), stop=(k == 7))
            ins.then_inc(P.h("mm"), 1)
            P.bump("mm")
            t.wait_ge(P.h("dve"), DVE(c, "aevac"))
            if c >= 1:
                t.wait_ge(P.h("pool"), pool0 + NPOOL * c)
            for nb in range(2):
                for dk in range(8):
                    ins = t.matmul(pp0[:, 512 * nb:512 * (nb + 1)], qfm[:, 128 * dk:128 * (dk + 1)],
                                   membf[:, 1024 * dk + 512 * nb:1024 * dk + 512 * (nb + 1)],
                                   start=(dk == 0), stop=(dk == 7))
            ins.then_inc(P.h("mm"), 1)
            P.bump("mm")
            t.wait_ge(P.h("act"), act0 + NACT * c + 2)
            t.wait_ge(P.h("dve"), DVE(c, "aevac"))
            t.wait_ge(P.h("dve"), DVE(c, "vevac"))
            for nb in range(2):
                ins = t.matmul(pp1[:, 512 * nb:512 * (nb + 1)], abf[:],
                               vbf[:, 512 * nb:512 * (nb + 1)], start=True, stop=True)
            ins.then_inc(P.h("mm"), 1)
            P.bump("mm")
            t.wait_ge(P.h("dve"), DVE(c, "ocomb"))
            t.wait_ge(P.h("dve"), DVE(c, "qfm"))
            for k in range(8):
                ins = t.transpose(pstB[:, 128 * k:128 * (k + 1)],
                                  otm[:, 128 * k:128 * (k + 1)], P.bufs["ident"][:])
            ins.then_inc(P.h("mm"), 1)
            P.bump("mm")
            t.wait_ge(P.h("dve"), DVE(c, "ofm"))
            t.wait_ge(P.h("dve"), DVE(c, "vevac"))
            for nb in range(2):
                for k in range(8):
                    t.matmul(pp2[:, 512 * nb:512 * (nb + 1)], ofm[:, 128 * k:128 * (k + 1)],
                             wo[:, 1024 * k + 512 * nb:1024 * k + 512 * (nb + 1)],
                             start=(k == 0), stop=False)
                ins = t.matmul(pp2[:, 512 * nb:512 * (nb + 1)], ONES[0:1, 0:128],
                               brows[0:1, 3072 + 512 * nb:3072 + 512 * (nb + 1)],
                               start=False, stop=True)
            ins.then_inc(P.h("mm"), 1)
            P.bump("mm")
            t.wait_ge(P.h("dve"), DVE(c, "kw"))
            for m in range(8):
                pp = pp0 if m % 2 == 0 else pp1
                if m == 0:
                    t.wait_ge(P.h("dve"), DVE(c, "cscale"))
                if m == 1:
                    t.wait_ge(P.h("dve"), DVE(c, "ocomb"))
                if m >= 2:
                    t.wait_ge(P.h("dve"), DVU(c, m - 2))
                for nb in range(2):
                    ins = t.matmul(pp[:, 512 * nb:512 * (nb + 1)], kwtm[:, 128 * m:128 * (m + 1)],
                                   vbf[:, 512 * nb:512 * (nb + 1)], start=True, stop=True)
                ins.then_inc(P.h("mm"), 1)
                P.bump("mm")
    block.tensor(tsec)

    def asec(s):
        for c in range(NG):
            s.wait_ge(P.h("mm"), MM(c, "kmm"))
            if c >= 1:
                s.wait_ge(P.h("dve"), DVE(c - 1, "kelu"))
            s.activation(rtmp[:], pp1[:], AF.Relu).then_inc(P.h("act"), 1)
            P.bump("act")
            s.activation(etmp[:], pp1[:], AF.Exp).then_inc(P.h("act"), 1)
            P.bump("act")
            for m in range(8):
                if m % 4 == 0:
                    continue
                s.wait_ge(P.h("dve"), DVU(c, m))
                s.wait_ge(P.h("mm"), MM(c, "cross"))
                ins = s.copy(membf[:, 1024 * m:1024 * (m + 1)],
                             mem[:, 1024 * m:1024 * (m + 1)])
                ins.then_inc(P.h("pool"), 1)
                P.bump("pool")
    block.scalar(asec)

    def vsec(v):
        for c in range(NG):
            x = xs[c % 2]
            v.wait_ge(P.h("ld"), ld0 + 16 * (c + 1))
            if c == 0:
                v.wait_ge(P.h("dve"), stats_ready)
                v.wait_ge(P.h("dve"), dec_ready)
                v.wait_ge(P.h("dve"), mem_init)
            if c >= 2:
                v.wait_ge(P.h("mm"), MM(c - 2, "ytrans"))
            ins = v.tensor_scalar(ybfs[c % 2][:], x[:], P.bufs["rst"][:, c:c + 1],
                                  P.bufs["nmr"][:, c:c + 1], op0=ALU.mult, op1=ALU.add)
            ins.then_inc(P.h("dve"), 1)
            P.bump("dve")
            v.wait_ge(P.h("mm"), MM(c, "ytrans"))
            if c >= 2:
                v.wait_ge(P.h("mm"), MM(c - 2, "vmm"))
            ins = v.tensor_copy(yfms[c % 2][:], pstA[:])
            ins.then_inc(P.h("dve"), 1)
            P.bump("dve")
            v.wait_ge(P.h("mm"), MM(c, "qmm"))
            if c >= 1:
                v.wait_ge(P.h("mm"), MM(c - 1, "qtrans"))
            ins = v.tensor_copy(qtm[:], pp0[:])
            ins.then_inc(P.h("dve"), 1)
            P.bump("dve")
            v.wait_ge(P.h("mm"), MM(c, "qtrans"))
            if c >= 1:
                v.wait_ge(P.h("mm"), MM(c - 1, "cross"))
            ins = v.tensor_copy(qfm[:], pstB[:])
            ins.then_inc(P.h("dve"), 1)
            P.bump("dve")
            v.wait_ge(P.h("act"), act0 + NACT * c + 2)
            if c >= 1:
                v.wait_ge(P.h("mm"), MM(c - 1, "ktrans"))
            ins = v.scalar_tensor_tensor(ktm[:], etmp[:], 1.0, rtmp[:],
                                         op0=ALU.min, op1=ALU.add)
            v.drain()
            ins.then_inc(P.h("dve"), 1)
            P.bump("dve")
            if c >= 1:
                v.wait_ge(P.h("mm"), MMU(c - 1, 7))
            ins = v.tensor_scalar(kwtm[:], ktm[:], dv["drev"][:, 0:1], None, op0=ALU.mult)
            ins.then_inc(P.h("dve"), 1)
            P.bump("dve")
            v.wait_ge(P.h("mm"), MM(c, "ktrans"))
            if c >= 1:
                v.wait_ge(P.h("mm"), MM(c - 1, "scores"))
            ins = v.tensor_copy(kfm[:], pstA[:])
            ins.then_inc(P.h("dve"), 1)
            P.bump("dve")
            v.wait_ge(P.h("mm"), MM(c, "vmm"))
            if c >= 1:
                v.wait_ge(P.h("mm"), MMU(c - 1, 7))
            ins = v.tensor_copy(vbf[:], pp2[:])
            ins.then_inc(P.h("dve"), 1)
            P.bump("dve")
            v.wait_ge(P.h("mm"), MM(c, "scores"))
            if c >= 1:
                v.wait_ge(P.h("mm"), MM(c - 1, "intra"))
            ins = v.tensor_tensor(abf[:], pp0[:, 0:128], dv["dmt"][:], op=ALU.mult)
            ins.then_inc(P.h("dve"), 1)
            P.bump("dve")
            v.wait_ge(P.h("mm"), MM(c, "cross"))
            ins = v.tensor_scalar(ctmp[:], pp0[:], dv["dpows"][:, 0:1], None, op0=ALU.mult)
            ins.then_inc(P.h("dve"), 1)
            P.bump("dve")
            v.wait_ge(P.h("mm"), MM(c, "intra"))
            if c >= 1:
                v.wait_ge(P.h("mm"), MM(c - 1, "atrans"))
            ins = v.tensor_tensor(otm[:], ctmp[:], pp1[:], op=ALU.add)
            ins.then_inc(P.h("dve"), 1)
            P.bump("dve")
            v.wait_ge(P.h("mm"), MM(c, "atrans"))
            if c >= 1:
                v.wait_ge(P.h("mm"), MM(c - 1, "fin"))
            ins = v.tensor_copy(ofm[:], pstB[:])
            ins.then_inc(P.h("dve"), 1)
            P.bump("dve")
            v.wait_ge(P.h("mm"), MM(c, "fin"))
            if c >= 2:
                v.wait_ge(P.h("st"), st0 + 16 * (c - 1))
            ins = v.tensor_tensor(os_[c % 2][:], x[:], pp2[:], op=ALU.add)
            ins.then_inc(P.h("dve"), 1)
            P.bump("dve")
            if do_stats:
                _emit_writer_stats(P, v, os_[c % 2], c)
            for m in range(8):
                pp = pp0 if m % 2 == 0 else pp1
                v.wait_ge(P.h("mm"), MMU(c, m))
                if c >= 1:
                    v.wait_ge(P.h("pool"), pool0 + NPOOL * c)
                ins = v.scalar_tensor_tensor(mem[:, 1024 * m:1024 * (m + 1)],
                                             mem[:, 1024 * m:1024 * (m + 1)],
                                             dv["d128"][:, 0:1], pp[:],
                                             op0=ALU.mult, op1=ALU.add)
                ins.then_inc(P.h("dve"), 1)
                P.bump("dve")
    block.vector(vsec)

    def psec(g):
        for c in range(NG):
            g.wait_ge(P.h("dve"), DVE(c, "resid"))
            g.dma_start(out=resid_out[128 * c:128 * (c + 1), :],
                        in_=os_[c % 2][:]).then_inc(P.h("st"), 16)
            P.bump("st", 16)
            for m in range(8):
                if m % 4 != 0:
                    continue
                g.wait_ge(P.h("dve"), DVU(c, m))
                g.wait_ge(P.h("mm"), MM(c, "cross"))
                ins = g.tensor_copy(membf[:, 1024 * m:1024 * (m + 1)],
                                    mem[:, 1024 * m:1024 * (m + 1)])
                ins.then_inc(P.h("pool"), 1)
                P.bump("pool")
    block.gpsimd(psec)
    return P.cnt("st")


def emit_stage_bf16(P, tag, jobs):
    nc, block = P.nc, P.block
    bufs = [P.sb(tag + f"_b{i}", [128, 4096], BF16) for i in range(2)]
    chunks = []
    for (src, dst, rows, cols) in jobs:
        for r0 in range(0, rows, 128):
            chunks.append((src, dst, r0, cols))
    gld0 = P.cnt("gld")
    wst0 = P.cnt("wst")

    def gsec(g):
        for i, (src, dst, r0, cols) in enumerate(chunks):
            if i >= 2:
                g.wait_ge(P.h("wst"), wst0 + 16 * (i - 1))
            g.dma_start(out=bufs[i % 2][:, 0:cols],
                        in_=src[r0:r0 + 128, :]).then_inc(P.h("gld"), 16)
            P.bump("gld", 16)
    block.gpsimd(gsec)

    def ssec(sy):
        for i, (src, dst, r0, cols) in enumerate(chunks):
            sy.wait_ge(P.h("gld"), gld0 + 16 * (i + 1))
            sy.dma_start(out=dst[r0:r0 + 128, :],
                         in_=bufs[i % 2][:, 0:cols]).then_inc(P.h("wst"), 16)
            P.bump("wst", 16)
    block.scalar(ssec)
    return P.cnt("wst")


def emit_l1ffn(P, tag, resid_in, resid_out, w, stats_ready, do_stats=True, reuse=None):
    nc, block = P.nc, P.block
    NT = 8

    if reuse is not None:
        (w1t, w2b, b1c, b2r, scs, sc16, xs, ybf, yfm, hfm, lrows, ocols1,
         spread, prows, d2k, e2f, pst, pbs, pout) = reuse
    else:
        w1t = P.sb(tag + "_w1t", [128, 8 * 4096], BF16)
        w2b = [P.sb(tag + f"_w2b{i}", [128, 8 * 1024], BF16) for i in range(2)]
        b1c = P.sb(tag + "_b1c", [128, 32], F32)
        b2r = P.sb(tag + "_b2r", [1, 1024], BF16)
        scs = P.sb(tag + "_scs", [1, 1], F32)
        sc16 = P.sb(tag + "_sc16", [16, 1], F32)
        xs = [P.sb(tag + f"_x{i}", [128, 1024], F32) for i in range(2)]
        ybf = P.sb(tag + "_y0", [128, 1024], BF16)
        yfm = P.sb(tag + "_yf", [128, 8 * 512], BF16)
        hfm = P.sb(tag + "_h", [128, 32 * 512], BF16)
        lrows = P.sb(tag + "_lr", [1, 8 * 1024], BF16)
        ocols1 = P.sb(tag + "_oc1", [128, 32], BF16)
        spread = P.sb(tag + "_spr", [128, 32 * 16], BF16)
        prows = P.sb(tag + "_pr", [1, 2048], BF16)
        d2k = P.sb(tag + "_d2k", [16, 512], BF16)
        e2f = P.sb(tag + "_e2f", [128, 1024], BF16)
        pst = P.psum(tag + "_pt", [128, 1024], BF16)
        pbs = [P.psum(tag + f"_pb{i}", [128, 512], F32) for i in range(4)]
        pout = P.psum(tag + "_po", [128, 1024], F32)
    os_ = [P.bufs["swp0"], P.bufs["swp1"]]

    ONES = P.bufs["ones"]
    IDENT = P.bufs["ident"]

    stage_done = P.cnt("wst")
    if reuse is None:
        def wloads(g):
            g.wait_ge(P.h("wst"), stage_done)
            for k in range(8):
                g.dma_start(out=w1t[:, 4096 * k:4096 * (k + 1)],
                            in_=w["w1bf"][128 * k:128 * (k + 1), :]).then_inc(P.h("gld"), 16)
                P.bump("gld", 16)
            for nm, buf in [("b1", b1c), ("b2", b2r)]:
                g.dma_start(out=buf[:], in_=w[nm]).then_inc(P.h("gld"), 16)
                P.bump("gld", 16)
            g.dma_start(out=scs[:], in_=w["scale"]).then_inc(P.h("gld"), 16)
            P.bump("gld", 16)
        block.gpsimd(wloads)
    w_ready = P.cnt("gld")

    def load0(s):
        s.dma_start(out=xs[0][:], in_=resid_in[0:128, :]).then_inc(P.h("ld"), 16)
        P.bump("ld", 16)
    block.sync(load0)
    t_ld0 = P.cnt("ld")

    def vnorm0(v):
        v.wait_ge(P.h("ld"), t_ld0)
        v.wait_ge(P.h("dve"), stats_ready)
        ins = v.tensor_scalar(ybf[:], xs[0][:], P.bufs["rst"][:, 0:1],
                              P.bufs["nmr"][:, 0:1], op0=ALU.mult, op1=ALU.add)
        ins.then_inc(P.h("dve"), 1)
        P.bump("dve")
    block.vector(vnorm0)
    t_n0 = P.cnt("dve")

    def tt0(t):
        t.wait_ge(P.h("dve"), t_n0)
        if reuse is None:
            t.wait_ge(P.h("gld"), w_ready)
        for k in range(8):
            ins = t.transpose(pst[:, 128 * k:128 * (k + 1)],
                              ybf[:, 128 * k:128 * (k + 1)], IDENT[:])
        ins.then_inc(P.h("mm"), 1)
        P.bump("mm")
    block.tensor(tt0)
    t_tt0 = P.cnt("mm")

    def vyf0(v):
        v.wait_ge(P.h("mm"), t_tt0)
        ins = v.tensor_copy(
            yfm[:].rearrange("p (k s) -> p k s", s=512)[:, :, 0:128],
            pst[:].rearrange("p (k s) -> p k s", s=128))
        ins.then_inc(P.h("dve"), 1)
        P.bump("dve")
    block.vector(vyf0)
    t_yf0 = P.cnt("dve")

    mmp0 = P.cnt("mm")
    dvp0 = P.cnt("dve")

    def pr_tsec(t):
        t.wait_ge(P.h("dve"), t_yf0)
        for j in range(16):
            if j >= 2:
                t.wait_ge(P.h("dve"), dvp0 + (j - 1))
            half = (j % 2) * 512
            for q in range(4):
                c = 4 * j + q
                tok, dk = 1 + 2 * (c // 8), c % 8
                ins = t.matmul(pout[0:1, half + 128 * q:half + 128 * (q + 1)],
                               yfm[:, 512 * dk + tok:512 * dk + tok + 1],
                               IDENT[:], start=True, stop=True)
            ins.then_inc(P.h("mm"), 1)
            P.bump("mm")
        t.wait_ge(P.h("dve"), dvp0 + 16)
        ins = t.matmul(pout[0:16, 0:1], P.bufs["onesf"][0:1, 0:16], scs[:],
                       start=True, stop=True)
        ins.then_inc(P.h("mm"), 1)
        P.bump("mm")
    block.tensor(pr_tsec)
    t_sc16mm = P.cnt("mm")

    def pr_vsec(v):
        for j in range(16):
            v.wait_ge(P.h("mm"), mmp0 + j + 1)
            half = (j % 2) * 512
            ins = v.tensor_copy(lrows[0:1, 512 * j:512 * (j + 1)],
                                pout[0:1, half:half + 512])
            ins.then_inc(P.h("dve"), 1)
            P.bump("dve")
        v.drain()
        for dk in range(8):
            v.tensor_copy(ocols1[:, 4 * dk:4 * (dk + 1)],
                          yfm[:, 512 * dk + 2:512 * dk + 9:2])
        v.memset(spread[:], 0.0)
        v.drain()
        for q in range(32):
            hi, dk = q // 8, q % 8
            v.tensor_copy(spread[:, 16 * q + hi:16 * q + 16:4],
                          yfm[:, 512 * dk + 10:512 * dk + 17:2])
        v.memset(P.bufs["swp0"][:], 0.0)
        v.drain()
        v.wait_ge(P.h("mm"), t_sc16mm)
        ins = v.tensor_copy(sc16[:], pout[0:16, 0:1])
        v.drain()
        ins.then_inc(P.h("dve"), 1)
        P.bump("dve")
    block.vector(pr_vsec)
    t_pr_done = P.cnt("dve")

    def pr_gz(g):
        g.wait_ge(P.h("dve"), t_pr_done)
        ez = w["e2x"].rearrange("(p a) c -> p (a c)", p=128)
        g.dma_start(out=ez[:], in_=P.bufs["swp0"][:]).then_inc(P.h("gld"), 16)
        P.bump("gld", 16)
    block.gpsimd(pr_gz)
    t_ez0 = P.cnt("gld")

    def pr_sync(s):
        s.wait_ge(P.h("dve"), t_pr_done)
        s.wait_ge(P.h("gld"), t_ez0)
        dst2 = w["e2x"].rearrange("(ro oo) c16 -> ro oo c16", ro=4)
        for ro in range(4):
            s.dma_start(out=dst2[ro:ro + 1, :, 4 * ro:4 * ro + 4],
                        in_=lrows[0:1, (4 + ro) * 1024:(5 + ro) * 1024]
                        .rearrange("o (oo hi) -> o oo hi", hi=4)).then_inc(P.h("ld"), 16)
            P.bump("ld", 16)
        s.wait_ge(P.h("ld"), P.cnt("ld"))
        s.dma_start_transpose(e2f[:], w["e2x"]).then_inc(P.h("ld"), 16)
        P.bump("ld", 16)
    block.sync(pr_sync)
    t_e2f = P.cnt("ld")

    mm0 = P.cnt("mm")
    dve0 = P.cnt("dve")
    act0 = P.cnt("act")
    st0 = P.cnt("st")
    ld1 = P.cnt("ld")
    wst0 = P.cnt("wst")

    NMM = 23
    NDVE = 19
    NACT = 32
    NLD = 8

    def MM(i, k):
        return mm0 + NMM * i + k

    def DVE(i, k):
        return dve0 + NDVE * i + k

    def ACT(i, r, q):
        return act0 + NACT * i + 4 * r + q + 1

    def LD(i, k):
        return ld1 + 16 * (NLD * i + k)

    def WST(i, j):
        return wst0 + 16 * (8 * i + j + 1)

    def loads(s):
        def w2load(i, j):
            q4 = j % 4
            if 8 * i + j >= 2:
                pj = 8 * i + j - 2
                s.wait_ge(P.h("mm"), MM(pj // 8, 16 + pj % 8))
            else:
                s.wait_ge(P.h("wst"), stage_done)
            src = w["w2bf"][1024 * q4:1024 * (q4 + 1), :]
            s.dma_start(out=w2b[j % 2][:].rearrange("p (c d) -> p c d", d=1024),
                        in_=src.rearrange("(c p) d -> p c d", p=128)
                        ).then_inc(P.h("wst"), 16)
            P.bump("wst", 16)

        for i in range(NT):
            for sub in range(4):
                if i == 0 and sub == 0:
                    s.wait_ge(P.h("dve"), t_n0)
                    s.dma_start(out=xs[0][:], in_=resid_in[0:128, :]).then_inc(P.h("ld"), 16)
                    P.bump("ld", 16)
                    continue
                if sub < 2:
                    if i >= 1:
                        s.wait_ge(P.h("dve"), DVE(i - 1, 17 + 2 * sub))
                else:
                    s.wait_ge(P.h("dve"), DVE(i, 2 * (sub - 2) + 1))
                s.dma_start(out=xs[sub % 2][:],
                            in_=resid_in[512 * i + 128 * sub:512 * i + 128 * (sub + 1), :]
                            ).then_inc(P.h("ld"), 16)
                P.bump("ld", 16)
            def xr(sub):
                if sub < 2:
                    s.wait_ge(P.h("dve"), DVE(i, 2 * sub + 5))
                else:
                    s.wait_ge(P.h("dve"), DVE(i, 12 + 2 * (sub - 2) + 1))
                s.dma_start(out=xs[sub % 2][:],
                            in_=resid_in[512 * i + 128 * sub:512 * i + 128 * (sub + 1), :]
                            ).then_inc(P.h("ld"), 16)
                P.bump("ld", 16)

            for j in range(4):
                w2load(i, j)
            xr(0)
            xr(1)
            w2load(i, 4)
            w2load(i, 5)
            xr(2)
            xr(3)
            w2load(i, 6)
            w2load(i, 7)
    block.sync(loads)

    def tsec(t):
        for i in range(NT):
            for sub in range(4):
                if i == 0 and sub == 0:
                    t.wait_ge(P.h("dve"), t_yf0)
                    ins = t.transpose(pst[:, 0:128], ybf[:, 0:128], IDENT[:])
                    ins.then_inc(P.h("mm"), 1)
                    P.bump("mm")
                    continue
                t.wait_ge(P.h("dve"), DVE(i, 2 * sub + 1))
                if sub >= 1:
                    if i == 0 and sub == 1:
                        t.wait_ge(P.h("dve"), t_yf0)
                    else:
                        t.wait_ge(P.h("dve"), DVE(i, 2 * sub))
                else:
                    t.wait_ge(P.h("dve"), DVE(i - 1, 8))
                for k in range(8):
                    ins = t.transpose(pst[:, 128 * k:128 * (k + 1)],
                                      ybf[:, 128 * k:128 * (k + 1)], IDENT[:])
                ins.then_inc(P.h("mm"), 1)
                P.bump("mm")
            t.wait_ge(P.h("dve"), DVE(i, 8))
            if i == 0:
                t.wait_ge(P.h("dve"), t_pr_done)
            if i >= 1:
                t.wait_ge(P.h("dve"), DVE(i - 1, 19))
            for pr in range(2):
                if pr == 1:
                    t.wait_ge(P.h("dve"), DVE(i, 9))
                for r in (0, 1) if pr == 0 else (2, 3):
                    for dk in range(8):
                        ins = t.matmul(pout[0:1, 512 * (r % 2):512 * (r % 2) + 512],
                                       ocols1[:, 4 * dk + r:4 * dk + r + 1],
                                       yfm[:, 512 * dk:512 * (dk + 1)],
                                       start=(dk == 0), stop=(dk == 7))
                ins.then_inc(P.h("mm"), 1)
                P.bump("mm")
            for r in range(8):
                for q in range(4):
                    if i > 0 or r > 0:
                        pr_, pi = (r - 1, i) if r > 0 else (7, i - 1)
                        t.wait_ge(P.h("act"), ACT(pi, pr_, q))
                    if r == 0 and q == 0:
                        t.wait_ge(P.h("dve"), DVE(i, 10))
                        if i >= 1:
                            t.wait_ge(P.h("dve"), DVE(i - 1, 19))
                    mH = 4 * r + q
                    for k in range(8):
                        t.matmul(pbs[q][:], w1t[:, 4096 * k + 128 * mH:4096 * k + 128 * (mH + 1)],
                                 yfm[:, 512 * k:512 * (k + 1)],
                                 start=(k == 0), stop=False)
                    rr = mH // 8
                    ins = t.matmul(pbs[q][:],
                                   lrows[0:1, 1024 * rr + 128 * (mH % 8):
                                         1024 * rr + 128 * (mH % 8) + 128],
                                   prows[0:1, 512 * rr:512 * (rr + 1)],
                                   start=False, stop=True)
                ins.then_inc(P.h("mm"), 1)
                P.bump("mm")
            t.wait_ge(P.h("act"), ACT(i, 7, 3))
            t.wait_ge(P.h("dve"), DVE(i, 10))
            for q in range(32):
                ins = t.matmul(pout[0:16, 0:512], spread[:, 16 * q:16 * (q + 1)],
                               hfm[:, 512 * q:512 * (q + 1)],
                               start=(q == 0), stop=(q == 31))
            ins.then_inc(P.h("mm"), 1)
            P.bump("mm")
            for pair in range(2):
                for q4 in range(4):
                    j = 4 * pair + q4
                    t.wait_ge(P.h("wst"), WST(i, j))
                    if q4 == 0:
                        t.wait_ge(P.h("dve"), DVE(i, 11))
                        if i == 0 and pair == 0:
                            t.wait_ge(P.h("ld"), t_e2f)
                        if pair == 1:
                            t.wait_ge(P.h("dve"), DVE(i, 15))
                        elif i >= 1:
                            t.wait_ge(P.h("dve"), DVE(i - 1, 19))
                    for ms in range(2):
                        msub = 2 * pair + ms
                        for nb in range(2):
                            for k in range(8):
                                hb = 8 * q4 + k
                                ins = t.matmul(pbs[2 * ms + nb][:],
                                               hfm[:, 512 * hb + 128 * msub:512 * hb + 128 * (msub + 1)],
                                               w2b[j % 2][:, 1024 * k + 512 * nb:1024 * k + 512 * (nb + 1)],
                                               start=(q4 == 0 and k == 0), stop=False)
                    if q4 == 3:
                        for ms in range(2):
                            msub = 2 * pair + ms
                            for nb in range(2):
                                t.matmul(pbs[2 * ms + nb][:], ONES[0:1, 0:128],
                                         b2r[0:1, 512 * nb:512 * (nb + 1)],
                                         start=False, stop=False)
                                ins = t.matmul(pbs[2 * ms + nb][:],
                                               d2k[0:16, 128 * msub:128 * (msub + 1)],
                                               e2f[0:16, 512 * nb:512 * (nb + 1)],
                                               start=False, stop=True)
                    ins.then_inc(P.h("mm"), 1)
                    P.bump("mm")
    block.tensor(tsec)

    def asec(s):
        for i in range(NT):
            for r in range(8):
                for q in range(4):
                    s.wait_ge(P.h("mm"), MM(i, 6 + (r + 1)))
                    if i == 0 and r == 0 and q == 0 and reuse is None:
                        s.wait_ge(P.h("gld"), w_ready)
                    mH = 4 * r + q
                    ins = s.activation(hfm[:, 512 * mH:512 * (mH + 1)], pbs[q][:],
                                       AF.Gelu, bias=b1c[:, mH:mH + 1])
                    ins.then_inc(P.h("act"), 1)
                    P.bump("act")
    block.scalar(asec)

    def vsec(v):
        for i in range(NT):
            for sub in range(4):
                if i == 0 and sub == 0:
                    for _ in range(2):
                        ins = v.tensor_copy(ybf[:, 0:1], ybf[:, 0:1])
                        ins.then_inc(P.h("dve"), 1)
                        P.bump("dve")
                    continue
                g = 4 * i + sub
                v.wait_ge(P.h("ld"), LD(i, sub + 1))
                if sub >= 1:
                    v.wait_ge(P.h("mm"), MM(i, sub))
                else:
                    v.wait_ge(P.h("mm"), MM(i - 1, 4))
                ins = v.tensor_scalar(ybf[:], xs[sub % 2][:], P.bufs["rst"][:, g:g + 1],
                                      P.bufs["nmr"][:, g:g + 1], op0=ALU.mult, op1=ALU.add)
                ins.then_inc(P.h("dve"), 1)
                P.bump("dve")
                v.wait_ge(P.h("mm"), MM(i, 1 + sub))
                if i >= 1 and sub == 0:
                    v.wait_ge(P.h("mm"), MM(i - 1, 15))
                ins = v.tensor_copy(
                    yfm[:].rearrange("p (k s) -> p k s", s=512)[:, :, 128 * sub:128 * (sub + 1)],
                    pst[:].rearrange("p (k s) -> p k s", s=128))
                ins.then_inc(P.h("dve"), 1)
                P.bump("dve")
            for pr in range(2):
                v.wait_ge(P.h("mm"), MM(i, 5 + pr))
                ins = v.tensor_scalar(prows[0:1, 1024 * pr:1024 * (pr + 1)],
                                      pout[0:1, 0:1024], scs[0:1, 0:1], None,
                                      op0=ALU.mult)
                ins.then_inc(P.h("dve"), 1)
                P.bump("dve")
            v.wait_ge(P.h("mm"), MM(i, 15))
            ins = v.tensor_scalar(d2k[:], pout[0:16, 0:512], sc16[:, 0:1], None,
                                  op0=ALU.mult)
            ins.then_inc(P.h("dve"), 1)
            P.bump("dve")
            for msub in range(4):
                pair, ms = msub // 2, msub % 2
                v.wait_ge(P.h("mm"), MM(i, 16 + 4 * pair + 3))
                v.wait_ge(P.h("ld"), LD(i, 5 + msub))
                if i >= 1 or msub >= 2:
                    pm = 4 * i + msub - 2
                    v.wait_ge(P.h("st"), st0 + 16 * (pm + 1))
                for nb in range(2):
                    ins = v.scalar_tensor_tensor(
                        os_[msub % 2][:, 512 * nb:512 * (nb + 1)],
                        pbs[2 * ms + nb][:], 0.5, xs[msub % 2][:, 512 * nb:512 * (nb + 1)],
                        op0=ALU.mult, op1=ALU.add)
                    ins.then_inc(P.h("dve"), 1)
                    P.bump("dve")
                if do_stats:
                    _emit_writer_stats(P, v, os_[msub % 2], 4 * i + msub)
    block.vector(vsec)

    def stores(g_):
        for i in range(NT):
            for msub in range(4):
                g_.wait_ge(P.h("dve"), DVE(i, 12 + 2 * msub + 1))
                g_.dma_start(out=resid_out[512 * i + 128 * msub:512 * i + 128 * (msub + 1), :],
                             in_=os_[msub % 2][:]).then_inc(P.h("st"), 16)
                P.bump("st", 16)
    block.gpsimd(stores)
    return P.cnt("st"), (w1t, w2b, b1c, b2r, scs, sc16, xs, ybf, yfm, hfm,
                         lrows, ocols1, spread, prows, d2k, e2f, pst, pbs, pout)


def emit_l1attn_proj(P, tag, resid_in, scr, stats_ready):
    nc, block = P.nc, P.block
    P.open_scope()
    wbufs = {}
    for nm in ["wq", "wk", "wv", "wg"]:
        wbufs[nm] = P.sb(tag + "_" + nm, [128, 8 * 2048], BF16)
    brow = P.sb(tag + "_br", [1, 4 * 2048], BF16)
    xs = [P.sb(tag + f"_x{i}", [128, 1024], F32) for i in range(2)]
    ybfs = [P.sb(tag + f"_y{i}", [128, 1024], BF16) for i in range(2)]
    yfms = [P.sb(tag + f"_yf{i}", [128, 1024], BF16) for i in range(2)]
    qtm = P.sb(tag + "_tmstage", [128, 2048], BF16)
    ktm = qtm
    qfm = P.sb(tag + "_qfm", [128, 2048], BF16)
    kfm = P.sb(tag + "_kfm", [128, 2048], BF16)
    vtm = P.sb(tag + "_vtm", [128, 2048], BF16)
    gtm = P.sb(tag + "_gtm", [128, 2048], BF16)

    pst = P.psum(tag + "_pt", [128, 1024], BF16)
    pstB = P.psum(tag + "_pB", [128, 2048], BF16)
    prod = P.psum(tag + "_pr", [128, 2048], F32)

    ONES = P.bufs["ones"]
    IDENT = P.bufs["ident"]

    def wloads(g):
        for i, nm in enumerate(["wq", "wk", "wv", "wg"]):
            buf = wbufs[nm]
            for k in range(8):
                g.dma_start(out=buf[:, 2048 * k:2048 * (k + 1)],
                            in_=scr[nm][128 * k:128 * (k + 1), :]).then_inc(P.h("gld"), 16)
                P.bump("gld", 16)
        for i, nm in enumerate(["bq", "bk", "bv", "bg"]):
            g.dma_start(out=brow[0:1, 2048 * i:2048 * (i + 1)],
                        in_=scr[nm]).then_inc(P.h("gld"), 16)
            P.bump("gld", 16)
    block.gpsimd(wloads)
    w_ready = P.cnt("gld")

    ld0 = P.cnt("ld")
    mm0 = P.cnt("mm")
    dve0 = P.cnt("dve")
    act0 = P.cnt("act")
    st0 = P.cnt("st")

    def MM(g, k):
        return mm0 + 7 * g + k

    def DVE(g, k):
        return dve0 + 7 * g + k

    def ST(g, k):
        return st0 + 16 * (5 * g + k)

    def loads(s):
        for g in range(NG):
            if g >= 2:
                s.wait_ge(P.h("dve"), DVE(g - 2, 1))
            s.dma_start(out=xs[g % 2][:],
                        in_=resid_in[128 * g:128 * (g + 1), :]).then_inc(P.h("ld"), 16)
            P.bump("ld", 16)
    block.sync(loads)

    def tsec(t):
        for g in range(NG):
            t.wait_ge(P.h("dve"), DVE(g, 1))
            if g == 0:
                t.wait_ge(P.h("gld"), w_ready)
            if g >= 1:
                t.wait_ge(P.h("dve"), DVE(g - 1, 2))
            for k in range(8):
                ins = t.transpose(pst[:, 128 * k:128 * (k + 1)],
                                  ybfs[g % 2][:, 128 * k:128 * (k + 1)], IDENT[:])
            ins.then_inc(P.h("mm"), 1)
            P.bump("mm")
            yf = yfms[g % 2]
            t.wait_ge(P.h("dve"), DVE(g, 2))
            if g >= 1:
                t.wait_ge(P.h("act"), act0 + g)
            for nm, bi, stepi in [("wq", 0, 3)]:
                for nb in range(4):
                    for k in range(8):
                        t.matmul(prod[:, 512 * nb:512 * (nb + 1)], yf[:, 128 * k:128 * (k + 1)],
                                 wbufs[nm][:, 2048 * k + 512 * nb:2048 * k + 512 * (nb + 1)],
                                 start=(k == 0), stop=False)
                    ins = t.matmul(prod[:, 512 * nb:512 * (nb + 1)], ONES[0:1, 0:128],
                                   brow[0:1, 2048 * bi + 512 * nb:2048 * bi + 512 * (nb + 1)],
                                   start=False, stop=True)
            ins.then_inc(P.h("mm"), 1)
            P.bump("mm")
            t.wait_ge(P.h("dve"), DVE(g, 3))
            if g >= 1:
                t.wait_ge(P.h("dve"), DVE(g - 1, 6))
            for k in range(16):
                ins = t.transpose(pstB[:, 128 * k:128 * (k + 1)],
                                  qtm[:, 128 * k:128 * (k + 1)], IDENT[:])
            ins.then_inc(P.h("mm"), 1)
            P.bump("mm")
            t.wait_ge(P.h("dve"), DVE(g, 3))
            for nb in range(4):
                for k in range(8):
                    t.matmul(prod[:, 512 * nb:512 * (nb + 1)], yf[:, 128 * k:128 * (k + 1)],
                             wbufs["wk"][:, 2048 * k + 512 * nb:2048 * k + 512 * (nb + 1)],
                             start=(k == 0), stop=False)
                ins = t.matmul(prod[:, 512 * nb:512 * (nb + 1)], ONES[0:1, 0:128],
                               brow[0:1, 2048 + 512 * nb:2048 + 512 * (nb + 1)],
                               start=False, stop=True)
            ins.then_inc(P.h("mm"), 1)
            P.bump("mm")
            t.wait_ge(P.h("dve"), DVE(g, 5))
            t.wait_ge(P.h("dve"), DVE(g, 4))
            for k in range(16):
                ins = t.transpose(pstB[:, 128 * k:128 * (k + 1)],
                                  ktm[:, 128 * k:128 * (k + 1)], IDENT[:])
            ins.then_inc(P.h("mm"), 1)
            P.bump("mm")
            t.wait_ge(P.h("dve"), DVE(g, 5))
            for nb in range(4):
                for k in range(8):
                    t.matmul(prod[:, 512 * nb:512 * (nb + 1)], yf[:, 128 * k:128 * (k + 1)],
                             wbufs["wv"][:, 2048 * k + 512 * nb:2048 * k + 512 * (nb + 1)],
                             start=(k == 0), stop=False)
                ins = t.matmul(prod[:, 512 * nb:512 * (nb + 1)], ONES[0:1, 0:128],
                               brow[0:1, 4096 + 512 * nb:4096 + 512 * (nb + 1)],
                               start=False, stop=True)
            ins.then_inc(P.h("mm"), 1)
            P.bump("mm")
            t.wait_ge(P.h("dve"), DVE(g, 7))
            for nb in range(4):
                for k in range(8):
                    t.matmul(prod[:, 512 * nb:512 * (nb + 1)], yf[:, 128 * k:128 * (k + 1)],
                             wbufs["wg"][:, 2048 * k + 512 * nb:2048 * k + 512 * (nb + 1)],
                             start=(k == 0), stop=False)
                ins = t.matmul(prod[:, 512 * nb:512 * (nb + 1)], ONES[0:1, 0:128],
                               brow[0:1, 6144 + 512 * nb:6144 + 512 * (nb + 1)],
                               start=False, stop=True)
            ins.then_inc(P.h("mm"), 1)
            P.bump("mm")
    block.tensor(tsec)

    def vsec(v):
        for g in range(NG):
            v.wait_ge(P.h("ld"), ld0 + 16 * (g + 1))
            if g == 0:
                v.wait_ge(P.h("dve"), stats_ready)
            if g >= 2:
                v.wait_ge(P.h("mm"), MM(g - 2, 1))
            ins = v.tensor_scalar(ybfs[g % 2][:], xs[g % 2][:], P.bufs["rst"][:, g:g + 1],
                                  P.bufs["nmr"][:, g:g + 1], op0=ALU.mult, op1=ALU.add)
            ins.then_inc(P.h("dve"), 1)
            P.bump("dve")
            v.wait_ge(P.h("mm"), MM(g, 1))
            if g >= 2:
                v.wait_ge(P.h("mm"), MM(g - 2, 7))
            ins = v.tensor_copy(yfms[g % 2][:], pst[:])
            ins.then_inc(P.h("dve"), 1)
            P.bump("dve")
            v.wait_ge(P.h("mm"), MM(g, 2))
            if g >= 1:
                v.wait_ge(P.h("mm"), MM(g - 1, 5))
                v.wait_ge(P.h("st"), ST(g - 1, 2))
            ins = v.tensor_copy(qtm[:], prod[:])
            ins.then_inc(P.h("dve"), 1)
            P.bump("dve")
            v.wait_ge(P.h("mm"), MM(g, 3))
            if g >= 1:
                v.wait_ge(P.h("st"), ST(g - 1, 1))
            ins = v.tensor_copy(qfm[:], pstB[:])
            ins.then_inc(P.h("dve"), 1)
            P.bump("dve")
            v.wait_ge(P.h("mm"), MM(g, 4))
            v.wait_ge(P.h("mm"), MM(g, 3))
            ins = v.tensor_copy(ktm[:], prod[:])
            ins.then_inc(P.h("dve"), 1)
            P.bump("dve")
            v.wait_ge(P.h("mm"), MM(g, 5))
            if g >= 1:
                v.wait_ge(P.h("st"), ST(g - 1, 3))
            ins = v.tensor_copy(kfm[:], pstB[:])
            ins.then_inc(P.h("dve"), 1)
            P.bump("dve")
            v.wait_ge(P.h("mm"), MM(g, 6))
            if g >= 1:
                v.wait_ge(P.h("st"), ST(g - 1, 4))
            ins = v.tensor_copy(vtm[:], prod[:])
            ins.then_inc(P.h("dve"), 1)
            P.bump("dve")
    block.vector(vsec)

    def asec(s):
        for g in range(NG):
            s.wait_ge(P.h("mm"), MM(g, 7))
            if g >= 1:
                s.wait_ge(P.h("st"), ST(g - 1, 5))
            for nb in range(4):
                ins = s.activation(gtm[:, 512 * nb:512 * (nb + 1)],
                                   prod[:, 512 * nb:512 * (nb + 1)], AF.Sigmoid)
            ins.then_inc(P.h("act"), 1)
            P.bump("act")
    block.scalar(asec)

    def stores(g_):
        for g in range(NG):
            g_.wait_ge(P.h("dve"), DVE(g, 4))
            g_.dma_start(out=scr["qd3"][g].rearrange("bk p x -> p bk x"),
                         in_=qfm[:].rearrange("p (bk x) -> p bk x", x=128)).then_inc(P.h("st"), 16)
            P.bump("st", 16)
            g_.wait_ge(P.h("dve"), DVE(g, 5))
            g_.dma_start(out=scr["ktd"][128 * g:128 * (g + 1), :],
                         in_=ktm[:]).then_inc(P.h("st"), 16)
            P.bump("st", 16)
            g_.wait_ge(P.h("dve"), DVE(g, 6))
            g_.dma_start(out=scr["kd3"][g].rearrange("bk p x -> p bk x"),
                         in_=kfm[:].rearrange("p (bk x) -> p bk x", x=128)).then_inc(P.h("st"), 16)
            P.bump("st", 16)
            g_.wait_ge(P.h("dve"), DVE(g, 7))
            g_.dma_start(out=scr["vd"][128 * g:128 * (g + 1), :],
                         in_=vtm[:]).then_inc(P.h("st"), 16)
            P.bump("st", 16)
            g_.wait_ge(P.h("act"), act0 + g + 1)
            g_.dma_start(out=scr["gd"][128 * g:128 * (g + 1), :],
                         in_=gtm[:]).then_inc(P.h("st"), 16)
            P.bump("st", 16)
    block.gpsimd(stores)
    st_done = P.cnt("st")
    P.close_scope()
    emit_phase_barrier(P)
    return st_done


def emit_l1attn_pass(P, tag, half, pass_a, resid_in, resid_out, scr, rings, nch=NG, do_stats=False):
    nc, block = P.nc, P.block
    G = 4
    P.open_scope()
    mem = P.sb(tag + "_mem", [128, 16 * 1024], F32)
    membf = P.sb(tag + "_mbf", [128, 16 * 1024], BF16)
    wob = P.sb(tag + "_wo", [128, 8 * 1024], BF16)
    borow = P.sb(tag + "_bo", [1, 1024], BF16)
    qs = [P.sb(tag + f"_q{i}", [128, 2048], BF16) for i in range(2)]
    kts = [P.sb(tag + f"_kt{i}", [128, 2048], BF16) for i in range(4)]
    kfs = [P.sb(tag + f"_kf{i}", [128, 2048], BF16) for i in range(4)] if pass_a else None
    vs = [P.sb(tag + f"_v{i}", [128, 1024], BF16) for i in range(4)]
    gs = [P.sb(tag + f"_g{i}", [128, 1024], BF16) for i in range(2)]
    smx = P.sb(tag + "_smx", [128, 128], F32)
    rmax = P.sb(tag + "_rmx", [128, 4], F32)
    pex = P.sb(tag + "_pex", [128, 128], F32)
    abf = P.sb(tag + "_abf", [128, 128], BF16)
    otm = P.sb(tag + "_otm", [128, 1024], BF16)
    ofm = P.sb(tag + "_ofm", [128, 1024], BF16)
    os_ = [P.bufs["swp0"], P.bufs["swp1"]]
    xs = os_

    pout = P.psum(tag + "_po", [128, 1024], F32)
    pus = [P.psum(tag + f"_pu{i}", [128, 1024], F32) for i in range(2)]
    psc = P.psum(tag + "_ps", [128, 512], F32)
    pstT = P.psum(tag + "_pT", [128, 1024], BF16)

    ONES = P.bufs["ones"]
    IDENT = P.bufs["ident"]
    atr = rings["at"]
    a2r = rings["a2"]

    def a2slot(c, k):
        j = c % 4
        base = [0, 0, 1, 3][j]
        return 6 * (c // 4) + base + (k - 1)

    def wloads(g):
        for k in range(8):
            g.dma_start(out=wob[:, 1024 * k:1024 * (k + 1)],
                        in_=scr["wo"][1024 * half + 128 * k:1024 * half + 128 * (k + 1), :]
                        ).then_inc(P.h("gld"), 16)
            P.bump("gld", 16)
        g.dma_start(out=borow[:], in_=scr["bo"]).then_inc(P.h("gld"), 16)
        P.bump("gld", 16)
    block.gpsimd(wloads)
    w_ready = P.cnt("gld")

    def vpre(v):
        v.memset(mem[:], 0.0)
        ins = v.memset(membf[:], 0.0)
        ins.then_inc(P.h("dve"), 1)
        P.bump("dve")
    block.vector(vpre)
    mem_init = P.cnt("dve")

    nld_c = 6 if pass_a else 5

    def mm_order(c):
        j = c % 4
        if pass_a:
            o = ["sc"] + [f"a2_{k}" for k in range(1, j + 1)] + ["at", "og", "tr", "fin"]
        else:
            o = ["og", "tr", "fin"]
        if j == 3:
            o += [f"u{m}" for m in range(16)]
        return o

    def dve_order(c):
        j = c % 4
        if pass_a:
            o = (["s1", "s2", "s3", "s4", "s5"]
                 + [f"s6_{k}" for k in range(1, j + 1)] + ["s7", "s8", "s9"])
        else:
            o = ["s7", "s8", "s9"]
        if j == 3:
            o += [f"u{m}" for m in range(16)]
        return o

    nmm_c = [len(mm_order(c)) for c in range(nch)]
    ndve_c = [len(dve_order(c)) for c in range(nch)]
    nact_c = [1 if pass_a else 0] * nch
    npool_c = [16 if c % 4 == 3 else 0 for c in range(nch)]

    ld0 = P.cnt("ld")
    mm0 = P.cnt("mm")
    dve0 = P.cnt("dve")
    act0 = P.cnt("act")
    pool0 = P.cnt("pool")
    st0 = P.cnt("st")
    mmb = [mm0] * nch
    dveb = [dve0] * nch
    actb = [act0] * nch
    poolb = [pool0] * nch
    for c in range(1, nch):
        mmb[c] = mmb[c - 1] + nmm_c[c - 1]
        dveb[c] = dveb[c - 1] + ndve_c[c - 1]
        actb[c] = actb[c - 1] + nact_c[c - 1]
        poolb[c] = poolb[c - 1] + npool_c[c - 1]

    def mm_slot(c, nm):
        return mmb[c] + mm_order(c).index(nm) + 1

    def dve_slot(c, nm):
        return dveb[c] + dve_order(c).index(nm) + 1

    def ue_chunk(c):
        return 4 * (c // 4) + 3

    sld0 = P.cnt("sld")

    def loads(s):
        for c in range(nch):
            if c >= 2:
                s.wait_ge(P.h("st"), st0 + 16 * (c - 1))
                if do_stats:
                    s.wait_ge(P.h("sld"), sld0 + (c - 1))
            s.dma_start(out=xs[c % 2][:],
                        in_=resid_in[128 * c:128 * (c + 1), :]).then_inc(P.h("ld"), 16)
            P.bump("ld", 16)
            if c >= 2:
                s.wait_ge(P.h("mm"), mm_slot(c - 2, "og"))
            s.dma_start(out=qs[c % 2][:].rearrange("p (bk x) -> p bk x", x=128),
                        in_=scr["qd3"][c].rearrange("bk p x -> p bk x")).then_inc(P.h("ld"), 16)
            P.bump("ld", 16)
            if c >= 4:
                s.wait_ge(P.h("mm"), mm_slot(ue_chunk(c - 4), "u15"))
            s.dma_start(out=kts[c % 4][:],
                        in_=scr["ktd"][128 * c:128 * (c + 1), :]).then_inc(P.h("ld"), 16)
            P.bump("ld", 16)
            if c >= 4:
                s.wait_ge(P.h("mm"), mm_slot(ue_chunk(c - 4), "u15"))
            s.dma_start(out=vs[c % 4][:],
                        in_=scr["vd"][128 * c:128 * (c + 1),
                                      1024 * half:1024 * (half + 1)]).then_inc(P.h("ld"), 16)
            P.bump("ld", 16)
            if c >= 2:
                s.wait_ge(P.h("dve"), dve_slot(c - 2, "s7"))
            s.dma_start(out=gs[c % 2][:],
                        in_=scr["gd"][128 * c:128 * (c + 1),
                                      1024 * half:1024 * (half + 1)]).then_inc(P.h("ld"), 16)
            P.bump("ld", 16)
            if pass_a:
                if c >= 4:
                    s.wait_ge(P.h("mm"), mm_slot(c - 1, "og"))
                s.dma_start(out=kfs[c % 4][:].rearrange("p (bk x) -> p bk x", x=128),
                            in_=scr["kd3"][c].rearrange("bk p x -> p bk x")).then_inc(P.h("ld"), 16)
                P.bump("ld", 16)
    block.sync(loads)

    def LD(c, k):
        return ld0 + 16 * (nld_c * c + k)

    SCL = 1.0 / float(np.sqrt(2048.0))

    def tsec(t):
        for c in range(nch):
            j = c % 4
            if pass_a:
                t.wait_ge(P.h("ld"), LD(c, 6))
                t.wait_ge(P.h("ld"), LD(c, 2))
                if c == 0:
                    t.wait_ge(P.h("gld"), w_ready)
                if c >= 1:
                    jp = (c - 1) % 4
                    last = f"s6_{jp}" if jp >= 1 else "s1"
                    t.wait_ge(P.h("dve"), dve_slot(c - 1, last))
                for bk in range(16):
                    ins = t.matmul(psc[:, 0:128], qs[c % 2][:, 128 * bk:128 * (bk + 1)],
                                   kfs[c % 4][:, 128 * bk:128 * (bk + 1)],
                                   start=(bk == 0), stop=(bk == 15))
                ins.then_inc(P.h("mm"), 1)
                P.bump("mm")
                for k in range(1, j + 1):
                    t.wait_ge(P.h("dve"), dve_slot(c, "s1") if k == 1 else 0) if False else None
                    if k == 1:
                        t.wait_ge(P.h("dve"), dve_slot(c, "s1"))
                    for bk in range(16):
                        ins = t.matmul(psc[:, 128 * k:128 * (k + 1)],
                                       kfs[(c - k) % 4][:, 128 * bk:128 * (bk + 1)],
                                       qs[c % 2][:, 128 * bk:128 * (bk + 1)],
                                       start=(bk == 0), stop=(bk == 15))
                    ins.then_inc(P.h("mm"), 1)
                    P.bump("mm")
                t.wait_ge(P.h("dve"), dve_slot(c, "s4"))
                if c >= 1:
                    t.wait_ge(P.h("dve"), dve_slot(c - 1, "s8"))
                ins = t.transpose(pstT[:, 0:128], abf[:], IDENT[:])
                ins.then_inc(P.h("mm"), 1)
                P.bump("mm")
            t.wait_ge(P.h("ld"), LD(c, 4))
            if not pass_a:
                if c == 0:
                    t.wait_ge(P.h("gld"), w_ready)
                t.wait_ge(P.h("ld"), LD(c, 2))
            if c == 0:
                t.wait_ge(P.h("dve"), mem_init)
            if c >= 1:
                t.wait_ge(P.h("dve"), dve_slot(c - 1, "s9"))
            if c >= 4:
                t.wait_ge(P.h("pool"), pool0 + 16 * (c // 4))
            for nb in range(2):
                for bk in range(16):
                    t.matmul(pout[:, 512 * nb:512 * (nb + 1)],
                             qs[c % 2][:, 128 * bk:128 * (bk + 1)],
                             membf[:, 1024 * bk + 512 * nb:1024 * bk + 512 * (nb + 1)],
                             start=(bk == 0), stop=False)
            if pass_a:
                t.wait_ge(P.h("dve"), dve_slot(c, "s5"))
            for nb in range(2):
                ins = t.matmul(pout[:, 512 * nb:512 * (nb + 1)], atr[:, 128 * c:128 * (c + 1)],
                               vs[c % 4][:, 512 * nb:512 * (nb + 1)], start=False,
                               stop=(j == 0))
            for k in range(1, j + 1):
                if pass_a:
                    t.wait_ge(P.h("dve"), dve_slot(c, f"s6_{k}"))
                for nb in range(2):
                    ins = t.matmul(pout[:, 512 * nb:512 * (nb + 1)],
                                   a2r[:, 128 * a2slot(c, k):128 * (a2slot(c, k) + 1)],
                                   vs[(c - k) % 4][:, 512 * nb:512 * (nb + 1)],
                                   start=False, stop=(k == j))
            ins.then_inc(P.h("mm"), 1)
            P.bump("mm")
            t.wait_ge(P.h("dve"), dve_slot(c, "s7"))
            if c >= 1:
                t.wait_ge(P.h("dve"), dve_slot(c - 1, "s8"))
            for k in range(8):
                ins = t.transpose(pstT[:, 128 * k:128 * (k + 1)],
                                  otm[:, 128 * k:128 * (k + 1)], IDENT[:])
            ins.then_inc(P.h("mm"), 1)
            P.bump("mm")
            t.wait_ge(P.h("dve"), dve_slot(c, "s8"))
            for nb in range(2):
                for k in range(8):
                    ins = t.matmul(pout[:, 512 * nb:512 * (nb + 1)], ofm[:, 128 * k:128 * (k + 1)],
                                   wob[:, 1024 * k + 512 * nb:1024 * k + 512 * (nb + 1)],
                                   start=(k == 0), stop=(not pass_a and k == 7))
                if pass_a:
                    ins = t.matmul(pout[:, 512 * nb:512 * (nb + 1)], ONES[0:1, 0:128],
                                   borow[0:1, 512 * nb:512 * (nb + 1)],
                                   start=False, stop=True)
            ins.then_inc(P.h("mm"), 1)
            P.bump("mm")
            if j == 3:
                t.wait_ge(P.h("ld"), LD(c, 3))
                for m in range(16):
                    pu = pus[m % 2]
                    if m >= 2:
                        t.wait_ge(P.h("dve"), dve_slot(c, f"u{m - 2}"))
                    for cc in range(c - 3, c + 1):
                        for nb in range(2):
                            ins = t.matmul(pu[:, 512 * nb:512 * (nb + 1)],
                                           kts[cc % 4][:, 128 * m:128 * (m + 1)],
                                           vs[cc % 4][:, 512 * nb:512 * (nb + 1)],
                                           start=(cc == c - 3), stop=(cc == c))
                    ins.then_inc(P.h("mm"), 1)
                    P.bump("mm")
    block.tensor(tsec)

    def asec(s):
        for c in range(nch):
            j = c % 4
            if pass_a:
                s.wait_ge(P.h("dve"), dve_slot(c, "s2"))
                ins = s.activation(pex[:], smx[:], AF.Exp, bias=rmax[:, 0:1],
                                   accum_out=rmax[:, 1:2])
                ins.then_inc(P.h("act"), 1)
                P.bump("act")
            if j == 3:
                for m in range(16):
                    if m % 4 == 0:
                        continue
                    s.wait_ge(P.h("dve"), dve_slot(c, f"u{m}"))
                    s.wait_ge(P.h("mm"), mm_slot(c, "og"))
                    ins = s.copy(membf[:, 1024 * m:1024 * (m + 1)],
                                 mem[:, 1024 * m:1024 * (m + 1)])
                    ins.then_inc(P.h("pool"), 1)
                    P.bump("pool")
    block.scalar(asec)

    def vsec(v):
        for c in range(nch):
            j = c % 4
            if pass_a:
                v.wait_ge(P.h("mm"), mm_slot(c, "sc"))
                if c >= 1:
                    v.wait_ge(P.h("act"), actb[c - 1] + 1)
                ins = v.scalar_tensor_tensor(smx[:], psc[:, 0:128], SCL, P.bufs["cmask"][:],
                                             op0=ALU.mult, op1=ALU.add)
                ins.then_inc(P.h("dve"), 1)
                P.bump("dve")
                ins = v.reduce_max(rmax[:, 0:1], smx[:], axis=AX.X, negate=True)
                v.drain()
                ins.then_inc(P.h("dve"), 1)
                P.bump("dve")
                v.wait_ge(P.h("act"), actb[c] + 1)
                ins = v.reciprocal(rmax[:, 2:3], rmax[:, 1:2])
                v.drain()
                ins.then_inc(P.h("dve"), 1)
                P.bump("dve")
                if c >= 1:
                    v.wait_ge(P.h("mm"), mm_slot(c - 1, "at"))
                ins = v.tensor_scalar(abf[:], pex[:], rmax[:, 2:3], None, op0=ALU.mult)
                ins.then_inc(P.h("dve"), 1)
                P.bump("dve")
                v.wait_ge(P.h("mm"), mm_slot(c, "at"))
                ins = v.tensor_copy(atr[:, 128 * c:128 * (c + 1)], pstT[:, 0:128])
                ins.then_inc(P.h("dve"), 1)
                P.bump("dve")
                for k in range(1, j + 1):
                    v.wait_ge(P.h("mm"), mm_slot(c, f"a2_{k}"))
                    ins = v.tensor_copy(a2r[:, 128 * a2slot(c, k):128 * (a2slot(c, k) + 1)],
                                        psc[:, 128 * k:128 * (k + 1)])
                    ins.then_inc(P.h("dve"), 1)
                    P.bump("dve")
            v.wait_ge(P.h("mm"), mm_slot(c, "og"))
            v.wait_ge(P.h("ld"), LD(c, 5))
            if c >= 1:
                v.wait_ge(P.h("mm"), mm_slot(c - 1, "tr"))
            ins = v.tensor_tensor(otm[:], pout[:], gs[c % 2][:], op=ALU.mult)
            ins.then_inc(P.h("dve"), 1)
            P.bump("dve")
            v.wait_ge(P.h("mm"), mm_slot(c, "tr"))
            if c >= 1:
                v.wait_ge(P.h("mm"), mm_slot(c - 1, "fin"))
            ins = v.tensor_copy(ofm[:], pstT[:])
            ins.then_inc(P.h("dve"), 1)
            P.bump("dve")
            v.wait_ge(P.h("mm"), mm_slot(c, "fin"))
            v.wait_ge(P.h("ld"), LD(c, 1))
            if c >= 2:
                v.wait_ge(P.h("st"), st0 + 16 * (c - 1))
            ins = v.tensor_tensor(os_[c % 2][:], xs[c % 2][:], pout[:], op=ALU.add)
            ins.then_inc(P.h("dve"), 1)
            P.bump("dve")
            if (not pass_a) and do_stats:
                _emit_writer_stats(P, v, os_[c % 2], c)
            if j == 3:
                for m in range(16):
                    v.wait_ge(P.h("mm"), mm_slot(c, f"u{m}"))
                    if c >= 4:
                        v.wait_ge(P.h("pool"), pool0 + 16 * (c // 4))
                    ins = v.tensor_tensor(mem[:, 1024 * m:1024 * (m + 1)],
                                          mem[:, 1024 * m:1024 * (m + 1)],
                                          pus[m % 2][:], op=ALU.add)
                    ins.then_inc(P.h("dve"), 1)
                    P.bump("dve")
    block.vector(vsec)

    def psec(g):
        for c in range(nch):
            g.wait_ge(P.h("dve"), dve_slot(c, "s9"))
            g.dma_start(out=resid_out[128 * c:128 * (c + 1), :],
                        in_=os_[c % 2][:]).then_inc(P.h("st"), 16)
            P.bump("st", 16)
            if c % 4 == 3:
                for m in range(16):
                    if m % 4 != 0:
                        continue
                    g.wait_ge(P.h("dve"), dve_slot(c, f"u{m}"))
                    g.wait_ge(P.h("mm"), mm_slot(c, "og"))
                    ins = g.tensor_copy(membf[:, 1024 * m:1024 * (m + 1)],
                                        mem[:, 1024 * m:1024 * (m + 1)])
                    ins.then_inc(P.h("pool"), 1)
                    P.bump("pool")
    block.gpsimd(psec)
    st_done = P.cnt("st")
    P.close_scope()
    emit_phase_barrier(P)
    return st_done

def prep_l0ffn_weights(W_in, b_in, W_out):
    perm = tree_perm()
    winT = np.zeros((1024, 2048), np.float32)
    winT[:, 0:2040] = np.asarray(W_in)[perm].T
    b = np.zeros((1, 2048), np.float32)
    b[0, 0:2040] = np.asarray(b_in)[perm]
    woutT = np.zeros((2048, 1024), np.float32)
    woutT[0:2040, :] = np.asarray(W_out)[:, perm].T
    return {"winT": winT, "b_in": b, "woutT": woutT}


def build_p_l0ffn():
    P = Prog()
    nc = P.nc
    xin = nc.dram_tensor("xin", [SEQ, DM], F32, kind="ExternalInput").ap()
    xout = nc.dram_tensor("xout", [SEQ, DM], F32, kind="ExternalOutput").ap()
    w = {"winT": nc.dram_tensor("winT", [1024, 2048], F32, kind="ExternalInput").ap(),
         "b_in": nc.dram_tensor("b_in", [1, 2048], F32, kind="ExternalInput").ap(),
         "woutT": nc.dram_tensor("woutT", [2048, 1024], F32, kind="ExternalInput").ap()}
    consts = {k: nc.dram_tensor(k, list(v.shape),
                                BF16 if v.dtype == BF16NP else F32,
                                kind="ExternalInput").ap()
              for k, v in host_consts().items()}
    alloc_common(P)
    P.block = FakeBlock(P) if getattr(Prog, 'use_fake', False) else P.ctx.enter_context(nc.Block())
    pre_dve, _ = emit_preamble(P, consts, want_cvecs=[1e-5, -0.5])
    stats_ready = emit_stats_sweep(P, xin, pre_dve)
    st_done = emit_l0ffn(P, "f0", xin, xout, w, stats_ready)

    def fin(g):
        g.wait_ge(P.h("st"), st_done)
    P.block.gpsimd(fin)
    P.finish()
    return P


def build_p_l0attn():
    P = Prog()
    nc = P.nc
    xin = nc.dram_tensor("xin", [SEQ, DM], F32, kind="ExternalInput").ap()
    xout = nc.dram_tensor("xout", [SEQ, DM], F32, kind="ExternalOutput").ap()
    w = {nm: nc.dram_tensor(nm, [1024, 1024], F32, kind="ExternalInput").ap()
         for nm in ["wq", "wk", "wv", "wo"]}
    for nm in ["bq", "bk", "bv", "bo"]:
        w[nm] = nc.dram_tensor(nm, [1, 1024], F32, kind="ExternalInput").ap()
    dp = nc.dram_tensor("decay", [1, 1], F32, kind="ExternalInput").ap()
    consts = {k: nc.dram_tensor(k, list(v.shape),
                                BF16 if v.dtype == BF16NP else F32,
                                kind="ExternalInput").ap()
              for k, v in host_consts().items()}
    alloc_common(P)
    P.block = FakeBlock(P) if getattr(Prog, 'use_fake', False) else P.ctx.enter_context(nc.Block())
    pre_dve, _ = emit_preamble(P, consts, want_cvecs=[1e-5, -0.5, 128.0])
    dec_ready, dv = emit_decay_prep(P, "dc", dp, pre_dve, consts)
    stats_ready = emit_stats_sweep(P, xin, pre_dve)
    st_done = emit_l0attn(P, "a0", xin, xout, w, stats_ready, dec_ready, dv)

    def fin(g):
        g.wait_ge(P.h("st"), st_done)
    P.block.gpsimd(fin)
    P.finish()
    return P


def prep_l1ffn_weights(W1, b1, W2, b2, scale):
    return {"w1T": np.ascontiguousarray(np.asarray(W1).T),
            "w2T": np.ascontiguousarray(np.asarray(W2).T),
            "b1": np.ascontiguousarray(np.asarray(b1).reshape(32, 128).T),
            "b2": np.asarray(b2).reshape(1, 1024),
            "scale": np.asarray(scale, np.float32).reshape(1, 1)}


def build_p_l1ffn():
    P = Prog()
    nc = P.nc
    xin = nc.dram_tensor("xin", [SEQ, DM], F32, kind="ExternalInput").ap()
    xout = nc.dram_tensor("xout", [SEQ, DM], F32, kind="ExternalOutput").ap()
    w = {"w1T": nc.dram_tensor("w1T", [1024, 4096], F32, kind="ExternalInput").ap(),
         "w2T": nc.dram_tensor("w2T", [4096, 1024], F32, kind="ExternalInput").ap(),
         "w1bf": nc.dram_tensor("w1bf", [1024, 4096], BF16, kind="Internal").ap(),
         "w2bf": nc.dram_tensor("w2bf", [4096, 1024], BF16, kind="Internal").ap(),
         "b1": nc.dram_tensor("b1", [128, 32], F32, kind="ExternalInput").ap(),
         "b2": nc.dram_tensor("b2", [1, 1024], F32, kind="ExternalInput").ap(),
         "scale": nc.dram_tensor("scale", [1, 1], F32, kind="ExternalInput").ap(),
         "e2x": nc.dram_tensor("e2x", [1024, 128], BF16, kind="Internal").ap()}
    consts = {k: nc.dram_tensor(k, list(v.shape),
                                BF16 if v.dtype == BF16NP else F32,
                                kind="ExternalInput").ap()
              for k, v in host_consts().items()}
    alloc_common(P)
    P.block = FakeBlock(P) if getattr(Prog, 'use_fake', False) else P.ctx.enter_context(nc.Block())
    pre_dve, _ = emit_preamble(P, consts, want_cvecs=[1e-5, -0.5])
    emit_stage_bf16(P, "stg", [(w["w1T"], w["w1bf"], 1024, 4096),
                               (w["w2T"], w["w2bf"], 4096, 1024)])
    stats_ready = emit_stats_sweep(P, xin, pre_dve)
    st_done, _ = emit_l1ffn(P, "f1", xin, xout, w, stats_ready, do_stats=False)

    def fin(g):
        g.wait_ge(P.h("st"), st_done)
    P.block.gpsimd(fin)
    P.finish()
    return P


def build_p_l1attn():
    P = Prog()
    nc = P.nc
    xin = nc.dram_tensor("xin", [SEQ, DM], F32, kind="ExternalInput").ap()
    xout = nc.dram_tensor("xout", [SEQ, DM], F32, kind="ExternalOutput").ap()
    scr = {}
    for nm in ["wq", "wk", "wv", "wg"]:
        scr[nm] = nc.dram_tensor(nm, [1024, 2048], F32, kind="ExternalInput").ap()
    for nm in ["bq", "bk", "bv", "bg"]:
        scr[nm] = nc.dram_tensor(nm, [1, 2048], F32, kind="ExternalInput").ap()
    scr["wo"] = nc.dram_tensor("wo", [2048, 1024], F32, kind="ExternalInput").ap()
    scr["bo"] = nc.dram_tensor("bo", [1, 1024], F32, kind="ExternalInput").ap()
    for nm in ["qd3", "kd3"]:
        scr[nm] = nc.dram_tensor(nm, [32, 16, 128, 128], BF16, kind="Internal").ap()
    for nm in ["ktd", "vd", "gd"]:
        scr[nm] = nc.dram_tensor(nm, [4096, 2048], BF16, kind="Internal").ap()
    xmid = nc.dram_tensor("xmid", [SEQ, DM], F32, kind="Internal").ap()
    consts = {k: nc.dram_tensor(k, list(v.shape),
                                BF16 if v.dtype == BF16NP else F32,
                                kind="ExternalInput").ap()
              for k, v in host_consts().items()}
    alloc_common(P)
    P.block = FakeBlock(P) if getattr(Prog, 'use_fake', False) else P.ctx.enter_context(nc.Block())
    pre_dve, _ = emit_preamble(P, consts, want_cvecs=[1e-5, -0.5])
    stats_ready = emit_stats_sweep(P, xin, pre_dve)
    emit_l1attn_proj(P, "pa", xin, scr, stats_ready)
    rings = {"at": P.sb("r_at", [128, 32 * 128], BF16),
             "a2": P.sb("r_a2", [128, 48 * 128], BF16)}
    emit_l1attn_pass(P, "pb", 0, True, xin, xmid, scr, rings)
    st_done = emit_l1attn_pass(P, "pc", 1, False, xmid, xout, scr, rings)

    def fin(g):
        g.wait_ge(P.h("st"), st_done)
    P.block.gpsimd(fin)
    P.finish()
    return P


def _run_single(P, in_map, n_cores=1):
    res = run_bass_kernel_spmd(P.nc, [in_map] * n_cores, core_ids=list(range(n_cores)))
    return res


def build_mega():
    P = Prog()
    nc = P.nc
    xin = nc.dram_tensor("xin", [SEQ, DM], F32, kind="ExternalInput").ap()
    out = nc.dram_tensor("out", [SEQ, DM], F32, kind="ExternalOutput").ap()
    rA = nc.dram_tensor("rA", [SEQ, DM], F32, kind="Internal").ap()
    rB = nc.dram_tensor("rB", [SEQ, DM], F32, kind="Internal").ap()

    w_f0 = {"winT": nc.dram_tensor("d_f0_winT", [1024, 2048], F32, kind="ExternalInput").ap(),
            "b_in": nc.dram_tensor("d_f0_bin", [1, 2048], F32, kind="ExternalInput").ap(),
            "woutT": nc.dram_tensor("d_f0_woutT", [2048, 1024], F32, kind="ExternalInput").ap()}
    w_a0 = {}
    for nm in ["wq", "wk", "wv", "wo"]:
        w_a0[nm] = nc.dram_tensor("d_a0_" + nm, [1024, 1024], F32, kind="ExternalInput").ap()
    for nm in ["bq", "bk", "bv", "bo"]:
        w_a0[nm] = nc.dram_tensor("d_a0_" + nm, [1, 1024], F32, kind="ExternalInput").ap()
    decay_ap = nc.dram_tensor("d_a0_decay", [1, 1], F32, kind="ExternalInput").ap()
    w_f1 = {"w1T": nc.dram_tensor("d_f1_w1T", [1024, 4096], F32, kind="ExternalInput").ap(),
            "w2T": nc.dram_tensor("d_f1_w2T", [4096, 1024], F32, kind="ExternalInput").ap(),
            "w1bf": nc.dram_tensor("d_f1_w1bf", [1024, 4096], BF16, kind="Internal").ap(),
            "w2bf": nc.dram_tensor("d_f1_w2bf", [4096, 1024], BF16, kind="Internal").ap(),
            "b1": nc.dram_tensor("d_f1_b1", [128, 32], F32, kind="ExternalInput").ap(),
            "b2": nc.dram_tensor("d_f1_b2", [1, 1024], F32, kind="ExternalInput").ap(),
            "scale": nc.dram_tensor("d_f1_scale", [1, 1], F32, kind="ExternalInput").ap(),
            "e2x": nc.dram_tensor("d_f1_e2x", [1024, 128], BF16, kind="Internal").ap()}
    w_a1 = {}
    for nm in ["wq", "wk", "wv", "wg"]:
        w_a1[nm] = nc.dram_tensor("d_a1_" + nm, [1024, 2048], F32, kind="ExternalInput").ap()
    for nm in ["bq", "bk", "bv", "bg"]:
        w_a1[nm] = nc.dram_tensor("d_a1_" + nm, [1, 2048], F32, kind="ExternalInput").ap()
    w_a1["wo"] = nc.dram_tensor("d_a1_wo", [2048, 1024], F32, kind="ExternalInput").ap()
    w_a1["bo"] = nc.dram_tensor("d_a1_bo", [1, 1024], F32, kind="ExternalInput").ap()
    for nm in ["qd3", "kd3"]:
        w_a1[nm] = nc.dram_tensor("d_a1_" + nm, [32, 16, 128, 128], BF16, kind="Internal").ap()
    for nm in ["ktd", "vd", "gd"]:
        w_a1[nm] = nc.dram_tensor("d_a1_" + nm, [4096, 2048], BF16, kind="Internal").ap()

    consts = {k: nc.dram_tensor(k, list(v.shape),
                                BF16 if v.dtype == BF16NP else F32,
                                kind="ExternalInput").ap()
              for k, v in host_consts().items()}
    alloc_common(P)
    P.block = FakeBlock(P) if getattr(Prog, 'use_fake', False) else P.ctx.enter_context(nc.Block())
    pre_dve, _ = emit_preamble(P, consts, want_cvecs=[1e-5, -0.5, 128.0])

    sw = emit_stats_sweep(P, xin, pre_dve)
    P.open_scope()
    emit_stage_bf16(P, "stg", [(w_f1["w1T"], w_f1["w1bf"], 1024, 4096),
                               (w_f1["w2T"], w_f1["w2bf"], 4096, 1024)])
    emit_l0ffn(P, "f0a", xin, rA, w_f0, sw)
    P.close_scope()
    emit_phase_barrier(P)
    sw = emit_stats_finish(P, P.cnt("sld"))
    P.open_scope()
    dec_ready, dv = emit_decay_prep(P, "dc", decay_ap, pre_dve, consts)
    emit_l0attn(P, "a0", rA, rB, w_a0, sw, dec_ready, dv)
    P.close_scope()
    emit_phase_barrier(P)
    sw = emit_stats_finish(P, P.cnt("sld"))
    P.open_scope()
    emit_l0ffn(P, "f0b", rB, rA, w_f0, sw)
    P.close_scope()
    emit_phase_barrier(P)
    sw = emit_stats_finish(P, P.cnt("sld"))
    P.open_scope()
    emit_l1ffn(P, "f1a", rA, rB, w_f1, sw)
    P.close_scope()
    emit_phase_barrier(P)
    sw = emit_stats_finish(P, P.cnt("sld"))
    emit_l1attn_proj(P, "a1p", rB, w_a1, sw)
    _rc1 = nc.sbuf_tensor("r_at", [128, 32 * 128], BF16)
    _rc2 = nc.sbuf_tensor("r_a2", [128, 48 * 128], BF16)
    rings = {"at": _rc1.__enter__(), "a2": _rc2.__enter__()}
    emit_l1attn_pass(P, "a1x", 0, True, rB, rA, w_a1, rings)
    emit_l1attn_pass(P, "a1y", 1, False, rA, rB, w_a1, rings, do_stats=True)
    _rc2.__exit__(None, None, None)
    _rc1.__exit__(None, None, None)
    sw = emit_stats_finish(P, P.cnt("sld"))
    P.open_scope()
    _, wre = emit_l1ffn(P, "f1b", rB, rA, w_f1, sw)
    emit_phase_barrier(P)
    sw = emit_stats_finish(P, P.cnt("sld"))
    st_done, _ = emit_l1ffn(P, "f1c", rA, out, w_f1, sw, do_stats=False, reuse=wre)
    P.close_scope()

    def fin(g):
        g.wait_ge(P.h("st"), st_done)
    P.block.gpsimd(fin)
    P.finish()
    return P


_PROG_CACHE = {}


def _get_prog(name):
    if name not in _PROG_CACHE:
        _PROG_CACHE[name] = {"l0ffn": build_p_l0ffn, "l0attn": build_p_l0attn,
                             "l1ffn": build_p_l1ffn, "l1attn": build_p_l1attn,
                             "mega": build_mega}[name]()
    return _PROG_CACHE[name]


def _prep_host_weights(inputs):
    w_f0 = prep_l0ffn_weights(inputs["L0_ffn_W_in"], inputs["L0_ffn_b_in"],
                              inputs["L0_ffn_W_out"])
    w_f1 = prep_l1ffn_weights(inputs["L1_ffn_W1"], inputs["L1_ffn_b1"],
                              inputs["L1_ffn_W2"], inputs["L1_ffn_b2"],
                              inputs["L1_ffn_scale"])
    m = {}
    m["d_f0_winT"] = w_f0["winT"]
    m["d_f0_bin"] = w_f0["b_in"]
    m["d_f0_woutT"] = w_f0["woutT"]
    for nm, key in [("wq", "Wq"), ("wk", "Wk"), ("wv", "Wv"), ("wo", "Wo")]:
        m["d_a0_" + nm] = np.asarray(inputs["L0_attn_" + key])
    for nm, key in [("bq", "bq"), ("bk", "bk"), ("bv", "bv"), ("bo", "bo")]:
        m["d_a0_" + nm] = np.asarray(inputs["L0_attn_" + key]).reshape(1, 1024)
    m["d_a0_decay"] = np.asarray(inputs["L0_attn_decay_param"], np.float32).reshape(1, 1)
    m["d_f1_w1T"] = w_f1["w1T"]
    m["d_f1_w2T"] = w_f1["w2T"]
    m["d_f1_b1"] = w_f1["b1"]
    m["d_f1_b2"] = w_f1["b2"]
    m["d_f1_scale"] = w_f1["scale"]
    for nm, key in [("wq", "Wq"), ("wk", "Wk"), ("wv", "Wv"), ("wg", "Wg")]:
        m["d_a1_" + nm] = np.asarray(inputs["L1_attn_" + key])
    for nm, key in [("bq", "bq"), ("bk", "bk"), ("bv", "bv"), ("bg", "bg")]:
        m["d_a1_" + nm] = np.asarray(inputs["L1_attn_" + key]).reshape(1, 2048)
    m["d_a1_wo"] = np.asarray(inputs["L1_attn_Wo"])
    m["d_a1_bo"] = np.asarray(inputs["L1_attn_bo"]).reshape(1, 1024)
    return m


def kernel(**inputs):
    consts = host_consts()
    n = 8
    m = _prep_host_weights(inputs)
    x = np.asarray(inputs["L0_attn_x"])
    P = _get_prog("mega")
    in_maps = [{"xin": np.ascontiguousarray(x[b]), **m, **consts} for b in range(n)]
    res = run_bass_kernel_spmd(P.nc, in_maps, core_ids=list(range(n)))
    return np.stack([res.results[b]["out"] for b in range(n)]).astype(np.float32)


def kernel_unfused(**inputs):
    consts = host_consts()
    n = 8
    w_f0 = prep_l0ffn_weights(inputs["L0_ffn_W_in"], inputs["L0_ffn_b_in"],
                              inputs["L0_ffn_W_out"])
    w_a0 = {"wq": np.asarray(inputs["L0_attn_Wq"]), "wk": np.asarray(inputs["L0_attn_Wk"]),
            "wv": np.asarray(inputs["L0_attn_Wv"]), "wo": np.asarray(inputs["L0_attn_Wo"]),
            "bq": np.asarray(inputs["L0_attn_bq"]).reshape(1, 1024),
            "bk": np.asarray(inputs["L0_attn_bk"]).reshape(1, 1024),
            "bv": np.asarray(inputs["L0_attn_bv"]).reshape(1, 1024),
            "bo": np.asarray(inputs["L0_attn_bo"]).reshape(1, 1024),
            "decay": np.asarray(inputs["L0_attn_decay_param"],
                                np.float32).reshape(1, 1)}
    w_f1 = prep_l1ffn_weights(inputs["L1_ffn_W1"], inputs["L1_ffn_b1"],
                              inputs["L1_ffn_W2"], inputs["L1_ffn_b2"],
                              inputs["L1_ffn_scale"])
    w_a1 = {"wq": np.asarray(inputs["L1_attn_Wq"]), "wk": np.asarray(inputs["L1_attn_Wk"]),
            "wv": np.asarray(inputs["L1_attn_Wv"]), "wg": np.asarray(inputs["L1_attn_Wg"]),
            "wo": np.asarray(inputs["L1_attn_Wo"]),
            "bq": np.asarray(inputs["L1_attn_bq"]).reshape(1, 2048),
            "bk": np.asarray(inputs["L1_attn_bk"]).reshape(1, 2048),
            "bv": np.asarray(inputs["L1_attn_bv"]).reshape(1, 2048),
            "bg": np.asarray(inputs["L1_attn_bg"]).reshape(1, 2048),
            "bo": np.asarray(inputs["L1_attn_bo"]).reshape(1, 1024)}

    x = np.asarray(inputs["L0_attn_x"])
    resid = [np.ascontiguousarray(x[b]) for b in range(n)]

    def launch(pname, extra):
        P = _get_prog(pname)
        in_maps = [{"xin": resid[b], **extra, **consts} for b in range(n)]
        res = run_bass_kernel_spmd(P.nc, in_maps, core_ids=list(range(n)))
        for b in range(n):
            resid[b] = res.results[b]["xout"]

    launch("l0ffn", w_f0)
    launch("l0attn", {k: v for k, v in w_a0.items()})
    launch("l0ffn", w_f0)
    launch("l1ffn", w_f1)
    launch("l1attn", w_a1)
    launch("l1ffn", w_f1)
    launch("l1ffn", w_f1)
    return np.stack(resid).astype(np.float32)
